# version 04
# baseline (speedup 1.0000x reference)
import math
import numpy as np
import concourse.bass as bass
import concourse.mybir as mybir
from concourse.bass_utils import run_bass_kernel_spmd
from contextlib import ExitStack

F32 = mybir.dt.float32
BF16 = mybir.dt.bfloat16
AF = mybir.ActivationFunctionType
ALU = mybir.AluOpType
AX = mybir.AxisListType

NEG = -1.0e9
EPS_LN = 1e-5
GROUPS = [[0, 1, 2, 3], [4, 5, 6, 7]]
MOFF0 = [0, 13, 42, 87]
MOFF_B = 148


class SemC:
    __slots__ = ("h", "v", "name")

    def __init__(self, h, name):
        self.h = h
        self.v = 0
        self.name = name


class Prog:
    ENGS = ("sync", "tensor", "scalar", "vector", "gpsimd")

    def __init__(self, nc, es):
        self.nc = nc
        self.es = es
        self.q = {e: [] for e in self.ENGS}

    def sem(self, name):
        return SemC(self.es.enter_context(self.nc.semaphore(name)), name)

    def emit(self, eng, fn, waits=(), incs=()):
        out_vals = []
        rec = []
        for s, n in incs:
            s.v += n
            out_vals.append(s.v)
            rec.append((s.h, n))
        ws = []
        seen = {}
        for s, v in waits:
            if v and v > 0 and seen.get(id(s), -1) < v:
                seen[id(s)] = v
        for s, v in waits:
            if seen.get(id(s), -1) == v and v and v > 0:
                ws.append((s.h, v))
                seen[id(s)] = -2
        self.q[eng].append((tuple(ws), fn, tuple(rec)))
        return out_vals

    def replay(self):
        nc = self.nc
        with nc.Block() as block:
            def mk(engq):
                def _(eng):
                    for waits, fn, incs in self.q[engq]:
                        for (h, v) in waits:
                            eng.wait_ge(h, v)
                        if fn is None:
                            continue
                        ins = fn(eng)
                        for (h, n) in incs:
                            ins = ins.then_inc(h, n)
                return _
            block.sync(mk("sync"))
            block.tensor(mk("tensor"))
            block.scalar(mk("scalar"))
            block.vector(mk("vector"))
            block.gpsimd(mk("gpsimd"))


def _build(stages=4):
    nc = bass.Bass("TRN2", target_bir_lowering=False)
    D = lambda n, sh, dt=F32, kind="ExternalInput": nc.dram_tensor(n, sh, dt, kind=kind).ap()
    x_own = D("x_own", [1024, 2048])
    w_dq = D("w_dq", [128, 1024])
    w_dqT = D("w_dqT", [1024, 128])
    w_uq = D("w_uq", [1024, 128])
    w_uk = D("w_uk", [1024, 128])
    w_uv = D("w_uv", [1024, 128])
    w_oT = D("w_oT", [1024, 1024])
    w_dkvT = D("w_dkvT", [1024, 128])
    wqkv = D("wqkv", [4, 1024, 192])
    bq_in = D("bq_in", [128, 4])
    bk_in = D("bk_in", [128, 4])
    bv_in = D("bv_in", [128, 4])
    ident64f = D("ident64f", [64, 64])
    ident2 = D("ident2", [128, 64], BF16)
    wout = D("wout", [1024, 1024])
    bout_cm = D("bout_cm", [128, 8])
    gate_wT = D("gate_wT", [1024, 8])
    dwT = D("dwT", [1024, 512])
    bd_rows = D("bd_rows", [128, 4])
    upT = D("upT", [512, 1024])
    upb_cm = D("upb_cm", [128, 64])
    halves = D("halves", [128, 2])
    tri = D("tri", [128, 128], BF16)
    ident = D("ident", [128, 128], BF16)
    moff = D("moff", [128, 209])
    onesf_r128 = D("onesf_r128", [1, 128])
    negonesf_r128 = D("negonesf_r128", [1, 128])
    onesf_r64 = D("onesf_r64", [1, 64])
    ind = D("ind", [128, 4])
    out_feat = D("out_feat", [1024, 2048], kind="ExternalOutput")

    cc_ckv_i = nc.dram_tensor("cc_ckv_i", [128, 2048], BF16)
    cc_ckv_o = nc.dram_tensor("cc_ckv_o", [512, 2048], BF16)
    cc_pool_i = nc.dram_tensor("cc_pool_i", [128, 8], F32)
    cc_pool_o = nc.dram_tensor("cc_pool_o", [128, 8], F32)
    cc_pool_i2 = nc.dram_tensor("cc_pool_i2", [128, 8], F32)
    cc_pool_o2 = nc.dram_tensor("cc_pool_o2", [128, 8], F32)
    cc_xn_i = nc.dram_tensor("cc_xn_i", [1024, 2048], BF16)
    cc_xn_o = nc.dram_tensor("cc_xn_o", [4096, 2048], BF16)
    cc_y_i = nc.dram_tensor("cc_y_i", [1024, 2048], BF16)
    cc_y_o = nc.dram_tensor("cc_y_o", [4096, 2048], BF16)

    es = ExitStack()
    S = lambda n, sh, dt=F32: es.enter_context(nc.sbuf_tensor(n, sh, dt))
    P = lambda n, sh, dt=F32: es.enter_context(nc.psum_tensor(n, sh, dt))

    xT = S("xT", [128, 16384])
    xnT = S("xnT", [128, 16384], BF16)
    vbuf = S("vbuf", [128, 130 * 64], BF16)
    u16 = S("u16", [128, 8224], BF16)
    CKVO, Q1, AB, VOWN = 0, 2048, 4096, 6144
    ST0, ST1 = 0, 4096
    wqkv_b = S("wqkv_b", [128, 3072], BF16)
    veff_b = wqkv_b[:, 0:1024]
    woutc_b = [S(f"woutc_b{i}", [128, 1024], BF16) for i in range(2)]
    wdqT_b = woutc_b[0][:]
    wdkvT_b = woutc_b[1][:]
    gwT_s = S("gwT_s", [128, 64])
    upb_cm_s = S("upb_cm_s", [128, 64])
    bout_cm_s = S("bout_cm_s", [128, 8])
    halves_s = S("halves_s", [128, 2])
    mkb_s = S("mkb_s", [128, 8])
    bumix_col = S("bumix_col", [128, 8])
    bdr_s = S("bdr_s", [128, 4])
    bq_s = S("bq_s", [128, 4])
    bk_s = S("bk_s", [128, 4])
    bv_s = S("bv_s", [128, 4])
    id64f_s = S("id64f_s", [64, 64])
    id2_s = S("id2_s", [128, 64], BF16)
    tri_b = S("tri_b", [128, 128], BF16)
    id_b = S("id_b", [128, 128], BF16)
    moff_s = S("moff_s", [128, 209])
    onesr_f = S("onesr_f", [1, 128])
    negonesr_f = S("negonesr_f", [1, 128])
    onesr64_f = S("onesr64_f", [1, 64])
    ind_s = S("ind_s", [128, 4])
    ones_col_b = S("ones_col_b", [128, 1], BF16)
    eps_s = S("eps_s", [1, 1])
    negln8_s = S("negln8_s", [1, 1])
    kmax_s = S("kmax_s", [1, 4])
    z_s = S("z_s", [1, 2])
    mhat_s = S("mhat_s", [1, 1])
    mneg = [S(f"mneg{i}", [128, 1]) for i in range(2)]
    bias_all = [S(f"bias_all{i}", [128, 64]) for i in range(2)]
    qlb = [S(f"qlb{i}", [128, 512], BF16) for i in range(2)]
    psb = [S(f"psb{i}", [128, 1024], BF16) for i in range(2)]
    sq_b = S("sq_b", [128, 512], BF16)
    xb_t = S("xb_t", [128, 512], BF16)
    m_row = S("m_row", [1, 512])
    q_row = S("q_row", [1, 512])
    r_row = S("r_row", [1, 512])
    u_sb = S("u_sb", [128, 512])
    ytmp_sb = u_sb[0:64, :]
    rec_sb = m_row[0:1, :]
    bc_sb = S("bc_sb", [64, 512])
    ynrm_b = [S(f"ynrm_b{i}", [64, 512], BF16) for i in range(2)]
    pool_acc = S("pool_acc", [128, 8])
    pool_in = S("pool_in", [128, 8])
    pool_t = S("pool_t", [128, 1])
    gate_row = S("gate_row", [1, 8])
    g2_row = S("g2_row", [1, 8])
    mk_row = S("mk_row", [1, 8])
    s11 = S("s11", [1, 1])
    s11b = S("s11b", [1, 1])
    mvec_s = S("mvec_s", [128, 4])
    cand = [S(f"cand{i}", [128, 512], BF16) for i in range(2)]
    scr = S("scr", [128, 10240])
    scr_b = scr[:].bitcast(BF16)
    ckv_g = scr_b[:, 0:8192]
    dwT_b = scr_b[:, 8192:12288]
    upT_b = scr_b[:, 12288:16384]
    wu_sc = scr_b[:, 16384:20480]

    spair = [P(f"spair{i}", [128, 1024]) for i in range(2)]
    y_ps = [P(f"y_ps{i}", [128, 512]) for i in range(2)]
    g_ps = [P(f"g_ps{i}", [128, 512]) for i in range(2)]
    h_ps = [spair[0][:, 0:512], spair[0][:, 512:1024]]
    s_ps = [spair[0][:, 0:512], spair[1][:, 0:512]]

    p = Prog(nc, es)
    ld = p.sem("ld")
    lk = p.sem("lk")
    lp = p.sem("lp")
    lx = p.sem("lx")
    lst = [p.sem("lst0"), p.sem("lst1")]
    ly = [p.sem("ly0"), p.sem("ly1")]
    lc = p.sem("lc")
    lds = p.sem("lds")
    smm = p.sem("smm")
    expd = p.sem("expd")
    pvd = p.sem("pvd")
    dvd = p.sem("dvd")
    ped = p.sem("ped")
    acd = p.sem("acd")
    ccd = p.sem("ccd")
    std = p.sem("std")

    def GUARD():
        return [(ped, ped.v), (acd, acd.v), (dvd, dvd.v), (pvd, pvd.v), (expd, expd.v), (smm, smm.v)]

    p.emit("sync", lambda e: e.dma_start(out=xT[:].rearrange("p (c t) -> p c t", c=8), in_=x_own.rearrange("(c p) t -> p c t", p=128)), incs=[(ld, 16)])
    x_ld = None
    for dst, src in ((tri_b, tri), (id_b, ident), (moff_s, moff), (onesr_f, onesf_r128),
                     (negonesr_f, negonesf_r128), (onesr64_f, onesf_r64), (ind_s, ind),
                     (upb_cm_s, upb_cm), (bout_cm_s, bout_cm), (halves_s, halves), (bdr_s, bd_rows), (bq_s, bq_in), (bk_s, bk_in), (bv_s, bv_in), (id64f_s, ident64f), (id2_s, ident2)):
        p.emit("sync", (lambda dst, src: lambda e: e.dma_start(out=dst[:], in_=src[:]))(dst, src), incs=[(ld, 16)])
    p.emit("sync", lambda e: e.dma_start(out=gwT_s[:].rearrange("p (c e) -> p c e", c=8), in_=gate_wT.rearrange("(c p) e -> p c e", p=128)), incs=[(ld, 16)])
    const_ld = None
    SC_WUQ, SC_WUK, SC_WUV, SC_WDQ, SC_J, SC_KEFF = 0, 1024, 2048, 3072, 4096, 4224
    SC_WOT = 5248
    p.emit("sync", lambda e: e.dma_start(out=scr[:, SC_WUQ:SC_WUQ + 1024].rearrange("p (c j) -> p c j", c=8), in_=w_uq.rearrange("(c p) j -> p c j", p=128)), incs=[(ld, 16)])
    p.emit("sync", lambda e: e.dma_start(out=scr[:, SC_WUK:SC_WUK + 1024].rearrange("p (c j) -> p c j", c=8), in_=w_uk.rearrange("(c p) j -> p c j", p=128)), incs=[(ld, 16)])
    p.emit("sync", lambda e: e.dma_start(out=scr[:, SC_WUV:SC_WUV + 1024].rearrange("p (c j) -> p c j", c=8), in_=w_uv.rearrange("(c p) j -> p c j", p=128)), incs=[(ld, 16)])
    p.emit("sync", lambda e: e.dma_start(out=scr[:, SC_WDQ:SC_WDQ + 1024], in_=w_dq[:]), incs=[(ld, 16)])
    [wot0] = p.emit("sync", lambda e: e.dma_start(out=scr[:, SC_WOT:SC_WOT + 4096].rearrange("p (c d) -> p c d", c=8),
        in_=w_oT[:, 0:512].rearrange("(c p) d -> p c d", p=128)), incs=[(ld, 16)])
    boot_ld = (ld, ld.v)
    x_ld = const_ld = wprep_ld = boot_ld
    p.emit("gpsimd", lambda e: e.dma_start(out=wdqT_b.rearrange("p (c j) -> p c j", c=8), in_=w_dqT.rearrange("(c p) j -> p c j", p=128)), incs=[(lds, 16)])
    p.emit("gpsimd", lambda e: e.dma_start(out=wdkvT_b.rearrange("p (c j) -> p c j", c=8), in_=w_dkvT.rearrange("(c p) j -> p c j", p=128)), incs=[(lds, 16)])
    wbf_ld = (lds, lds.v)

    p.emit("vector", lambda e: e.memset(eps_s[:], EPS_LN), waits=[const_ld], incs=[(dvd, 1)])
    p.emit("vector", lambda e: e.memset(negln8_s[:], -math.log(8.0)), incs=[(dvd, 1)])
    [v_ones] = p.emit("vector", lambda e: e.memset(ones_col_b[:], 1.0), incs=[(dvd, 1)])

    for c in range(8):
        p.emit("tensor", (lambda c: lambda e: e.matmul(
            g_ps[0][0:128, 0:128], scr[:, SC_WUQ + 128 * c:SC_WUQ + 128 * c + 128],
            scr[:, SC_WUK + 128 * c:SC_WUK + 128 * c + 128], start=(c == 0), stop=(c == 7)))(c),
            waits=[wprep_ld] if c == 0 else (), incs=[(ped, 1)])
    [vJ] = p.emit("vector", lambda e: e.tensor_copy(scr[:, SC_J:SC_J + 128], g_ps[0][0:128, 0:128]),
                  waits=[(ped, ped.v)], incs=[(dvd, 1)])
    for c in range(8):
        p.emit("tensor", (lambda c: lambda e: e.matmul(
            g_ps[1][0:128, 0:128], scr[:, SC_WDQ + 128 * c:SC_WDQ + 128 * c + 128],
            scr[:, SC_J:SC_J + 128], start=True, stop=True))(c),
            waits=[(dvd, dvd.v), (ped, ped.v)], incs=[(ped, 1)])
        p.emit("vector", (lambda c: lambda e: e.tensor_copy(
            scr[:, SC_KEFF + 128 * c:SC_KEFF + 128 * c + 128], g_ps[1][0:128, 0:128]))(c),
            waits=[(ped, ped.v)], incs=[(dvd, 1)])
    a_done = []
    for h in range(16):
        ro = 64 * (h % 2)
        co = 128 * (h // 2)
        p.emit("tensor", (lambda h, ro, co: lambda e: e.matmul(
            g_ps[h % 2][0:128, 128:256], scr[ro:ro + 64, SC_WUQ + co:SC_WUQ + co + 128],
            scr[ro:ro + 64, SC_KEFF + co:SC_KEFF + co + 128], start=True, stop=True))(h, ro, co),
            waits=[(dvd, dvd.v), (ped, ped.v)], incs=[(ped, 1)])
        [ad] = p.emit("vector", (lambda h: lambda e: e.tensor_copy(
            u16[:, AB + 128 * h:AB + 128 * h + 128], g_ps[h % 2][0:128, 128:256]))(h),
            waits=[(ped, ped.v)], incs=[(dvd, 1)])
        a_done.append(ad)
    for half in range(2):
        wv = [(ld, wot0)] if half == 0 else [(ld, ld.v)]
        for c in range(8):
            p.emit("tensor", (lambda half, c: lambda e: e.matmul(
                h_ps[half][0:128, 0:512], scr[:, SC_WUV + 128 * c:SC_WUV + 128 * c + 128],
                scr[:, SC_WOT + 512 * c:SC_WOT + 512 * c + 512],
                start=(c == 0), stop=(c == 7)))(half, c),
                waits=(wv + [(ped, ped.v)]) if c == 0 else (), incs=[(ped, 1)])
        pe_half = (ped, ped.v)
        if half == 0:
            p.emit("sync", lambda e: e.dma_start(out=scr[:, SC_WOT:SC_WOT + 4096].rearrange("p (c d) -> p c d", c=8),
                in_=w_oT[:, 512:1024].rearrange("(c p) d -> p c d", p=128)),
                waits=[pe_half], incs=[(ld, 16)])
        p.emit("vector", (lambda half: lambda e: e.tensor_copy(
            veff_b[:, 512 * half:512 * half + 512], h_ps[half][0:128, 0:512]))(half),
            waits=[pe_half, (dvd, dvd.v)], incs=[(dvd, 1)])
    prep_done = (dvd, dvd.v)
    p.emit("gpsimd", lambda e: e.dma_start(out=dwT_b.rearrange("p (c f) -> p c f", c=8), in_=dwT.rearrange("(c p) f -> p c f", p=128)),
           waits=[prep_done, (ped, ped.v), wbf_ld], incs=[(lds, 16)])
    p.emit("gpsimd", lambda e: e.dma_start(out=upT_b.rearrange("p (c d) -> p c d", c=4), in_=upT.rearrange("(c p) d -> p c d", p=128)),
           incs=[(lds, 16)])
    ffnw_ld = (lds, lds.v)

    state = {"stream_ready": (dvd, 0), "xn_ready": None, "pool_ready": None, "ld_guard": (lp, 0)}
    fl = {"sfree": [0, 0], "pfree": [0, 0], "yfree": [0, 0], "bias_free": [0, 0],
          "z_free": 0, "mhat_free": 0, "epi_free": 0, "ynrm_free": [0, 0], "epi_par": 0,
          "pend_epi": None, "jobctr": 0}

    def emit_ln(with_pool):
        for tt in range(4):
            for c in range(8):
                w = [state["stream_ready"], (dvd, dvd.v)] if (tt == 0 and c == 0) else [(dvd, dvd.v)]
                w = list(w) + [(ped, ped.v)]
                sl = xT[:, 2048 * c + 512 * tt: 2048 * c + 512 * tt + 512]
                p.emit("vector", (lambda sl: lambda e: e.tensor_copy(xb_t[:], sl))(sl), waits=w, incs=[(dvd, 1)])
                cast_v = dvd.v
                p.emit("vector", lambda e: e.tensor_tensor(sq_b[:], xb_t[:], xb_t[:], ALU.mult),
                       waits=[(dvd, dvd.v)], incs=[(dvd, 1)])
                p.emit("tensor", (lambda c: lambda e: e.matmul(
                    g_ps[0][0:1, 0:512], ones_col_b[:], xb_t[:], start=(c == 0), stop=(c == 7)))(c),
                    waits=([(dvd, cast_v), (dvd, v_ones)] + (GUARD() if c == 0 else [])), incs=[(ped, 1)])
                p.emit("tensor", (lambda c: lambda e: e.matmul(
                    g_ps[1][0:1, 0:512], ones_col_b[:], sq_b[:], start=(c == 0), stop=(c == 7)))(c),
                    waits=[(dvd, dvd.v)], incs=[(ped, 1)])
            stat_done = (ped, ped.v)
            p.emit("vector", lambda e: e.tensor_scalar(m_row[:], g_ps[0][0:1, 0:512], 1.0 / 1024, None, ALU.mult),
                   waits=[stat_done, (dvd, dvd.v)], incs=[(dvd, 1)])
            p.emit("vector", lambda e: e.tensor_scalar(q_row[:], g_ps[1][0:1, 0:512], 1.0 / 1024, None, ALU.mult),
                   waits=[stat_done, (dvd, dvd.v)], incs=[(dvd, 1)])
            p.emit("vector", lambda e: e.tensor_tensor(r_row[:], m_row[:], m_row[:], ALU.mult),
                   waits=[(dvd, dvd.v), (acd, acd.v)], incs=[(dvd, 1)])
            p.emit("vector", lambda e: e.tensor_tensor(q_row[:], q_row[:], r_row[:], ALU.subtract),
                   waits=[(dvd, dvd.v)], incs=[(dvd, 1)])
            p.emit("scalar", lambda e: e.activation(out=q_row[:], in_=q_row[:], func=AF.Ln, bias=eps_s[:]),
                   waits=[(dvd, dvd.v), (acd, acd.v)], incs=[(acd, 1)])
            p.emit("scalar", lambda e: e.activation(out=r_row[:], in_=q_row[:], func=AF.Exp, scale=-0.5),
                   waits=[(acd, acd.v)], incs=[(acd, 1)])
            rrow_v = acd.v
            [v_nmr] = p.emit("vector", lambda e: e.scalar_tensor_tensor(
                m_row[:], m_row[:], -1.0, r_row[:], ALU.mult, ALU.mult),
                waits=[(acd, rrow_v), (dvd, dvd.v)], incs=[(dvd, 1)])
            p.emit("tensor", lambda e: e.matmul(h_ps[0][0:128, 0:512], onesr_f[:], r_row[:], start=True, stop=True),
                   waits=[(dvd, v_nmr)] + GUARD(), incs=[(ped, 1)])
            p.emit("tensor", lambda e: e.matmul(h_ps[1][0:128, 0:512], onesr_f[:], m_row[:], start=True, stop=True),
                   incs=[(ped, 1)])
            bcs_done = (ped, ped.v)
            if with_pool:
                p.emit("vector", lambda e: e.reduce_sum(s11b[:], m_row[:], axis=AX.X),
                       waits=[(dvd, v_nmr)], incs=[(dvd, 1)])
                if tt == 0:
                    p.emit("vector", lambda e: e.tensor_copy(s11[:], s11b[:]), waits=[(dvd, dvd.v)], incs=[(dvd, 1)])
                else:
                    p.emit("vector", lambda e: e.tensor_tensor(s11[:], s11[:], s11b[:], ALU.add),
                           waits=[(dvd, dvd.v)], incs=[(dvd, 1)])
            for c in range(8):
                sl = xT[:, 2048 * c + 512 * tt: 2048 * c + 512 * tt + 512]
                osl = xnT[:, 2048 * c + 512 * tt: 2048 * c + 512 * tt + 512]
                p.emit("vector", (lambda sl: lambda e: e.tensor_tensor(u_sb[:], sl, h_ps[0][0:128, 0:512], ALU.mult))(sl),
                       waits=[bcs_done, (dvd, dvd.v)], incs=[(dvd, 1)])
                u_v = dvd.v
                p.emit("vector", (lambda osl: lambda e: e.tensor_tensor(osl, u_sb[:], h_ps[1][0:128, 0:512], ALU.add))(osl),
                       waits=[(dvd, dvd.v), (smm, smm.v), (ped, ped.v)], incs=[(dvd, 1)])
                if with_pool:
                    p.emit("vector", lambda e: e.reduce_sum(pool_t[:], u_sb[:], axis=AX.X),
                           waits=[(dvd, u_v)], incs=[(dvd, 1)])
                    if tt == 0:
                        p.emit("vector", (lambda c: lambda e: e.tensor_copy(pool_acc[:, c:c + 1], pool_t[:]))(c),
                               waits=[(dvd, dvd.v), state["ld_guard"]], incs=[(dvd, 1)])
                    else:
                        p.emit("vector", (lambda c: lambda e: e.tensor_tensor(
                            pool_acc[:, c:c + 1], pool_acc[:, c:c + 1], pool_t[:], ALU.add))(c),
                            waits=[(dvd, dvd.v)], incs=[(dvd, 1)])
        if with_pool:
            p.emit("tensor", lambda e: e.matmul(g_ps[0][0:128, 0:1], onesr_f[:], s11[:], start=True, stop=True),
                   waits=[(dvd, dvd.v)] + GUARD(), incs=[(ped, 1)])
            p.emit("vector", lambda e: e.tensor_copy(pool_t[:], g_ps[0][0:128, 0:1]),
                   waits=[(ped, ped.v), (dvd, dvd.v)], incs=[(dvd, 1)])
            [vp] = p.emit("vector", lambda e: e.tensor_scalar(pool_acc[:], pool_acc[:], pool_t[:], None, ALU.add),
                          waits=[(dvd, dvd.v)], incs=[(dvd, 1)])
            state["pool_ready"] = (dvd, vp)
        state["xn_ready"] = (dvd, dvd.v)

    def emit_prelude(job):
        par = fl["jobctr"] & 1
        fl["jobctr"] += 1
        job["par"] = par
        qp = job["qp"]
        qb = job.get("qb", 0)
        nA = job["nA"]
        if job.get("qlmm") is not None:
            job["qlmm"]()
        p.emit("vector", (lambda job, qp, qb: lambda e: e.tensor_tensor(
            sq_b[qb:qb + qp, :], job["ql_ap"], job["ql_ap"], ALU.mult))(job, qp, qb),
            waits=job["wq"] + [(dvd, dvd.v), (ped, fl["mhat_free"])], incs=[(dvd, 1)])
        p.emit("tensor", (lambda par, qp, qb: lambda e: e.matmul(
            g_ps[par][0:1, 0:512], ones_col_b[qb:qb + qp, :], sq_b[qb:qb + qp, :], start=True, stop=True))(par, qp, qb),
            waits=[(dvd, dvd.v), (ped, ped.v), (dvd, v_ones)], incs=[(ped, 1)])
        p.emit("vector", (lambda par: lambda e: e.reduce_max(z_s[0:1, 0:1], g_ps[par][0:1, 0:512], axis=AX.X))(par),
               waits=[(ped, ped.v), (acd, fl["z_free"]), (dvd, dvd.v)], incs=[(dvd, 1)])
        p.emit("vector", (lambda job: lambda e: e.tensor_tensor(
            z_s[0:1, 0:1], z_s[0:1, 0:1], kmax_s[0:1, job["kn_col"]:job["kn_col"] + 1], ALU.mult))(job),
            waits=[(dvd, dvd.v)] + job.get("wkn", []), incs=[(dvd, 1)])
        p.emit("scalar", lambda e: e.activation(out=z_s[0:1, 1:2], in_=z_s[0:1, 0:1], func=AF.Ln),
               waits=[(dvd, dvd.v), (acd, acd.v)], incs=[(acd, 1)])
        p.emit("scalar", lambda e: e.activation(out=mhat_s[:], in_=z_s[0:1, 1:2], func=AF.Exp, scale=0.5, bias=negln8_s[:]),
               waits=[(acd, acd.v)], incs=[(acd, 1)])
        fl["z_free"] = acd.v
        p.emit("tensor", (lambda par: lambda e: e.matmul(
            g_ps[par][0:128, 256:257], negonesr_f[:], mhat_s[:], start=True, stop=True))(par),
            waits=[(acd, acd.v), (ped, ped.v)], incs=[(ped, 1)])
        fl["mhat_free"] = ped.v
        p.emit("vector", (lambda par: lambda e: e.tensor_copy(mneg[par][:], g_ps[par][0:128, 256:257]))(par),
               waits=[(ped, ped.v), (dvd, dvd.v)], incs=[(dvd, 1)])
        [bv] = p.emit("vector", (lambda job, par, nA: lambda e: e.tensor_scalar(
            bias_all[par][:, 0:nA + 1], moff_s[:, job["moff0"]:job["moff0"] + nA + 1],
            mneg[par][:], None, ALU.add))(job, par, nA),
            waits=[(dvd, dvd.v), (expd, fl["bias_free"][par])], incs=[(dvd, 1)])
        job["pre_done"] = (dvd, bv)
        return job

    def emit_kloop(job):
        par = job["par"]
        nA = job["nA"]
        pre_done = job["pre_done"]
        npair = nA // 2
        nE = npair + 4
        exp_free = []
        pv_last = []
        pending_pv = None
        first_pv = [True]
        for e in range(nE):
            diag = e >= npair
            i = e - npair
            col0 = 128 * i if diag else 0
            bk = e % 2
            if e >= 2:
                w = [(expd, exp_free[e - 2])]
            else:
                w = [pre_done, (expd, fl["sfree"][bk]), (dvd, fl["yfree"][par])] + job.get("wk0", [])
            if diag:
                lhs = job["lhsB"](i)
                p.emit("tensor", (lambda lhs, bk, col0, job: lambda e_: e_.matmul(
                    spair[bk][0:128, col0:512], lhs, job["ql_ap"][:, col0:512], start=True, stop=True))(lhs, bk, col0, job),
                    waits=w, incs=[(smm, 1)])
            else:
                lhs0 = job["lhsA"](2 * e)
                lhs1 = job["lhsA"](2 * e + 1)
                p.emit("tensor", (lambda lhs0, bk, job: lambda e_: e_.matmul(
                    spair[bk][0:128, 0:512], lhs0, job["ql_ap"], start=True, stop=True))(lhs0, bk, job),
                    waits=w, incs=[(smm, 1)])
                p.emit("tensor", (lambda lhs1, bk, job: lambda e_: e_.matmul(
                    spair[bk][0:128, 512:1024], lhs1, job["ql_ap"], start=True, stop=True))(lhs1, bk, job),
                    incs=[(smm, 1)])
            if pending_pv is not None:
                pending_pv()
                pending_pv = None
            if e == 2 and fl.get("pend_epi") is not None:
                pe_fn = fl["pend_epi"]
                fl["pend_epi"] = None
                pe_fn()
            if e == 4 and fl.get("pend_pre") is not None:
                pp = fl["pend_pre"]
                fl["pend_pre"] = None
                pp()
            bcol = nA if diag else 2 * e
            we = [(smm, smm.v)]
            we.append((pvd, pv_last[e - 2]) if e >= 2 else (pvd, fl["pfree"][bk]))
            if diag:
                [ev] = p.emit("scalar", (lambda bk, col0, bcol, par: lambda e_: e_.activation(
                    out=psb[bk][:, col0:512], in_=spair[bk][0:128, col0:512], func=AF.Exp,
                    scale=0.125, bias=bias_all[par][:, bcol:bcol + 1]))(bk, col0, bcol, par),
                    waits=we, incs=[(expd, 1)])
            else:
                [ev] = p.emit("scalar", (lambda bk, bcol, par: lambda e_: e_.activation(
                    out=psb[bk][:, 0:1024], in_=spair[bk][0:128, 0:1024], func=AF.Exp,
                    scale=0.125, bias=bias_all[par][:, bcol:bcol + 1]))(bk, bcol, par),
                    waits=we, incs=[(expd, 1)])
            exp_free.append(ev)
            if diag:
                [mv] = p.emit("vector", (lambda bk, i: lambda e_: e_.tensor_tensor(
                    psb[bk][:, 128 * i:128 * i + 128], psb[bk][:, 128 * i:128 * i + 128], tri_b[:], ALU.mult))(bk, i),
                    waits=[(expd, ev), (dvd, dvd.v)], incs=[(dvd, 1)])
                pv_w = [(dvd, mv)]
            else:
                pv_w = [(expd, ev)]
            last_e = (e == nE - 1)
            if diag:
                vap = job["vB"](i)
                def mk_pv(vap=vap, bk=bk, col0=col0, par=par, last_e=last_e, pv_w=tuple(pv_w)):
                    st = first_pv[0]
                    first_pv[0] = False
                    [pv] = p.emit("tensor", lambda e_: e_.matmul(
                        y_ps[par][0:65, col0:512], vap, psb[bk][:, col0:512],
                        start=st, stop=last_e),
                        waits=list(pv_w), incs=[(pvd, 1)])
                    pv_last.append(pv)
            else:
                vap0 = job["vA"](2 * e)
                vap1 = job["vA"](2 * e + 1)
                def mk_pv(vap0=vap0, vap1=vap1, bk=bk, par=par, pv_w=tuple(pv_w)):
                    st = first_pv[0]
                    first_pv[0] = False
                    p.emit("tensor", lambda e_: e_.matmul(
                        y_ps[par][0:65, 0:512], vap0, psb[bk][:, 0:512],
                        start=st, stop=False),
                        waits=list(pv_w), incs=[(pvd, 1)])
                    [pv] = p.emit("tensor", lambda e_: e_.matmul(
                        y_ps[par][0:65, 0:512], vap1, psb[bk][:, 512:1024],
                        start=False, stop=False),
                        incs=[(pvd, 1)])
                    pv_last.append(pv)
            pending_pv = mk_pv
        pending_pv()
        if fl.get("pend_epi") is not None:
            pe_fn = fl["pend_epi"]; fl["pend_epi"] = None; pe_fn()
        if fl.get("pend_pre") is not None:
            pp = fl["pend_pre"]; fl["pend_pre"] = None; pp()
        fl["sfree"][nE % 2] = exp_free[-2]
        fl["sfree"][(nE - 1) % 2] = exp_free[-1]
        fl["pfree"][nE % 2] = pv_last[-2]
        fl["pfree"][(nE - 1) % 2] = pv_last[-1]
        fl["bias_free"][par] = expd.v
        return (pvd, pv_last[-1])
        fl["sfree"][nT % 2] = exp_free[-2] if nT >= 2 else exp_free[-1]
        fl["sfree"][(nT - 1) % 2] = exp_free[-1]
        fl["pfree"][nT % 2] = pv_at[-2] if nT >= 2 else pv_at[-1]
        fl["pfree"][(nT - 1) % 2] = pv_at[-1]
        fl["bias_free"][par] = expd.v
        return (pvd, pv_at[-1])

    def flash_epilogue(job, pv_done, dst_slice=None, dram_fn=None):
        par = job["par"]
        p.emit("vector", (lambda par: lambda e: e.reciprocal(rec_sb[:], y_ps[par][64:65, 0:512]))(par),
               waits=[pv_done, (dvd, dvd.v), (ped, fl["epi_free"])], incs=[(dvd, 1)])
        p.emit("tensor", (lambda par: lambda e: e.matmul(
            g_ps[par][0:64, 0:512], onesr64_f[:], rec_sb[:], start=True, stop=True))(par),
            waits=[(dvd, dvd.v), (ped, ped.v)], incs=[(ped, 1)])
        fl["epi_free"] = ped.v
        p.emit("vector", (lambda par: lambda e: e.tensor_copy(bc_sb[:], g_ps[par][0:64, 0:512]))(par),
               waits=[(ped, ped.v), (dvd, dvd.v)], incs=[(dvd, 1)])
        if dram_fn is None:
            hi = job.get("hi", False)
            p.emit("vector", (lambda par: lambda e: e.tensor_tensor(
                ytmp_sb[:], y_ps[par][0:64, 0:512], bc_sb[:], ALU.mult))(par),
                waits=[(dvd, dvd.v)], incs=[(dvd, 1)])
            if hi:
                p.emit("tensor", (lambda par: lambda e: e.matmul(
                    g_ps[par][64:128, 0:512], id64f_s[:], ytmp_sb[:],
                    start=True, stop=True, tile_position=(0, 64)))(par),
                    waits=[(dvd, dvd.v), (ped, ped.v)], incs=[(ped, 1)])
                [va] = p.emit("vector", (lambda dst_slice, par: lambda e: e.tensor_tensor(
                    dst_slice, dst_slice, g_ps[par][64:128, 0:512], ALU.add))(dst_slice, par),
                    waits=[(ped, ped.v), (dvd, dvd.v)], incs=[(dvd, 1)])
                fl["epi_free"] = ped.v
            else:
                [va] = p.emit("vector", (lambda dst_slice: lambda e: e.tensor_tensor(
                    dst_slice, dst_slice, ytmp_sb[:], ALU.add))(dst_slice),
                    waits=[(dvd, dvd.v)], incs=[(dvd, 1)])
            fl["yfree"][par] = dvd.v
            return (dvd, va)
        else:
            yp = fl["epi_par"]
            fl["epi_par"] ^= 1
            [vm] = p.emit("vector", (lambda par, yp: lambda e: e.tensor_tensor(
                ynrm_b[yp][:], y_ps[par][0:64, 0:512], bc_sb[:], ALU.mult))(par, yp),
                waits=[(dvd, dvd.v), (ly[yp], fl["ynrm_free"][yp])], incs=[(dvd, 1)])
            fl["yfree"][par] = dvd.v
            [sv] = p.emit("sync", (lambda yp: lambda e: dram_fn(e, ynrm_b[yp]))(yp),
                          waits=[(dvd, vm)], incs=[(ly[yp], 16)])
            fl["ynrm_free"][yp] = sv
            return (ly[yp], sv)

    assert stages >= 1
    emit_ln(False)
    xn1 = state["xn_ready"]
    if True:
        ckv_own_done = []
        q1_done = []
        for tt in range(4):
            for c in range(8):
                p.emit("tensor", (lambda tt, c: lambda e: e.matmul(
                    g_ps[tt % 2][0:128, 0:512], wdkvT_b[:, 128 * c:128 * c + 128],
                    xnT[:, 2048 * c + 512 * tt:2048 * c + 512 * tt + 512], start=(c == 0), stop=(c == 7)))(tt, c),
                    waits=([xn1, wbf_ld] + GUARD()) if c == 0 else (), incs=[(ped, 1)])
            [vv] = p.emit("vector", (lambda tt: lambda e: e.tensor_copy(
                u16[:, CKVO + 512 * tt:CKVO + 512 * tt + 512], g_ps[tt % 2][0:128, 0:512]))(tt),
                waits=[(ped, ped.v), (dvd, dvd.v)], incs=[(dvd, 1)])
            ckv_own_done.append(vv)
            for c in range(8):
                p.emit("tensor", (lambda tt, c: lambda e: e.matmul(
                    h_ps[tt % 2][0:128, 0:512], wdqT_b[:, 128 * c:128 * c + 128],
                    xnT[:, 2048 * c + 512 * tt:2048 * c + 512 * tt + 512], start=(c == 0), stop=(c == 7)))(tt, c),
                    incs=[(ped, 1)])
            [vq] = p.emit("vector", (lambda tt: lambda e: e.tensor_copy(
                u16[:, Q1 + 512 * tt:Q1 + 512 * tt + 512], h_ps[tt % 2][0:128, 0:512]))(tt),
                waits=[(ped, ped.v), (dvd, dvd.v)], incs=[(dvd, 1)])
            q1_done.append(vq)

        [sv] = p.emit("sync", lambda e: e.dma_start(out=cc_ckv_i.ap(), in_=u16[:, CKVO:CKVO + 2048]),
                      waits=[(dvd, ckv_own_done[-1])], incs=[(lk, 16)])
        [cc1] = p.emit("gpsimd", lambda e: e.collective_compute(
            "AllGather", ALU.bypass, replica_groups=GROUPS, ins=[cc_ckv_i.ap().opt()], outs=[cc_ckv_o.ap().opt()]),
            waits=[(lk, sv), wbf_ld], incs=[(ccd, 1)])
        for i in range(4):
            for s4 in range(4):
                j = 4 * s4 + i
                p.emit("sync", (lambda i, s4, j: lambda e: e.dma_start(
                    out=ckv_g[:, 512 * j:512 * j + 512],
                    in_=cc_ckv_o.ap()[128 * i:128 * i + 128, 512 * s4:512 * s4 + 512]))(i, s4, j),
                    waits=[(ccd, cc1), prep_done], incs=[(lk, 16)])
        ckv_ld = (lk, lk.v)

        for cc in range(16):
            p.emit("vector", (lambda cc: lambda e: e.tensor_tensor(
                sq_b[:], ckv_g[:, 512 * cc:512 * cc + 512], ckv_g[:, 512 * cc:512 * cc + 512], ALU.mult))(cc),
                waits=[ckv_ld, (dvd, dvd.v)], incs=[(dvd, 1)])
            p.emit("tensor", (lambda cc: lambda e: e.matmul(
                g_ps[cc % 2][0:1, 0:512], ones_col_b[:], sq_b[:], start=True, stop=True))(cc),
                waits=[(dvd, dvd.v)] + GUARD(), incs=[(ped, 1)])
            if cc == 0:
                p.emit("vector", (lambda cc: lambda e: e.reduce_max(
                    kmax_s[0:1, 0:1], g_ps[cc % 2][0:1, 0:512], axis=AX.X))(cc),
                    waits=[(ped, ped.v), (dvd, dvd.v)], incs=[(dvd, 1)])
            else:
                p.emit("vector", (lambda cc: lambda e: e.reduce_max(
                    z_s[0:1, 0:1], g_ps[cc % 2][0:1, 0:512], axis=AX.X))(cc),
                    waits=[(ped, ped.v), (dvd, dvd.v)], incs=[(dvd, 1)])
                p.emit("vector", lambda e: e.tensor_max(kmax_s[0:1, 0:1], kmax_s[0:1, 0:1], z_s[0:1, 0:1]),
                       waits=[(dvd, dvd.v)], incs=[(dvd, 1)])
        kn_done = (dvd, dvd.v)

        two_run_out = lambda ap130: ap130.rearrange("p (a b) -> p a b", b=65)[:, :, 0:64]
        two_run_in = lambda ap128: ap128.rearrange("p (a b) -> p a b", b=64)

        def mk_job_a(hp, hh, s4):
            h = 2 * hp + hh
            par_dummy = 0
            def qlmm():
                p.emit("tensor", (lambda h, s4: lambda e: e.matmul(
                    g_ps[job["par"]][0:128, 0:512], u16[:, AB + 128 * h:AB + 128 * h + 128],
                    u16[:, Q1 + 512 * s4:Q1 + 512 * s4 + 512], start=True, stop=True))(h, s4),
                    waits=[(dvd, q1_done[s4]), (ped, ped.v), (dvd, a_done[h]),
                           (acd, acd.v), (dvd, dvd.v)], incs=[(ped, 1)])
                parx = job["par"]
                [vq] = p.emit("vector", (lambda parx: lambda e: e.tensor_copy(
                    qlb[parx][:], g_ps[parx][0:128, 0:512]))(parx),
                    waits=[(ped, ped.v), (dvd, dvd.v), (smm, smm.v)], incs=[(dvd, 1)])
                job["wq"] = [(dvd, vq)]
                job["ql_ap"] = qlb[parx][:]
            job = {
                "qp": 128, "hi": (hh == 1), "nA": 16 * s4 + 12, "moff0": MOFF0[s4], "kn_col": 0,
                "wkn": [kn_done],
                "wq": [],
                "lhsA": (lambda t: ckv_g[:, 128 * t:128 * t + 128]),
                "lhsB": (lambda i, s4=s4: u16[:, CKVO + 512 * s4 + 128 * i:CKVO + 512 * s4 + 128 * i + 128]),
                "vA": (lambda t, hh=hh: vbuf[:, 130 * t + 65 * hh:130 * t + 65 * hh + 65]),
                "vB": (lambda i, s4=s4, hh=hh: u16[:, VOWN + 130 * (4 * s4 + i) + 65 * hh:
                                                   VOWN + 130 * (4 * s4 + i) + 65 * hh + 65]),
                "dst": (hp, hh, s4),
            }
            job["qlmm"] = qlmm
            return job

        jobs_a = [mk_job_a(hp, hh, s4) for hp in range(8) for hh in range(2) for s4 in range(4)]
        jidx = 0
        for hp in range(8):
            p.emit("vector", lambda e: e.memset(vbuf[:], 1.0),
                   waits=[(dvd, dvd.v), (pvd, pvd.v)], incs=[(dvd, 1)])
            [vo_ms] = p.emit("vector", lambda e: e.memset(u16[:, VOWN:VOWN + 2080], 1.0),
                             waits=[(dvd, dvd.v)], incs=[(dvd, 1)])
            vb_done = []
            for t in range(64):
                w = ([prep_done, kn_done, (dvd, vo_ms)] + GUARD()) if t == 0 else                     ([(dvd, vb_done[t - 2])] if t >= 2 else [])
                p.emit("tensor", (lambda t, hp: lambda e: e.matmul(
                    g_ps[t % 2][0:128, 0:128], ckv_g[:, 128 * t:128 * t + 128],
                    veff_b[:, 128 * hp:128 * hp + 128], start=True, stop=True))(t, hp),
                    waits=w, incs=[(ped, 1)])
                [vb] = p.emit("vector", (lambda t: lambda e: e.tensor_copy(
                    two_run_out(vbuf[:, 130 * t:130 * t + 130]),
                    two_run_in(g_ps[t % 2][0:128, 0:128])))(t),
                    waits=[(ped, ped.v), (dvd, dvd.v)], incs=[(dvd, 1)])
                vb_done.append(vb)
            vo_done = []
            for t in range(16):
                w = [(dvd, vo_done[t - 2])] if t >= 2 else [(dvd, vb_done[0])]
                p.emit("tensor", (lambda t, hp: lambda e: e.matmul(
                    h_ps[t % 2][0:128, 0:128], u16[:, CKVO + 128 * t:CKVO + 128 * t + 128],
                    veff_b[:, 128 * hp:128 * hp + 128], start=True, stop=True))(t, hp),
                    waits=w, incs=[(ped, 1)])
                [vo] = p.emit("vector", (lambda t: lambda e: e.tensor_copy(
                    two_run_out(u16[:, VOWN + 130 * t:VOWN + 130 * t + 130]),
                    two_run_in(h_ps[t % 2][0:128, 0:128])))(t),
                    waits=[(ped, ped.v), (dvd, dvd.v)], incs=[(dvd, 1)])
                vo_done.append(vo)
            vready = (dvd, dvd.v)
            for k in range(8):
                job = jobs_a[jidx]
                job["wk0"] = [vready]
                if jidx == 0:
                    emit_prelude(job)
                if jidx + 1 < 64:
                    nxt = jobs_a[jidx + 1]
                    fl["pend_pre"] = (lambda nxt: lambda: emit_prelude(nxt))(nxt)
                emit_kloop(job)
                pv_done = (pvd, pvd.v)
                hpj, hhj, s4j = job["dst"]
                dst = xT[:, 2048 * hpj + 512 * s4j:2048 * hpj + 512 * s4j + 512][64 * hhj:64 * hhj + 64, :]
                fl["pend_epi"] = (lambda job, pv_done, dst: lambda: flash_epilogue(job, pv_done, dst_slice=dst))(job, pv_done, dst)
                jidx += 1
        if fl.get("pend_epi") is not None:
            fn = fl["pend_epi"]; fl["pend_epi"] = None; fn()
        state["stream_ready"] = (dvd, dvd.v)

    def emit_ffn(call_idx):
        cci, cco = (cc_pool_i, cc_pool_o) if call_idx == 0 else (cc_pool_i2, cc_pool_o2)
        emit_ln(True)
        xn = state["xn_ready"]
        [sv] = p.emit("sync", (lambda cci: lambda e: e.dma_start(out=cci.ap(), in_=pool_acc[:]))(cci),
                      waits=[state["pool_ready"]], incs=[(lp, 16)])
        [ccv] = p.emit("gpsimd", (lambda cci, cco: lambda e: e.collective_compute(
            "AllReduce", ALU.add, replica_groups=GROUPS, ins=[cci.ap().opt()], outs=[cco.ap().opt()]))(cci, cco),
            waits=[(lp, sv)], incs=[(ccd, 1)])
        [pl] = p.emit("sync", (lambda cco: lambda e: e.dma_start(out=pool_in[:], in_=cco.ap()))(cco),
                      waits=[(ccd, ccv)], incs=[(lp, 16)])
        pool_ld = (lp, pl)
        state["ld_guard"] = (lp, pl)
        gelu_at = [0, 0]
        for pr in range(4):
            for tt in range(4):
                bk = (4 * pr + tt) % 2
                for c in range(8):
                    w = []
                    if pr == 0 and tt == 0 and c == 0:
                        w = [xn, ffnw_ld] + GUARD()
                    elif c == 0:
                        w = [(acd, gelu_at[bk]), (dvd, dvd.v)]
                    p.emit("tensor", (lambda pr, tt, c, bk: lambda e: e.matmul(
                        g_ps[bk][0:128, 0:512], dwT_b[:, 512 * c + 128 * pr:512 * c + 128 * pr + 128],
                        xnT[:, 2048 * c + 512 * tt:2048 * c + 512 * tt + 512], start=(c == 0), stop=(c == 7)))(pr, tt, c, bk),
                        waits=w, incs=[(ped, 1)])
                [hv] = p.emit("scalar", (lambda pr, tt, bk: lambda e: e.activation(
                    out=ckv_g[:, 2048 * pr + 512 * tt:2048 * pr + 512 * tt + 512],
                    in_=g_ps[bk][0:128, 0:512], func=AF.Gelu, bias=bdr_s[:, pr:pr + 1]))(pr, tt, bk),
                    waits=[(ped, ped.v), (acd, acd.v), (smm, smm.v)], incs=[(acd, 1)])
                gelu_at[bk] = hv
        h_all = (acd, acd.v)
        for c in range(8):
            p.emit("tensor", (lambda c: lambda e: e.matmul(
                h_ps[0][0:1, 0:8], pool_in[:, c:c + 1], gwT_s[:, 8 * c:8 * c + 8], start=(c == 0), stop=(c == 7)))(c),
                waits=([pool_ld, const_ld] + GUARD()) if c == 0 else (), incs=[(ped, 1)])
        p.emit("vector", lambda e: e.tensor_scalar(gate_row[:], h_ps[0][0:1, 0:8], 1.0 / 8192.0, None, ALU.mult),
               waits=[(ped, ped.v), (dvd, dvd.v)], incs=[(dvd, 1)])
        p.emit("vector", lambda e: e.reduce_max(s11[:], gate_row[:], axis=AX.X), waits=[(dvd, dvd.v)], incs=[(dvd, 1)])
        p.emit("vector", lambda e: e.tensor_scalar(g2_row[:], gate_row[:], s11[0:1, 0:1], None, ALU.subtract),
               waits=[(dvd, dvd.v)], incs=[(dvd, 1)])
        p.emit("scalar", lambda e: e.activation(out=gate_row[:], in_=g2_row[:], func=AF.Exp),
               waits=[(dvd, dvd.v), (acd, acd.v)], incs=[(acd, 1)])
        p.emit("vector", lambda e: e.reduce_sum(s11[:], gate_row[:], axis=AX.X),
               waits=[(acd, acd.v), (dvd, dvd.v)], incs=[(dvd, 1)])
        p.emit("vector", lambda e: e.reciprocal(s11b[:], s11[:]), waits=[(dvd, dvd.v)], incs=[(dvd, 1)])
        p.emit("vector", lambda e: e.tensor_scalar(gate_row[:], gate_row[:], s11b[0:1, 0:1], None, ALU.mult),
               waits=[(dvd, dvd.v)], incs=[(dvd, 1)])
        p.emit("vector", lambda e: e.reduce_max(s11[:], gate_row[:], axis=AX.X), waits=[(dvd, dvd.v)], incs=[(dvd, 1)])
        p.emit("vector", lambda e: e.tensor_scalar(g2_row[:], gate_row[:], s11[0:1, 0:1], None, ALU.is_ge),
               waits=[(dvd, dvd.v)], incs=[(dvd, 1)])
        p.emit("vector", lambda e: e.scalar_tensor_tensor(g2_row[:], g2_row[:], -1e9, gate_row[:], ALU.mult, ALU.add),
               waits=[(dvd, dvd.v)], incs=[(dvd, 1)])
        p.emit("vector", lambda e: e.reduce_max(s11[:], g2_row[:], axis=AX.X), waits=[(dvd, dvd.v)], incs=[(dvd, 1)])
        p.emit("vector", lambda e: e.tensor_scalar(g2_row[:], gate_row[:], s11[0:1, 0:1], None, ALU.is_ge),
               waits=[(dvd, dvd.v)], incs=[(dvd, 1)])
        p.emit("vector", lambda e: e.tensor_tensor(mk_row[:], gate_row[:], g2_row[:], ALU.mult),
               waits=[(dvd, dvd.v)], incs=[(dvd, 1)])
        p.emit("vector", lambda e: e.reduce_sum(s11[:], mk_row[:], axis=AX.X), waits=[(dvd, dvd.v)], incs=[(dvd, 1)])
        p.emit("vector", lambda e: e.tensor_scalar(s11[:], s11[:], 1e-8, None, ALU.add), waits=[(dvd, dvd.v)], incs=[(dvd, 1)])
        p.emit("vector", lambda e: e.reciprocal(s11b[:], s11[:]), waits=[(dvd, dvd.v)], incs=[(dvd, 1)])
        [vmk] = p.emit("vector", lambda e: e.tensor_scalar(mk_row[:], mk_row[:], s11b[0:1, 0:1], None, ALU.mult),
                       waits=[(dvd, dvd.v)], incs=[(dvd, 1)])
        p.emit("tensor", lambda e: e.matmul(h_ps[1][0:128, 0:8], onesr_f[:], mk_row[:], start=True, stop=True),
               waits=[(dvd, vmk)] + GUARD(), incs=[(ped, 1)])
        p.emit("vector", lambda e: e.tensor_copy(mkb_s[:], h_ps[1][0:128, 0:8]),
               waits=[(ped, ped.v), (dvd, dvd.v)], incs=[(dvd, 1)])
        for pr in range(4):
            p.emit("vector", (lambda pr: lambda e: e.tensor_scalar(
                mvec_s[:, pr:pr + 1], halves_s[:, 0:1], mkb_s[:, 2 * pr:2 * pr + 1], None, ALU.mult))(pr),
                waits=[(dvd, dvd.v)], incs=[(dvd, 1)])
            p.emit("vector", (lambda pr: lambda e: e.scalar_tensor_tensor(
                mvec_s[:, pr:pr + 1], halves_s[:, 1:2], mkb_s[:, 2 * pr + 1:2 * pr + 2],
                mvec_s[:, pr:pr + 1], ALU.mult, ALU.add))(pr),
                waits=[(dvd, dvd.v)], incs=[(dvd, 1)])
        mvec_done = (dvd, dvd.v)
        for c in range(8):
            p.emit("vector", (lambda c: lambda e: e.tensor_scalar(
                bumix_col[:, c:c + 1], upb_cm_s[:, 8 * c:8 * c + 1], mkb_s[:, 0:1], None, ALU.mult))(c),
                waits=[(dvd, dvd.v)], incs=[(dvd, 1)])
            for e8 in range(1, 8):
                p.emit("vector", (lambda c, e8: lambda e: e.scalar_tensor_tensor(
                    bumix_col[:, c:c + 1], upb_cm_s[:, 8 * c + e8:8 * c + e8 + 1], mkb_s[:, e8:e8 + 1],
                    bumix_col[:, c:c + 1], ALU.mult, ALU.add))(c, e8),
                    waits=[(dvd, dvd.v)], incs=[(dvd, 1)])
        bumix_done = (dvd, dvd.v)
        for pr in range(4):
            p.emit("vector", (lambda pr: lambda e: e.tensor_scalar(
                wu_sc[:, 1024 * pr:1024 * pr + 1024], upT_b[:, 1024 * pr:1024 * pr + 1024],
                mvec_s[:, pr:pr + 1], None, ALU.mult))(pr),
                waits=([mvec_done, ffnw_ld, (ped, ped.v)] if pr == 0 else [(dvd, dvd.v)]), incs=[(dvd, 1)])
        wu_all = (dvd, dvd.v)
        res_bank_free = [0, 0]
        for tt in range(4):
            for c in range(8):
                bk = (4 * tt + c) % 2
                for pr in range(4):
                    w = []
                    if tt == 0 and c == 0 and pr == 0:
                        w = [wu_all, h_all, bumix_done] + GUARD()
                    elif pr == 0:
                        w = [(dvd, res_bank_free[bk])]
                    p.emit("tensor", (lambda tt, c, pr, bk: lambda e: e.matmul(
                        g_ps[bk][0:128, 0:512], wu_sc[:, 1024 * pr + 128 * c:1024 * pr + 128 * c + 128],
                        ckv_g[:, 2048 * pr + 512 * tt:2048 * pr + 512 * tt + 512], start=(pr == 0), stop=(pr == 3)))(tt, c, pr, bk),
                        waits=w, incs=[(ped, 1)])
                [rv] = p.emit("vector", (lambda tt, c, bk: lambda e: e.scalar_tensor_tensor(
                    xT[:, 2048 * c + 512 * tt:2048 * c + 512 * tt + 512],
                    g_ps[bk][0:128, 0:512], bumix_col[:, c:c + 1],
                    xT[:, 2048 * c + 512 * tt:2048 * c + 512 * tt + 512], ALU.add, ALU.add))(tt, c, bk),
                    waits=[(ped, ped.v), (dvd, dvd.v)], incs=[(dvd, 1)])
                res_bank_free[bk] = rv
        state["stream_ready"] = (dvd, dvd.v)

    if stages >= 2:
        emit_ffn(0)

    if stages >= 3:
        emit_ln(False)
        xn3 = state["xn_ready"]
        [sv] = p.emit("sync", lambda e: e.dma_start(
            out=cc_xn_i.ap().rearrange("(c p) t -> p c t", p=128), in_=xnT[:].rearrange("p (c t) -> p c t", c=8)), waits=[xn3], incs=[(lx, 16)])
        [ccx] = p.emit("gpsimd", lambda e: e.collective_compute(
            "AllGather", ALU.bypass, replica_groups=GROUPS, ins=[cc_xn_i.ap().opt()], outs=[cc_xn_o.ap().opt()]),
            waits=[(lx, sv)], incs=[(ccd, 1)])
        xn_ag = (ccd, ccx)

        for pr2 in range(2):
            [wq_ld] = p.emit("gpsimd", (lambda pr2: lambda e: e.dma_start(
                out=wqkv_b[:].rearrange("p (h c f) -> p h c f", h=2, c=8), in_=wqkv[2 * pr2:2 * pr2 + 2, :, :].rearrange("h (c p) f -> p h c f", p=128)))(pr2),
                waits=[(pvd, pvd.v), (expd, expd.v), (smm, smm.v), (dvd, dvd.v), (acd, acd.v), (ped, ped.v)], incs=[(lds, 16)])
            Qd = [lambda g: xnT[:, 512 * g:512 * g + 512][0:64, :],
                  lambda g: ckv_g[:, 512 * g:512 * g + 512][64:128, :]]
            Kd = [lambda g: xnT[:, 8192 + 512 * g:8192 + 512 * g + 512][0:64, :],
                  lambda g: xnT[:, 8192 + 512 * g:8192 + 512 * g + 512][64:128, :]]
            Vd = [lambda g: xnT[:, 512 * g:512 * g + 512][64:128, :],
                  lambda g: ckv_g[:, 512 * g:512 * g + 512][0:64, :]]
            Kc = [lambda t: xnT[:, 8192 + 128 * t:8192 + 128 * t + 128][0:64, :],
                  lambda t: xnT[:, 8192 + 128 * t:8192 + 128 * t + 128][64:128, :]]
            stage_free = [0, 0]
            evac_free = [0, 0]
            for g in range(16):
                i4, s4 = g % 4, g // 4
                st = ST0 if g % 2 == 0 else ST1
                w = [xn_ag, (pvd, pvd.v), (expd, expd.v), (smm, smm.v), (dvd, dvd.v)] if g < 2 else [(ped, stage_free[g % 2])]
                [stl] = p.emit("sync", (lambda i4, s4, st: lambda e: e.dma_start(
                    out=u16[:, st:st + 4096].rearrange("p (c t) -> p c t", c=8),
                    in_=cc_xn_o.ap()[1024 * i4:1024 * i4 + 1024, 512 * s4:512 * s4 + 512].rearrange("(c p) t -> p c t", p=128)))(i4, s4, st),
                    waits=w, incs=[(lst[g % 2], 16)])
                for hh in range(2):
                    bk = g % 2
                    qb = 64 * hh
                    wqk = [(lst[g % 2], stl), (lds, wq_ld), (acd, evac_free[bk]), (ped, ped.v)]
                    if g < 2 and hh == 0:
                        wqk += GUARD()
                    for c in range(8):
                        p.emit("tensor", (lambda hh, c, st, bk, qb: lambda e: e.matmul(
                            g_ps[bk][qb:qb + 64, 0:512], wqkv_b[:, 1536 * hh + 192 * c:1536 * hh + 192 * c + 64],
                            u16[:, st + 512 * c:st + 512 * c + 512], start=(c == 0), stop=(c == 7),
                            tile_position=(0, qb)))(hh, c, st, bk, qb),
                            waits=wqk if c == 0 else (), incs=[(ped, 1)])
                    for c in range(8):
                        p.emit("tensor", (lambda hh, c, st, bk: lambda e: e.matmul(
                            h_ps[bk][0:128, 0:512], wqkv_b[:, 1536 * hh + 192 * c + 64:1536 * hh + 192 * c + 192],
                            u16[:, st + 512 * c:st + 512 * c + 512], start=(c == 0), stop=(c == 7)))(hh, c, st, bk),
                            incs=[(ped, 1)])
                    pe_g = (ped, ped.v)
                    p.emit("scalar", (lambda hh, g, bk, pr2, qb: lambda e: e.activation(
                        out=Qd[hh](g), in_=g_ps[bk][qb:qb + 64, 0:512],
                        func=AF.Identity, bias=bq_s[qb:qb + 64, 2 * pr2 + hh:2 * pr2 + hh + 1]))(hh, g, bk, pr2, qb),
                        waits=[pe_g, (acd, acd.v)], incs=[(acd, 1)])
                    p.emit("scalar", (lambda hh, g, bk, pr2, qb: lambda e: e.activation(
                        out=Kd[hh](g), in_=h_ps[bk][qb:qb + 64, 0:512],
                        func=AF.Identity, bias=bk_s[qb:qb + 64, 2 * pr2 + hh:2 * pr2 + hh + 1]))(hh, g, bk, pr2, qb),
                        incs=[(acd, 1)])
                    vb0 = 64 * (1 - hh)
                    [ve] = p.emit("scalar", (lambda hh, g, bk, pr2, vb0: lambda e: e.activation(
                        out=Vd[hh](g), in_=h_ps[bk][vb0:vb0 + 64, 0:512],
                        func=AF.Identity, bias=bv_s[vb0:vb0 + 64, 2 * pr2 + hh:2 * pr2 + hh + 1]))(hh, g, bk, pr2, vb0),
                        incs=[(acd, 1)])
                    evac_free[bk] = ve
                stage_free[g % 2] = ped.v
            qkv_all = (acd, acd.v)
            [vms] = p.emit("vector", lambda e: e.memset(vbuf[:], 1.0),
                           waits=[(pvd, pvd.v), (dvd, dvd.v)], incs=[(dvd, 1)])
            two_run_out2 = lambda ap130: ap130.rearrange("p (a b) -> p a b", b=65)[:, :, 0:64]
            vt_done = []
            for t in range(64):
                for hh in range(2):
                    w = ([qkv_all, (dvd, vms), (expd, expd.v), (smm, smm.v), (ped, ped.v)] if (t == 0 and hh == 0)
                         else ([(dvd, vt_done[2 * t + hh - 2])] if 2 * t + hh >= 2 else []))
                    vsrc = (xnT[:, 128 * t:128 * t + 128][64:128, :] if hh == 0
                            else ckv_g[:, 128 * t:128 * t + 128][0:64, :])
                    ident_sl = id2_s[64:128, :] if hh == 0 else id2_s[0:64, :]
                    p.emit("tensor", (lambda t, hh, vsrc, ident_sl: lambda e: e.transpose(
                        s_ps[(2 * t + hh) % 2][0:128, 0:32].bitcast(BF16)[:, 0:64], vsrc, ident_sl))(t, hh, vsrc, ident_sl),
                        waits=w, incs=[(ped, 1)])
                    [vt] = p.emit("vector", (lambda t, hh: lambda e: e.tensor_copy(
                        vbuf[:, 130 * t + 65 * hh:130 * t + 65 * hh + 64],
                        s_ps[(2 * t + hh) % 2][0:128, 0:32].bitcast(BF16)[:, 0:64]))(t, hh),
                        waits=[(ped, ped.v), (dvd, dvd.v)], incs=[(dvd, 1)])
                    vt_done.append(vt)
            fl["sfree"] = [expd.v, expd.v]
            vb_ready = (dvd, dvd.v)
            for hh in range(2):
                qb = 64 * hh
                for cc in range(16):
                    p.emit("vector", (lambda hh, cc, qb: lambda e: e.tensor_tensor(
                        sq_b[qb:qb + 64, :], Kd[hh](cc), Kd[hh](cc), ALU.mult))(hh, cc, qb),
                        waits=[qkv_all, (dvd, dvd.v)], incs=[(dvd, 1)])
                    p.emit("tensor", (lambda hh, cc, qb: lambda e: e.matmul(
                        g_ps[cc % 2][0:1, 0:512], ones_col_b[qb:qb + 64, :], sq_b[qb:qb + 64, :], start=True, stop=True))(hh, cc, qb),
                        waits=[(dvd, dvd.v)] + (GUARD() if (hh == 0 and cc == 0) else [(ped, ped.v), (dvd, dvd.v)]),
                        incs=[(ped, 1)])
                    if cc == 0:
                        p.emit("vector", (lambda hh, cc: lambda e: e.reduce_max(
                            kmax_s[0:1, 1 + hh:2 + hh], g_ps[cc % 2][0:1, 0:512], axis=AX.X))(hh, cc),
                            waits=[(ped, ped.v), (dvd, dvd.v)], incs=[(dvd, 1)])
                    else:
                        p.emit("vector", (lambda hh, cc: lambda e: e.reduce_max(
                            z_s[0:1, 0:1], g_ps[cc % 2][0:1, 0:512], axis=AX.X))(hh, cc),
                            waits=[(ped, ped.v), (dvd, dvd.v), (acd, fl["z_free"])], incs=[(dvd, 1)])
                        p.emit("vector", (lambda hh: lambda e: e.tensor_max(
                            kmax_s[0:1, 1 + hh:2 + hh], kmax_s[0:1, 1 + hh:2 + hh], z_s[0:1, 0:1]))(hh),
                            waits=[(dvd, dvd.v)], incs=[(dvd, 1)])
            knb_done = (dvd, dvd.v)
            def mk_job_b(hh, jb):
                return {
                    "qp": 64, "qb": 64 * hh, "nA": 4 * jb, "moff0": MOFF_B, "kn_col": 1 + hh,
                    "wkn": [knb_done],
                    "ql_ap": Qd[hh](jb),
                    "wq": [qkv_all, vb_ready],
                    "wk0": [],
                    "lhsA": (lambda t, hh=hh: Kc[hh](t)),
                    "lhsB": (lambda i, hh=hh, jb=jb: Kc[hh](4 * jb + i)),
                    "vA": (lambda t, hh=hh: vbuf[:, 130 * t + 65 * hh:130 * t + 65 * hh + 65]),
                    "vB": (lambda i, hh=hh, jb=jb: vbuf[:, 130 * (4 * jb + i) + 65 * hh:
                                                        130 * (4 * jb + i) + 65 * hh + 65]),
                    "dst": (hh, jb),
                }
            jobs_b = [mk_job_b(hh, jb) for hh in range(2) for jb in range(16)]
            for ji, job in enumerate(jobs_b):
                if ji == 0:
                    emit_prelude(job)
                if ji + 1 < len(jobs_b):
                    fl["pend_pre"] = (lambda nxt: lambda: emit_prelude(nxt))(jobs_b[ji + 1])
                emit_kloop(job)
                pv_done = (pvd, pvd.v)
                hhj, jbj = job["dst"]
                hl = 2 * pr2 + hhj
                own, slot = jbj % 4, jbj // 4
                dram_fn = (lambda own, slot, hl: lambda e, ybuf: e.dma_start(
                    out=cc_y_i.ap()[256 * own + 64 * hl:256 * own + 64 * hl + 64, 512 * slot:512 * slot + 512],
                    in_=ybuf[:]))(own, slot, hl)
                fl["pend_epi"] = (lambda job, pv_done, dram_fn: lambda: flash_epilogue(job, pv_done, dram_fn=dram_fn))(job, pv_done, dram_fn)
        if fl.get("pend_epi") is not None:
            _fn = fl["pend_epi"]
            fl["pend_epi"] = None
            _fn()
        [ccy] = p.emit("gpsimd", lambda e: e.collective_compute(
            "AllGather", ALU.bypass, replica_groups=GROUPS, ins=[cc_y_i.ap().opt()], outs=[cc_y_o.ap().opt()]),
            waits=[(ly[0], fl["ynrm_free"][0]), (ly[1], fl["ynrm_free"][1])], incs=[(ccd, 1)])
        y_ag = (ccd, ccy)
        for tt in range(4):
            for c in range(8):
                i_rk = c // 2
                hl0 = (2 * c) % 4
                def ysrc(o):
                    return cc_y_o.ap()[1024 * i_rk + 256 * o + 64 * hl0:
                                       1024 * i_rk + 256 * o + 64 * hl0 + 128, 512 * tt:512 * tt + 512]
                w0 = [y_ag, (dvd, dvd.v)] if (tt == 0 and c == 0) else [(dvd, dvd.v)]
                [l0] = p.emit("sync", (lambda a=ysrc(0): lambda e: e.dma_start(out=cand[0][:], in_=a))(),
                              waits=w0, incs=[(lc, 16)])
                [l1] = p.emit("sync", (lambda a=ysrc(1): lambda e: e.dma_start(out=cand[1][:], in_=a))(),
                              incs=[(lc, 16)])
                p.emit("vector", lambda e: e.tensor_scalar(cand[0][:], cand[0][:], ind_s[:, 0:1], None, ALU.mult),
                       waits=[(lc, l1), (dvd, dvd.v), (smm, smm.v), (ped, ped.v)], incs=[(dvd, 1)])
                [a1] = p.emit("vector", lambda e: e.scalar_tensor_tensor(
                    cand[0][:], cand[1][:], ind_s[:, 1:2], cand[0][:], ALU.mult, ALU.add),
                    waits=[(dvd, dvd.v)], incs=[(dvd, 1)])
                [l2] = p.emit("sync", (lambda a=ysrc(2): lambda e: e.dma_start(out=cand[1][:], in_=a))(),
                              waits=[(dvd, a1)], incs=[(lc, 16)])
                p.emit("vector", lambda e: e.scalar_tensor_tensor(
                    cand[0][:], cand[1][:], ind_s[:, 2:3], cand[0][:], ALU.mult, ALU.add),
                    waits=[(lc, l2), (dvd, dvd.v)], incs=[(dvd, 1)])
                [l3] = p.emit("sync", (lambda a=ysrc(3): lambda e: e.dma_start(out=cand[1][:], in_=a))(),
                              waits=[(dvd, dvd.v)], incs=[(lc, 16)])
                p.emit("vector", lambda e: e.scalar_tensor_tensor(
                    cand[0][:], cand[1][:], ind_s[:, 3:4], cand[0][:], ALU.mult, ALU.add),
                    waits=[(lc, l3), (dvd, dvd.v)], incs=[(dvd, 1)])
                p.emit("vector", (lambda c, tt: lambda e: e.tensor_copy(
                    xnT[:, 2048 * c + 512 * tt:2048 * c + 512 * tt + 512], cand[0][:]))(c, tt),
                    waits=[(dvd, dvd.v)], incs=[(dvd, 1)])
        ysel_all = (dvd, dvd.v)
        res_bank_free = [0, 0]
        for do in range(8):
            [wl] = p.emit("gpsimd", (lambda do: lambda e: e.dma_start(
                out=woutc_b[do % 2][:].rearrange("p (c f) -> p c f", c=8), in_=wout[:, 128 * do:128 * do + 128].rearrange("(c p) f -> p c f", p=128)))(do),
                waits=[(ped, ped.v), (smm, smm.v)], incs=[(lds, 16)])
            for tt in range(4):
                bk = (4 * do + tt) % 2
                w = [(lds, wl)]
                if do == 0 and tt == 0:
                    w += [ysel_all] + GUARD()
                else:
                    w += [(dvd, res_bank_free[bk])]
                for c in range(8):
                    p.emit("tensor", (lambda do, tt, c, bk: lambda e: e.matmul(
                        g_ps[bk][0:128, 0:512], woutc_b[do % 2][:, 128 * c:128 * c + 128],
                        xnT[:, 2048 * c + 512 * tt:2048 * c + 512 * tt + 512], start=(c == 0), stop=(c == 7)))(do, tt, c, bk),
                        waits=w if c == 0 else (), incs=[(ped, 1)])
                [rv] = p.emit("vector", (lambda do, tt, bk: lambda e: e.scalar_tensor_tensor(
                    xT[:, 2048 * do + 512 * tt:2048 * do + 512 * tt + 512],
                    g_ps[bk][0:128, 0:512], bout_cm_s[:, do:do + 1],
                    xT[:, 2048 * do + 512 * tt:2048 * do + 512 * tt + 512], ALU.add, ALU.add))(do, tt, bk),
                    waits=[(ped, ped.v), (dvd, dvd.v)], incs=[(dvd, 1)])
                res_bank_free[bk] = rv
        state["stream_ready"] = (dvd, dvd.v)

    if stages >= 4:
        emit_ffn(1)

    [so] = p.emit("sync", lambda e: e.dma_start(
        out=out_feat.rearrange("(c p) t -> p c t", p=128), in_=xT[:].rearrange("p (c t) -> p c t", c=8)),
        waits=[state["stream_ready"]], incs=[(std, 16)])
    p.emit("gpsimd", lambda e: e.engine_nop(), waits=[(std, so)])

    p.replay()
    return nc


_NC_CACHE = {}


def _get_nc(stages=4):
    if stages not in _NC_CACHE:
        _NC_CACHE[stages] = _build(stages)
    return _NC_CACHE[stages]


def _make_inmaps(inputs):
    f32 = np.float32
    import ml_dtypes
    bf = ml_dtypes.bfloat16
    C = np.ascontiguousarray
    x = np.asarray(inputs["L0_attn_a_x"], f32)
    W_dq = np.asarray(inputs["L0_attn_a_W_dq"], f32)
    W_uq = np.asarray(inputs["L0_attn_a_W_uq"], f32)
    W_dkv = np.asarray(inputs["L0_attn_a_W_dkv"], f32)
    W_uk = np.asarray(inputs["L0_attn_a_W_uk"], f32)
    W_uv = np.asarray(inputs["L0_attn_a_W_uv"], f32)
    W_o = np.asarray(inputs["L0_attn_a_W_o"], f32)
    Wqkv = np.asarray(inputs["L0_attn_b_Wqkv"], f32)
    bqkv = np.asarray(inputs["L0_attn_b_bqkv"], f32)
    Wout = np.asarray(inputs["L0_attn_b_Wout"], f32)
    bout = np.asarray(inputs["L0_attn_b_bout"], f32)
    gate_w = np.asarray(inputs["L0_ffn_gate_w"], f32)
    down_w = np.asarray(inputs["L0_ffn_down_w"], f32)
    down_b = np.asarray(inputs["L0_ffn_down_b"], f32)
    up_w = np.asarray(inputs["L0_ffn_up_w"], f32)
    up_b = np.asarray(inputs["L0_ffn_up_b"], f32)

    bdr = np.zeros((128, 4), f32)
    for pr in range(4):
        bdr[0:64, pr] = down_b[2 * pr]
        bdr[64:128, pr] = down_b[2 * pr + 1]
    shared = {
        "w_dq": C(W_dq), "w_dqT": C(W_dq.T), "w_uq": C(W_uq), "w_uk": C(W_uk), "w_uv": C(W_uv),
        "w_oT": C(W_o.T), "w_dkvT": C(W_dkv.T),
        "wout": C(Wout), "bout_cm": C(bout.reshape(8, 128).T),
        "gate_wT": C(gate_w.T),
        "dwT": C(down_w.transpose(2, 0, 1).reshape(1024, 512)),
        "bd_rows": bdr,
        "upT": C(up_w.transpose(0, 2, 1).reshape(512, 1024)),
        "upb_cm": C(up_b.reshape(8, 8, 128).transpose(2, 1, 0).reshape(128, 64)),
        "halves": np.concatenate([(np.arange(128) < 64).astype(f32)[:, None],
                                  (np.arange(128) >= 64).astype(f32)[:, None]], axis=1),
        "tri": C(np.triu(np.ones((128, 128))).astype(bf)),
        "ident": C(np.eye(128).astype(bf)),
        "onesf_r128": np.ones((1, 128), f32),
        "negonesf_r128": -np.ones((1, 128), f32),
        "onesf_r64": np.ones((1, 64), f32),
        "ident64f": np.eye(64, dtype=f32),
        "ident2": C(np.vstack([np.eye(64), np.eye(64)]).astype(bf)),
    }
    in_maps = []
    for c in range(8):
        B, R = c // 4, c % 4
        im = dict(shared)
        xo = np.empty((2048, 1024), f32)
        for s in range(4):
            j = 4 * s + R
            xo[512 * s:512 * s + 512] = x[B, 512 * j:512 * j + 512]
        im["x_own"] = C(xo.T)
        mo = np.zeros((128, 209), f32)
        for s in range(4):
            nA = 16 * s + 12
            real = 16 * s + 4 * R
            col = MOFF0[s]
            if real < nA:
                mo[:, col + real:col + nA] = NEG
        im["moff"] = mo
        wq4 = Wqkv[4 * R:4 * R + 4].copy()
        for hl in range(1, 4, 2):
            kcols = wq4[hl, :, 64:128].copy()
            wq4[hl, :, 64:128] = wq4[hl, :, 128:192]
            wq4[hl, :, 128:192] = kcols
        im["wqkv"] = C(wq4)
        bq = np.zeros((128, 4), f32)
        bk = np.zeros((128, 4), f32)
        bv = np.zeros((128, 4), f32)
        for hl in range(4):
            qb = 64 * (hl % 2)
            vb = 64 * (1 - hl % 2)
            bq[qb:qb + 64, hl] = bqkv[4 * R + hl, 0:64]
            bk[qb:qb + 64, hl] = bqkv[4 * R + hl, 64:128]
            bv[vb:vb + 64, hl] = bqkv[4 * R + hl, 128:192]
        im["bq_in"] = bq
        im["bk_in"] = bk
        im["bv_in"] = bv
        indm = np.zeros((128, 4), f32)
        indm[:, R] = 1.0
        im["ind"] = indm
        in_maps.append(im)
    return in_maps


def kernel(**inputs):
    stages = int(inputs.pop("_stages", 4))
    nc = _get_nc(stages)
    in_maps = _make_inmaps(inputs)
    res = run_bass_kernel_spmd(nc, in_maps, core_ids=list(range(8)))
    out = np.empty((2, 8192, 1024), np.float32)
    for c in range(8):
        B, R = c // 4, c % 4
        shard = np.asarray(res.results[c]["out_feat"]).T
        for s in range(4):
            j = 4 * s + R
            out[B, 512 * j:512 * j + 512] = shard[512 * s:512 * s + 512]
    return out


# version 08
# speedup vs baseline: 1.1835x; 1.1835x over previous
import math
import numpy as np
import concourse.bass as bass
import concourse.mybir as mybir
from concourse.bass_utils import run_bass_kernel_spmd
from contextlib import ExitStack

F32 = mybir.dt.float32
BF16 = mybir.dt.bfloat16
AF = mybir.ActivationFunctionType
ALU = mybir.AluOpType
AX = mybir.AxisListType

NEG = -1.0e9
EPS_LN = 1e-5
GROUPS = [[0, 1, 2, 3], [4, 5, 6, 7]]
MOFF0 = [0, 13, 42, 87]
MOFF_B = 148


class SemC:
    __slots__ = ("h", "v", "name")

    def __init__(self, h, name):
        self.h = h
        self.v = 0
        self.name = name


class Prog:
    ENGS = ("sync", "tensor", "scalar", "vector", "gpsimd")

    def __init__(self, nc, es):
        self.nc = nc
        self.es = es
        self.q = {e: [] for e in self.ENGS}

    def sem(self, name):
        return SemC(self.es.enter_context(self.nc.semaphore(name)), name)

    def emit(self, eng, fn, waits=(), incs=()):
        out_vals = []
        rec = []
        for s, n in incs:
            s.v += n
            out_vals.append(s.v)
            rec.append((s.h, n))
        ws = []
        seen = {}
        for s, v in waits:
            if v and v > 0 and seen.get(id(s), -1) < v:
                seen[id(s)] = v
        for s, v in waits:
            if seen.get(id(s), -1) == v and v and v > 0:
                ws.append((s.h, v))
                seen[id(s)] = -2
        self.q[eng].append((tuple(ws), fn, tuple(rec)))
        return out_vals

    def replay(self):
        nc = self.nc
        with nc.Block() as block:
            def mk(engq):
                def _(eng):
                    for waits, fn, incs in self.q[engq]:
                        for (h, v) in waits:
                            eng.wait_ge(h, v)
                        if fn is None:
                            continue
                        ins = fn(eng)
                        for (h, n) in incs:
                            ins = ins.then_inc(h, n)
                return _
            block.sync(mk("sync"))
            block.tensor(mk("tensor"))
            block.scalar(mk("scalar"))
            block.vector(mk("vector"))
            block.gpsimd(mk("gpsimd"))


def _build(stages=4):
    nc = bass.Bass("TRN2", target_bir_lowering=False)
    D = lambda n, sh, dt=F32, kind="ExternalInput": nc.dram_tensor(n, sh, dt, kind=kind).ap()
    x_own = D("x_own", [1024, 2048])
    w_dq = D("w_dq", [128, 1024])
    w_dqT = D("w_dqT", [1024, 128])
    w_uq = D("w_uq", [1024, 128])
    w_uk = D("w_uk", [1024, 128])
    w_uv = D("w_uv", [1024, 128])
    w_oT = D("w_oT", [1024, 1024])
    w_dkvT = D("w_dkvT", [1024, 128])
    wqkv = D("wqkv", [4, 1024, 192])
    bq_in = D("bq_in", [128, 4])
    bk_in = D("bk_in", [128, 4])
    bv_in = D("bv_in", [128, 4])
    ident64f = D("ident64f", [64, 64])
    ident2 = D("ident2", [128, 64], BF16)
    wout = D("wout", [1024, 1024])
    bout_cm = D("bout_cm", [128, 8])
    gate_wT = D("gate_wT", [1024, 8])
    dwT = D("dwT", [1024, 512])
    bd_rows = D("bd_rows", [128, 4])
    upT = D("upT", [512, 1024])
    upb_cm = D("upb_cm", [128, 64])
    halves = D("halves", [128, 2])
    tri = D("tri", [128, 128], BF16)
    ident = D("ident", [128, 128], BF16)
    moff = D("moff", [128, 209])
    onesf_r128 = D("onesf_r128", [1, 128])
    negonesf_r128 = D("negonesf_r128", [1, 128])
    onesf_r64 = D("onesf_r64", [1, 64])
    ind = D("ind", [128, 4])
    out_feat = D("out_feat", [1024, 2048], kind="ExternalOutput")

    cc_ckv_i = nc.dram_tensor("cc_ckv_i", [128, 2048], BF16)
    cc_ckv_o = nc.dram_tensor("cc_ckv_o", [512, 2048], BF16)
    cc_pool_i = nc.dram_tensor("cc_pool_i", [128, 8], F32)
    cc_pool_o = nc.dram_tensor("cc_pool_o", [128, 8], F32)
    cc_pool_i2 = nc.dram_tensor("cc_pool_i2", [128, 8], F32)
    cc_pool_o2 = nc.dram_tensor("cc_pool_o2", [128, 8], F32)
    cc_xn_i1 = nc.dram_tensor("cc_xn_i1", [1024, 1024], BF16)
    cc_xn_i2 = nc.dram_tensor("cc_xn_i2", [1024, 1024], BF16)
    cc_xn_o1 = nc.dram_tensor("cc_xn_o1", [4096, 1024], BF16)
    cc_xn_o2 = nc.dram_tensor("cc_xn_o2", [4096, 1024], BF16)
    cc_y_i = nc.dram_tensor("cc_y_i", [1024, 2048], BF16)
    cc_y_o = nc.dram_tensor("cc_y_o", [4096, 2048], BF16)

    es = ExitStack()
    S = lambda n, sh, dt=F32: es.enter_context(nc.sbuf_tensor(n, sh, dt))
    P = lambda n, sh, dt=F32: es.enter_context(nc.psum_tensor(n, sh, dt))

    xT = S("xT", [128, 16384])
    xnT = S("xnT", [128, 16384], BF16)
    vbuf = S("vbuf", [128, 130 * 64], BF16)
    u16 = S("u16", [128, 8224], BF16)
    CKVO, Q1, AB, VOWN = 0, 2048, 4096, 6144
    ST0, ST1 = 0, 4096
    wqkv_b = S("wqkv_b", [128, 3072], BF16)
    veff_b = wqkv_b[:, 0:1024]
    woutc_b = [S(f"woutc_b{i}", [128, 1024], BF16) for i in range(2)]
    wdqT_b = woutc_b[0][:]
    wdkvT_b = woutc_b[1][:]
    gwT_s = S("gwT_s", [128, 64])
    upb_cm_s = S("upb_cm_s", [128, 64])
    bout_cm_s = S("bout_cm_s", [128, 8])
    halves_s = S("halves_s", [128, 2])
    mkb_s = S("mkb_s", [128, 8])
    bumix_col = S("bumix_col", [128, 8])
    bdr_s = S("bdr_s", [128, 4])
    bq_s = S("bq_s", [128, 4])
    bk_s = S("bk_s", [128, 4])
    bv_s = S("bv_s", [128, 4])
    id64f_s = S("id64f_s", [64, 64])
    id2_s = S("id2_s", [128, 64], BF16)
    tri_b = S("tri_b", [128, 128], BF16)
    id_b = S("id_b", [128, 128], BF16)
    moff_s = S("moff_s", [128, 209])
    onesr_f = S("onesr_f", [1, 128])
    negonesr_f = S("negonesr_f", [1, 128])
    onesr64_f = S("onesr64_f", [1, 64])
    ind_s = S("ind_s", [128, 4])
    ones_col_b = S("ones_col_b", [128, 1], BF16)
    eps_s = S("eps_s", [1, 1])
    negln8_s = S("negln8_s", [1, 1])
    kmax_s = S("kmax_s", [1, 4])
    z_s = S("z_s", [1, 2])
    mhat_s = S("mhat_s", [1, 1])
    mneg = [S(f"mneg{i}", [128, 1]) for i in range(2)]
    bias_all = [S(f"bias_all{i}", [128, 64]) for i in range(2)]
    qlb = [S(f"qlb{i}", [128, 512], BF16) for i in range(2)]
    psb = [S(f"psb{i}", [128, 1024], BF16) for i in range(2)]
    sq_b = S("sq_b", [128, 512], BF16)
    xb_t = S("xb_t", [128, 512], BF16)
    m_row = S("m_row", [1, 512])
    q_row = S("q_row", [1, 512])
    r_row = S("r_row", [1, 512])
    u_sb = S("u_sb", [128, 512])
    ytmp_sb = u_sb[0:64, :]
    rec_sb = m_row[0:1, :]
    sums_sb = q_row[0:1, :]
    onesf_row = S("onesf_row", [1, 512])
    bc_sb = S("bc_sb", [64, 512])
    ynrm_b = [S(f"ynrm_b{i}", [64, 512], BF16) for i in range(2)]
    pool_acc = S("pool_acc", [128, 8])
    pool_in = S("pool_in", [128, 8])
    pool_t = S("pool_t", [128, 1])
    gate_row = S("gate_row", [1, 8])
    g2_row = S("g2_row", [1, 8])
    mk_row = S("mk_row", [1, 8])
    s11 = S("s11", [1, 1])
    s11b = S("s11b", [1, 1])
    mvec_s = S("mvec_s", [128, 4])
    cand = [S(f"cand{i}", [128, 512], BF16) for i in range(2)]
    scr = S("scr", [128, 10240])
    scr_b = scr[:].bitcast(BF16)
    ckv_g = scr_b[:, 0:8192]
    dwT_b = scr_b[:, 8192:12288]
    upT_b = scr_b[:, 12288:16384]
    wu_sc = scr_b[:, 16384:20480]

    spair = [P(f"spair{i}", [128, 1024]) for i in range(2)]
    y_ps = [P(f"y_ps{i}", [128, 512]) for i in range(2)]
    g_ps = [P(f"g_ps{i}", [128, 512]) for i in range(2)]
    h_ps = [spair[0][:, 0:512], spair[0][:, 512:1024]]
    s_ps = [spair[0][:, 0:512], spair[1][:, 0:512]]

    p = Prog(nc, es)
    ld = p.sem("ld")
    lk = p.sem("lk")
    lp = p.sem("lp")
    lx = p.sem("lx")
    lx2 = p.sem("lx2")
    lst = [p.sem("lst0"), p.sem("lst1")]
    ly = [p.sem("ly0"), p.sem("ly1")]
    lc = p.sem("lc")
    lds = p.sem("lds")
    smm = p.sem("smm")
    expd = p.sem("expd")
    pvd = p.sem("pvd")
    dvd = p.sem("dvd")
    ped = p.sem("ped")
    acd = p.sem("acd")
    ccd = p.sem("ccd")
    plr = p.sem("plr")
    std = p.sem("std")

    def GUARD():
        return [(ped, ped.v), (acd, acd.v), (dvd, dvd.v), (pvd, pvd.v), (expd, expd.v), (smm, smm.v)]

    p.emit("sync", lambda e: e.dma_start(out=xT[:].rearrange("p (c t) -> p c t", c=8), in_=x_own.rearrange("(c p) t -> p c t", p=128)), incs=[(ld, 16)])
    x_ld = None
    for dst, src in ((tri_b, tri), (id_b, ident), (moff_s, moff), (onesr_f, onesf_r128),
                     (negonesr_f, negonesf_r128), (onesr64_f, onesf_r64), (ind_s, ind),
                     (upb_cm_s, upb_cm), (bout_cm_s, bout_cm), (halves_s, halves), (bdr_s, bd_rows), (bq_s, bq_in), (bk_s, bk_in), (bv_s, bv_in), (id64f_s, ident64f), (id2_s, ident2)):
        p.emit("sync", (lambda dst, src: lambda e: e.dma_start(out=dst[:], in_=src[:]))(dst, src), incs=[(ld, 16)])
    p.emit("sync", lambda e: e.dma_start(out=gwT_s[:].rearrange("p (c e) -> p c e", c=8), in_=gate_wT.rearrange("(c p) e -> p c e", p=128)), incs=[(ld, 16)])
    const_ld = None
    SC_WUQ, SC_WUK, SC_WUV, SC_WDQ, SC_J, SC_KEFF = 0, 1024, 2048, 3072, 4096, 4224
    SC_WOT = 5248
    p.emit("sync", lambda e: e.dma_start(out=scr[:, SC_WUQ:SC_WUQ + 1024].rearrange("p (c j) -> p c j", c=8), in_=w_uq.rearrange("(c p) j -> p c j", p=128)), incs=[(ld, 16)])
    p.emit("sync", lambda e: e.dma_start(out=scr[:, SC_WUK:SC_WUK + 1024].rearrange("p (c j) -> p c j", c=8), in_=w_uk.rearrange("(c p) j -> p c j", p=128)), incs=[(ld, 16)])
    p.emit("sync", lambda e: e.dma_start(out=scr[:, SC_WUV:SC_WUV + 1024].rearrange("p (c j) -> p c j", c=8), in_=w_uv.rearrange("(c p) j -> p c j", p=128)), incs=[(ld, 16)])
    p.emit("sync", lambda e: e.dma_start(out=scr[:, SC_WDQ:SC_WDQ + 1024], in_=w_dq[:]), incs=[(ld, 16)])
    [wot0] = p.emit("sync", lambda e: e.dma_start(out=scr[:, SC_WOT:SC_WOT + 4096].rearrange("p (c d) -> p c d", c=8),
        in_=w_oT[:, 0:512].rearrange("(c p) d -> p c d", p=128)), incs=[(ld, 16)])
    boot_ld = (ld, ld.v)
    x_ld = const_ld = wprep_ld = boot_ld
    p.emit("gpsimd", lambda e: e.dma_start(out=wdqT_b.rearrange("p (c j) -> p c j", c=8), in_=w_dqT.rearrange("(c p) j -> p c j", p=128)), incs=[(lds, 16)])
    p.emit("gpsimd", lambda e: e.dma_start(out=wdkvT_b.rearrange("p (c j) -> p c j", c=8), in_=w_dkvT.rearrange("(c p) j -> p c j", p=128)), incs=[(lds, 16)])
    wbf_ld = (lds, lds.v)

    p.emit("vector", lambda e: e.memset(eps_s[:], EPS_LN), waits=[const_ld], incs=[(dvd, 1)])
    p.emit("vector", lambda e: e.memset(negln8_s[:], -math.log(8.0)), incs=[(dvd, 1)])
    [v_ones] = p.emit("vector", lambda e: e.memset(ones_col_b[:], 1.0), incs=[(dvd, 1)])
    p.emit("vector", lambda e: e.memset(onesf_row[:], 1.0), incs=[(dvd, 1)])

    for c in range(8):
        p.emit("tensor", (lambda c: lambda e: e.matmul(
            g_ps[0][0:128, 0:128], scr[:, SC_WUQ + 128 * c:SC_WUQ + 128 * c + 128],
            scr[:, SC_WUK + 128 * c:SC_WUK + 128 * c + 128], start=(c == 0), stop=(c == 7)))(c),
            waits=[wprep_ld] if c == 0 else (), incs=[(ped, 1)])
    [vJ] = p.emit("vector", lambda e: e.tensor_copy(scr[:, SC_J:SC_J + 128], g_ps[0][0:128, 0:128]),
                  waits=[(ped, ped.v)], incs=[(dvd, 1)])
    for c in range(8):
        p.emit("tensor", (lambda c: lambda e: e.matmul(
            g_ps[1][0:128, 0:128], scr[:, SC_WDQ + 128 * c:SC_WDQ + 128 * c + 128],
            scr[:, SC_J:SC_J + 128], start=True, stop=True))(c),
            waits=[(dvd, dvd.v), (ped, ped.v)], incs=[(ped, 1)])
        p.emit("vector", (lambda c: lambda e: e.tensor_copy(
            scr[:, SC_KEFF + 128 * c:SC_KEFF + 128 * c + 128], g_ps[1][0:128, 0:128]))(c),
            waits=[(ped, ped.v)], incs=[(dvd, 1)])
    a_done = []
    for h in range(16):
        ro = 64 * (h % 2)
        co = 128 * (h // 2)
        p.emit("tensor", (lambda h, ro, co: lambda e: e.matmul(
            g_ps[h % 2][0:128, 128:256], scr[ro:ro + 64, SC_WUQ + co:SC_WUQ + co + 128],
            scr[ro:ro + 64, SC_KEFF + co:SC_KEFF + co + 128], start=True, stop=True))(h, ro, co),
            waits=[(dvd, dvd.v), (ped, ped.v)], incs=[(ped, 1)])
        [ad] = p.emit("vector", (lambda h: lambda e: e.tensor_copy(
            u16[:, AB + 128 * h:AB + 128 * h + 128], g_ps[h % 2][0:128, 128:256]))(h),
            waits=[(ped, ped.v)], incs=[(dvd, 1)])
        a_done.append(ad)
    for half in range(2):
        wv = [(ld, wot0)] if half == 0 else [(ld, ld.v)]
        for c in range(8):
            p.emit("tensor", (lambda half, c: lambda e: e.matmul(
                h_ps[half][0:128, 0:512], scr[:, SC_WUV + 128 * c:SC_WUV + 128 * c + 128],
                scr[:, SC_WOT + 512 * c:SC_WOT + 512 * c + 512],
                start=(c == 0), stop=(c == 7)))(half, c),
                waits=(wv + [(ped, ped.v)]) if c == 0 else (), incs=[(ped, 1)])
        pe_half = (ped, ped.v)
        if half == 0:
            p.emit("sync", lambda e: e.dma_start(out=scr[:, SC_WOT:SC_WOT + 4096].rearrange("p (c d) -> p c d", c=8),
                in_=w_oT[:, 512:1024].rearrange("(c p) d -> p c d", p=128)),
                waits=[pe_half], incs=[(ld, 16)])
        p.emit("vector", (lambda half: lambda e: e.tensor_copy(
            veff_b[:, 512 * half:512 * half + 512], h_ps[half][0:128, 0:512]))(half),
            waits=[pe_half, (dvd, dvd.v)], incs=[(dvd, 1)])
    prep_done = (dvd, dvd.v)
    p.emit("gpsimd", lambda e: e.dma_start(out=dwT_b.rearrange("p (c f) -> p c f", c=8), in_=dwT.rearrange("(c p) f -> p c f", p=128)),
           waits=[prep_done, (ped, ped.v), wbf_ld], incs=[(lds, 16)])
    p.emit("gpsimd", lambda e: e.dma_start(out=upT_b.rearrange("p (c d) -> p c d", c=4), in_=upT.rearrange("(c p) d -> p c d", p=128)),
           incs=[(lds, 16)])
    ffnw_ld = (lds, lds.v)

    state = {"stream_ready": (dvd, 0), "xn_ready": None, "pool_ready": None, "ld_guard": (lp, 0)}
    fl = {"sfree": [0, 0], "pfree": [0, 0], "yfree": [0, 0], "bias_free": [0, 0],
          "z_free": 0, "mhat_free": 0, "epi_free": 0, "ynrm_free": [0, 0], "epi_par": 0,
          "pend_epi": None, "jobctr": 0, "rec_free": 0}

    def emit_ln(with_pool):
        for tt in range(4):
            for c in range(8):
                w = [state["stream_ready"], (dvd, dvd.v)] if (tt == 0 and c == 0) else [(dvd, dvd.v)]
                w = list(w) + [(ped, ped.v)]
                sl = xT[:, 2048 * c + 512 * tt: 2048 * c + 512 * tt + 512]
                p.emit("vector", (lambda sl: lambda e: e.tensor_copy(xb_t[:], sl))(sl), waits=w, incs=[(dvd, 1)])
                cast_v = dvd.v
                p.emit("vector", lambda e: e.tensor_tensor(sq_b[:], xb_t[:], xb_t[:], ALU.mult),
                       waits=[(dvd, dvd.v)], incs=[(dvd, 1)])
                p.emit("tensor", (lambda c: lambda e: e.matmul(
                    g_ps[0][0:1, 0:512], ones_col_b[:], xb_t[:], start=(c == 0), stop=(c == 7)))(c),
                    waits=([(dvd, cast_v), (dvd, v_ones)] + (GUARD() if c == 0 else [])), incs=[(ped, 1)])
                p.emit("tensor", (lambda c: lambda e: e.matmul(
                    g_ps[1][0:1, 0:512], ones_col_b[:], sq_b[:], start=(c == 0), stop=(c == 7)))(c),
                    waits=[(dvd, dvd.v)], incs=[(ped, 1)])
            stat_done = (ped, ped.v)
            p.emit("vector", lambda e: e.tensor_scalar(m_row[:], g_ps[0][0:1, 0:512], 1.0 / 1024, None, ALU.mult),
                   waits=[stat_done, (dvd, dvd.v)], incs=[(dvd, 1)])
            p.emit("vector", lambda e: e.tensor_scalar(q_row[:], g_ps[1][0:1, 0:512], 1.0 / 1024, None, ALU.mult),
                   waits=[stat_done, (dvd, dvd.v)], incs=[(dvd, 1)])
            p.emit("vector", lambda e: e.tensor_tensor(r_row[:], m_row[:], m_row[:], ALU.mult),
                   waits=[(dvd, dvd.v), (acd, acd.v)], incs=[(dvd, 1)])
            p.emit("vector", lambda e: e.tensor_tensor(q_row[:], q_row[:], r_row[:], ALU.subtract),
                   waits=[(dvd, dvd.v)], incs=[(dvd, 1)])
            p.emit("scalar", lambda e: e.activation(out=q_row[:], in_=q_row[:], func=AF.Ln, bias=eps_s[:]),
                   waits=[(dvd, dvd.v), (acd, acd.v)], incs=[(acd, 1)])
            p.emit("scalar", lambda e: e.activation(out=r_row[:], in_=q_row[:], func=AF.Exp, scale=-0.5),
                   waits=[(acd, acd.v)], incs=[(acd, 1)])
            rrow_v = acd.v
            [v_nmr] = p.emit("vector", lambda e: e.scalar_tensor_tensor(
                m_row[:], m_row[:], -1.0, r_row[:], ALU.mult, ALU.mult),
                waits=[(acd, rrow_v), (dvd, dvd.v)], incs=[(dvd, 1)])
            p.emit("tensor", lambda e: e.matmul(h_ps[0][0:128, 0:512], onesr_f[:], r_row[:], start=True, stop=True),
                   waits=[(dvd, v_nmr)] + GUARD(), incs=[(ped, 1)])
            p.emit("tensor", lambda e: e.matmul(h_ps[1][0:128, 0:512], onesr_f[:], m_row[:], start=True, stop=True),
                   incs=[(ped, 1)])
            bcs_done = (ped, ped.v)
            if with_pool:
                p.emit("vector", lambda e: e.reduce_sum(s11b[:], m_row[:], axis=AX.X),
                       waits=[(dvd, v_nmr)], incs=[(dvd, 1)])
                if tt == 0:
                    p.emit("vector", lambda e: e.tensor_copy(s11[:], s11b[:]), waits=[(dvd, dvd.v)], incs=[(dvd, 1)])
                else:
                    p.emit("vector", lambda e: e.tensor_tensor(s11[:], s11[:], s11b[:], ALU.add),
                           waits=[(dvd, dvd.v)], incs=[(dvd, 1)])
            for c in range(8):
                sl = xT[:, 2048 * c + 512 * tt: 2048 * c + 512 * tt + 512]
                osl = xnT[:, 2048 * c + 512 * tt: 2048 * c + 512 * tt + 512]
                p.emit("vector", (lambda sl: lambda e: e.tensor_tensor(u_sb[:], sl, h_ps[0][0:128, 0:512], ALU.mult))(sl),
                       waits=[bcs_done, (dvd, dvd.v)], incs=[(dvd, 1)])
                u_v = dvd.v
                p.emit("vector", (lambda osl: lambda e: e.tensor_tensor(osl, u_sb[:], h_ps[1][0:128, 0:512], ALU.add))(osl),
                       waits=[(dvd, dvd.v), (smm, smm.v), (ped, ped.v)], incs=[(dvd, 1)])
                if with_pool:
                    p.emit("vector", lambda e: e.reduce_sum(pool_t[:], u_sb[:], axis=AX.X),
                           waits=[(dvd, u_v)], incs=[(dvd, 1)])
                    if tt == 0:
                        p.emit("vector", (lambda c: lambda e: e.tensor_copy(pool_acc[:, c:c + 1], pool_t[:]))(c),
                               waits=[(dvd, dvd.v), state["ld_guard"]], incs=[(dvd, 1)])
                    else:
                        p.emit("vector", (lambda c: lambda e: e.tensor_tensor(
                            pool_acc[:, c:c + 1], pool_acc[:, c:c + 1], pool_t[:], ALU.add))(c),
                            waits=[(dvd, dvd.v)], incs=[(dvd, 1)])
            state.setdefault("xn_tt", [0, 0, 0, 0])[tt] = dvd.v
        if with_pool:
            p.emit("tensor", lambda e: e.matmul(g_ps[0][0:128, 0:1], onesr_f[:], s11[:], start=True, stop=True),
                   waits=[(dvd, dvd.v)] + GUARD(), incs=[(ped, 1)])
            p.emit("vector", lambda e: e.tensor_copy(pool_t[:], g_ps[0][0:128, 0:1]),
                   waits=[(ped, ped.v), (dvd, dvd.v)], incs=[(dvd, 1)])
            [vp] = p.emit("vector", lambda e: e.tensor_scalar(pool_acc[:], pool_acc[:], pool_t[:], None, ALU.add),
                          waits=[(dvd, dvd.v)], incs=[(dvd, 1)])
            state["pool_ready"] = (dvd, vp)
        state["xn_ready"] = (dvd, dvd.v)

    def emit_prelude(job):
        par = fl["jobctr"] & 1
        fl["jobctr"] += 1
        job["par"] = par
        qp = job["qp"]
        qb = job.get("qb", 0)
        nA = job["nA"]
        if job.get("qlmm") is not None:
            job["qlmm"]()
        p.emit("vector", (lambda job, qp, qb: lambda e: e.tensor_tensor(
            sq_b[qb:qb + qp, :], job["ql_ap"], job["ql_ap"], ALU.mult))(job, qp, qb),
            waits=job["wq"] + [(dvd, dvd.v), (ped, fl["mhat_free"])], incs=[(dvd, 1)])
        p.emit("tensor", (lambda par, qp, qb: lambda e: e.matmul(
            g_ps[par][0:1, 0:512], ones_col_b[qb:qb + qp, :], sq_b[qb:qb + qp, :], start=True, stop=True))(par, qp, qb),
            waits=[(dvd, dvd.v), (ped, ped.v), (dvd, v_ones)], incs=[(ped, 1)])
        p.emit("vector", (lambda par: lambda e: e.reduce_max(z_s[0:1, 0:1], g_ps[par][0:1, 0:512], axis=AX.X))(par),
               waits=[(ped, ped.v), (acd, fl["z_free"]), (dvd, dvd.v)], incs=[(dvd, 1)])
        p.emit("vector", (lambda job: lambda e: e.tensor_tensor(
            z_s[0:1, 0:1], z_s[0:1, 0:1], kmax_s[0:1, job["kn_col"]:job["kn_col"] + 1], ALU.mult))(job),
            waits=[(dvd, dvd.v)] + job.get("wkn", []), incs=[(dvd, 1)])
        p.emit("scalar", lambda e: e.activation(out=z_s[0:1, 1:2], in_=z_s[0:1, 0:1], func=AF.Ln),
               waits=[(dvd, dvd.v), (acd, acd.v)], incs=[(acd, 1)])
        p.emit("scalar", lambda e: e.activation(out=mhat_s[:], in_=z_s[0:1, 1:2], func=AF.Exp, scale=0.5, bias=negln8_s[:]),
               waits=[(acd, acd.v)], incs=[(acd, 1)])
        fl["z_free"] = acd.v
        p.emit("tensor", (lambda par: lambda e: e.matmul(
            g_ps[par][0:128, 256:257], negonesr_f[:], mhat_s[:], start=True, stop=True))(par),
            waits=[(acd, acd.v), (ped, ped.v)], incs=[(ped, 1)])
        fl["mhat_free"] = ped.v
        p.emit("vector", (lambda par: lambda e: e.tensor_copy(mneg[par][:], g_ps[par][0:128, 256:257]))(par),
               waits=[(ped, ped.v), (dvd, dvd.v)], incs=[(dvd, 1)])
        [bv] = p.emit("vector", (lambda job, par, nA: lambda e: e.tensor_scalar(
            bias_all[par][:, 0:nA + 1], moff_s[:, job["moff0"]:job["moff0"] + nA + 1],
            mneg[par][:], None, ALU.add))(job, par, nA),
            waits=[(dvd, dvd.v), (expd, fl["bias_free"][par])], incs=[(dvd, 1)])
        job["pre_done"] = (dvd, bv)
        return job

    def emit_kloop(job):
        par = job["par"]
        nA = job["nA"]
        pre_done = job["pre_done"]
        npair = nA // 2
        nE = npair + 4
        exp_free = []
        pv_last = []
        pending_pv = None
        first_pv = [True]
        for e in range(nE):
            diag = e >= npair
            i = e - npair
            col0 = 128 * i if diag else 0
            bk = e % 2
            if e >= 2:
                w = [(expd, exp_free[e - 2])]
            else:
                w = [pre_done, (expd, fl["sfree"][bk]), (dvd, fl["yfree"][par])] + job.get("wk0", [])
            if diag:
                lhs = job["lhsB"](i)
                p.emit("tensor", (lambda lhs, bk, col0, job: lambda e_: e_.matmul(
                    spair[bk][0:128, col0:512], lhs, job["ql_ap"][:, col0:512], start=True, stop=True))(lhs, bk, col0, job),
                    waits=w, incs=[(smm, 1)])
            else:
                lhs0 = job["lhsA"](2 * e)
                lhs1 = job["lhsA"](2 * e + 1)
                p.emit("tensor", (lambda lhs0, bk, job: lambda e_: e_.matmul(
                    spair[bk][0:128, 0:512], lhs0, job["ql_ap"], start=True, stop=True))(lhs0, bk, job),
                    waits=w, incs=[(smm, 1)])
                p.emit("tensor", (lambda lhs1, bk, job: lambda e_: e_.matmul(
                    spair[bk][0:128, 512:1024], lhs1, job["ql_ap"], start=True, stop=True))(lhs1, bk, job),
                    incs=[(smm, 1)])
            if pending_pv is not None:
                pending_pv()
                pending_pv = None
            if e == 2 and fl.get("pend_epi") is not None:
                pe_fn = fl["pend_epi"]
                fl["pend_epi"] = None
                pe_fn()
            if e == 4 and fl.get("pend_pre") is not None:
                pp = fl["pend_pre"]
                fl["pend_pre"] = None
                pp()
            bcol = nA if diag else 2 * e
            we = [(smm, smm.v)]
            we.append((pvd, pv_last[e - 2]) if e >= 2 else (pvd, fl["pfree"][bk]))
            if diag:
                [ev] = p.emit("scalar", (lambda bk, col0, bcol, par: lambda e_: e_.activation(
                    out=psb[bk][:, col0:512], in_=spair[bk][0:128, col0:512], func=AF.Exp,
                    scale=0.125, bias=bias_all[par][:, bcol:bcol + 1]))(bk, col0, bcol, par),
                    waits=we, incs=[(expd, 1)])
            else:
                [ev] = p.emit("scalar", (lambda bk, bcol, par: lambda e_: e_.activation(
                    out=psb[bk][:, 0:1024], in_=spair[bk][0:128, 0:1024], func=AF.Exp,
                    scale=0.125, bias=bias_all[par][:, bcol:bcol + 1]))(bk, bcol, par),
                    waits=we, incs=[(expd, 1)])
            exp_free.append(ev)
            if diag:
                [mv] = p.emit("vector", (lambda bk, i: lambda e_: e_.tensor_tensor(
                    psb[bk][:, 128 * i:128 * i + 128], psb[bk][:, 128 * i:128 * i + 128], tri_b[:], ALU.mult))(bk, i),
                    waits=[(expd, ev), (dvd, dvd.v)], incs=[(dvd, 1)])
                pv_w = [(dvd, mv)]
            else:
                pv_w = [(expd, ev)]
            last_e = (e == nE - 1)
            if diag:
                vap = job["vB"](i)
                def mk_pv(vap=vap, bk=bk, col0=col0, par=par, last_e=last_e, pv_w=tuple(pv_w)):
                    st = first_pv[0]
                    first_pv[0] = False
                    [pv] = p.emit("tensor", lambda e_: e_.matmul(
                        y_ps[par][0:65, col0:512], vap, psb[bk][:, col0:512],
                        start=st, stop=last_e),
                        waits=list(pv_w), incs=[(pvd, 1)])
                    pv_last.append(pv)
            else:
                vap0 = job["vA"](2 * e)
                vap1 = job["vA"](2 * e + 1)
                def mk_pv(vap0=vap0, vap1=vap1, bk=bk, par=par, pv_w=tuple(pv_w)):
                    st = first_pv[0]
                    first_pv[0] = False
                    p.emit("tensor", lambda e_: e_.matmul(
                        y_ps[par][0:65, 0:512], vap0, psb[bk][:, 0:512],
                        start=st, stop=False),
                        waits=list(pv_w), incs=[(pvd, 1)])
                    [pv] = p.emit("tensor", lambda e_: e_.matmul(
                        y_ps[par][0:65, 0:512], vap1, psb[bk][:, 512:1024],
                        start=False, stop=False),
                        incs=[(pvd, 1)])
                    pv_last.append(pv)
            pending_pv = mk_pv
        pending_pv()
        if fl.get("pend_epi") is not None:
            pe_fn = fl["pend_epi"]; fl["pend_epi"] = None; pe_fn()
        if fl.get("pend_pre") is not None:
            pp = fl["pend_pre"]; fl["pend_pre"] = None; pp()
        fl["sfree"][nE % 2] = exp_free[-2]
        fl["sfree"][(nE - 1) % 2] = exp_free[-1]
        fl["pfree"][nE % 2] = pv_last[-2]
        fl["pfree"][(nE - 1) % 2] = pv_last[-1]
        fl["bias_free"][par] = expd.v
        return (pvd, pv_last[-1])
        fl["sfree"][nT % 2] = exp_free[-2] if nT >= 2 else exp_free[-1]
        fl["sfree"][(nT - 1) % 2] = exp_free[-1]
        fl["pfree"][nT % 2] = pv_at[-2] if nT >= 2 else pv_at[-1]
        fl["pfree"][(nT - 1) % 2] = pv_at[-1]
        fl["bias_free"][par] = expd.v
        return (pvd, pv_at[-1])

    def flash_epilogue(job, pv_done, dst_slice=None, dram_fn=None):
        par = job["par"]
        p.emit("vector", (lambda par: lambda e: e.reciprocal(rec_sb[:], y_ps[par][64:65, 0:512]))(par),
               waits=[pv_done, (dvd, dvd.v), (ped, fl["epi_free"])], incs=[(dvd, 1)])
        p.emit("tensor", (lambda par: lambda e: e.matmul(
            g_ps[par][0:64, 0:512], onesr64_f[:], rec_sb[:], start=True, stop=True))(par),
            waits=[(dvd, dvd.v), (ped, ped.v)], incs=[(ped, 1)])
        fl["epi_free"] = ped.v
        p.emit("vector", (lambda par: lambda e: e.tensor_copy(bc_sb[:], g_ps[par][0:64, 0:512]))(par),
               waits=[(ped, ped.v), (dvd, dvd.v)], incs=[(dvd, 1)])
        if dram_fn is None:
            hi = job.get("hi", False)
            p.emit("vector", (lambda par: lambda e: e.tensor_tensor(
                ytmp_sb[:], y_ps[par][0:64, 0:512], bc_sb[:], ALU.mult))(par),
                waits=[(dvd, dvd.v)], incs=[(dvd, 1)])
            if hi:
                p.emit("tensor", (lambda par: lambda e: e.matmul(
                    g_ps[par][64:128, 0:512], id64f_s[:], ytmp_sb[:],
                    start=True, stop=True, tile_position=(0, 64)))(par),
                    waits=[(dvd, dvd.v), (ped, ped.v)], incs=[(ped, 1)])
                [va] = p.emit("vector", (lambda dst_slice, par: lambda e: e.tensor_tensor(
                    dst_slice, dst_slice, g_ps[par][64:128, 0:512], ALU.add))(dst_slice, par),
                    waits=[(ped, ped.v), (dvd, dvd.v)], incs=[(dvd, 1)])
                fl["epi_free"] = ped.v
            else:
                [va] = p.emit("vector", (lambda dst_slice: lambda e: e.tensor_tensor(
                    dst_slice, dst_slice, ytmp_sb[:], ALU.add))(dst_slice),
                    waits=[(dvd, dvd.v)], incs=[(dvd, 1)])
            fl["yfree"][par] = dvd.v
            return (dvd, va)
        else:
            yp = fl["epi_par"]
            fl["epi_par"] ^= 1
            [vm] = p.emit("vector", (lambda par, yp: lambda e: e.tensor_tensor(
                ynrm_b[yp][:], y_ps[par][0:64, 0:512], bc_sb[:], ALU.mult))(par, yp),
                waits=[(dvd, dvd.v), (ly[yp], fl["ynrm_free"][yp])], incs=[(dvd, 1)])
            fl["yfree"][par] = dvd.v
            [sv] = p.emit("sync", (lambda yp: lambda e: dram_fn(e, ynrm_b[yp]))(yp),
                          waits=[(dvd, vm)], incs=[(ly[yp], 16)])
            fl["ynrm_free"][yp] = sv
            return (ly[yp], sv)

    assert stages >= 1
    emit_ln(False)
    xn1 = state["xn_ready"]
    if True:
        ckv_own_done = []
        q1_done = []
        for tt in range(4):
            for c in range(8):
                p.emit("tensor", (lambda tt, c: lambda e: e.matmul(
                    g_ps[tt % 2][0:128, 0:512], wdkvT_b[:, 128 * c:128 * c + 128],
                    xnT[:, 2048 * c + 512 * tt:2048 * c + 512 * tt + 512], start=(c == 0), stop=(c == 7)))(tt, c),
                    waits=([xn1, wbf_ld] + GUARD()) if c == 0 else (), incs=[(ped, 1)])
            [vv] = p.emit("vector", (lambda tt: lambda e: e.tensor_copy(
                u16[:, CKVO + 512 * tt:CKVO + 512 * tt + 512], g_ps[tt % 2][0:128, 0:512]))(tt),
                waits=[(ped, ped.v), (dvd, dvd.v)], incs=[(dvd, 1)])
            ckv_own_done.append(vv)
            for c in range(8):
                p.emit("tensor", (lambda tt, c: lambda e: e.matmul(
                    h_ps[tt % 2][0:128, 0:512], wdqT_b[:, 128 * c:128 * c + 128],
                    xnT[:, 2048 * c + 512 * tt:2048 * c + 512 * tt + 512], start=(c == 0), stop=(c == 7)))(tt, c),
                    incs=[(ped, 1)])
            [vq] = p.emit("vector", (lambda tt: lambda e: e.tensor_copy(
                u16[:, Q1 + 512 * tt:Q1 + 512 * tt + 512], h_ps[tt % 2][0:128, 0:512]))(tt),
                waits=[(ped, ped.v), (dvd, dvd.v)], incs=[(dvd, 1)])
            q1_done.append(vq)

        [sv] = p.emit("sync", lambda e: e.dma_start(out=cc_ckv_i.ap(), in_=u16[:, CKVO:CKVO + 2048]),
                      waits=[(dvd, ckv_own_done[-1])], incs=[(lk, 16)])
        [cc1] = p.emit("gpsimd", lambda e: e.collective_compute(
            "AllGather", ALU.bypass, replica_groups=GROUPS, ins=[cc_ckv_i.ap().opt()], outs=[cc_ckv_o.ap().opt()]),
            waits=[(lk, sv), wbf_ld], incs=[(ccd, 1)])
        for i in range(4):
            for s4 in range(4):
                j = 4 * s4 + i
                p.emit("sync", (lambda i, s4, j: lambda e: e.dma_start(
                    out=ckv_g[:, 512 * j:512 * j + 512],
                    in_=cc_ckv_o.ap()[128 * i:128 * i + 128, 512 * s4:512 * s4 + 512]))(i, s4, j),
                    waits=[(ccd, cc1), prep_done], incs=[(lk, 16)])
        ckv_ld = (lk, lk.v)

        for cc in range(16):
            p.emit("vector", (lambda cc: lambda e: e.tensor_tensor(
                sq_b[:], ckv_g[:, 512 * cc:512 * cc + 512], ckv_g[:, 512 * cc:512 * cc + 512], ALU.mult))(cc),
                waits=[ckv_ld, (dvd, dvd.v)], incs=[(dvd, 1)])
            p.emit("tensor", (lambda cc: lambda e: e.matmul(
                g_ps[cc % 2][0:1, 0:512], ones_col_b[:], sq_b[:], start=True, stop=True))(cc),
                waits=[(dvd, dvd.v)] + GUARD(), incs=[(ped, 1)])
            if cc == 0:
                p.emit("vector", (lambda cc: lambda e: e.reduce_max(
                    kmax_s[0:1, 0:1], g_ps[cc % 2][0:1, 0:512], axis=AX.X))(cc),
                    waits=[(ped, ped.v), (dvd, dvd.v)], incs=[(dvd, 1)])
            else:
                p.emit("vector", (lambda cc: lambda e: e.reduce_max(
                    z_s[0:1, 0:1], g_ps[cc % 2][0:1, 0:512], axis=AX.X))(cc),
                    waits=[(ped, ped.v), (dvd, dvd.v)], incs=[(dvd, 1)])
                p.emit("vector", lambda e: e.tensor_max(kmax_s[0:1, 0:1], kmax_s[0:1, 0:1], z_s[0:1, 0:1]),
                       waits=[(dvd, dvd.v)], incs=[(dvd, 1)])
        kn_done = (dvd, dvd.v)

        two_run_out = lambda ap130: ap130.rearrange("p (a b) -> p a b", b=65)[:, :, 0:64]
        two_run_in = lambda ap128: ap128.rearrange("p (a b) -> p a b", b=64)

        def mk_job_a(hp, hh, s4):
            h = 2 * hp + hh
            par_dummy = 0
            def qlmm():
                p.emit("tensor", (lambda h, s4: lambda e: e.matmul(
                    g_ps[job["par"]][0:128, 0:512], u16[:, AB + 128 * h:AB + 128 * h + 128],
                    u16[:, Q1 + 512 * s4:Q1 + 512 * s4 + 512], start=True, stop=True))(h, s4),
                    waits=[(dvd, q1_done[s4]), (ped, ped.v), (dvd, a_done[h]),
                           (acd, acd.v), (dvd, dvd.v)], incs=[(ped, 1)])
                parx = job["par"]
                [vq] = p.emit("vector", (lambda parx: lambda e: e.tensor_copy(
                    qlb[parx][:], g_ps[parx][0:128, 0:512]))(parx),
                    waits=[(ped, ped.v), (dvd, dvd.v), (smm, smm.v)], incs=[(dvd, 1)])
                job["wq"] = [(dvd, vq)]
                job["ql_ap"] = qlb[parx][:]
            job = {
                "qp": 128, "hi": (hh == 1), "nA": 16 * s4 + 12, "moff0": MOFF0[s4], "kn_col": 0,
                "wkn": [kn_done],
                "wq": [],
                "lhsA": (lambda t: ckv_g[:, 128 * t:128 * t + 128]),
                "lhsB": (lambda i, s4=s4: u16[:, CKVO + 512 * s4 + 128 * i:CKVO + 512 * s4 + 128 * i + 128]),
                "vA": (lambda t, hh=hh: vbuf[:, 130 * t + 65 * hh:130 * t + 65 * hh + 65]),
                "vB": (lambda i, s4=s4, hh=hh: u16[:, VOWN + 130 * (4 * s4 + i) + 65 * hh:
                                                   VOWN + 130 * (4 * s4 + i) + 65 * hh + 65]),
                "dst": (hp, hh, s4),
            }
            job["qlmm"] = qlmm
            return job

        jobs_a = [mk_job_a(hp, hh, s4) for hp in range(8) for hh in range(2) for s4 in range(4)]
        jidx = 0
        for hp in range(8):
            p.emit("vector", lambda e: e.memset(vbuf[:], 1.0),
                   waits=[(dvd, dvd.v), (pvd, pvd.v)], incs=[(dvd, 1)])
            [vo_ms] = p.emit("vector", lambda e: e.memset(u16[:, VOWN:VOWN + 2080], 1.0),
                             waits=[(dvd, dvd.v)], incs=[(dvd, 1)])
            vb_done = []
            for t in range(64):
                w = ([prep_done, kn_done, (dvd, vo_ms)] + GUARD()) if t == 0 else                     ([(dvd, vb_done[t - 2])] if t >= 2 else [])
                p.emit("tensor", (lambda t, hp: lambda e: e.matmul(
                    g_ps[t % 2][0:128, 0:128], ckv_g[:, 128 * t:128 * t + 128],
                    veff_b[:, 128 * hp:128 * hp + 128], start=True, stop=True))(t, hp),
                    waits=w, incs=[(ped, 1)])
                [vb] = p.emit("vector", (lambda t: lambda e: e.tensor_copy(
                    two_run_out(vbuf[:, 130 * t:130 * t + 130]),
                    two_run_in(g_ps[t % 2][0:128, 0:128])))(t),
                    waits=[(ped, ped.v), (dvd, dvd.v)], incs=[(dvd, 1)])
                vb_done.append(vb)
            vo_done = []
            for t in range(16):
                w = [(dvd, vo_done[t - 2])] if t >= 2 else [(dvd, vb_done[0])]
                p.emit("tensor", (lambda t, hp: lambda e: e.matmul(
                    h_ps[t % 2][0:128, 0:128], u16[:, CKVO + 128 * t:CKVO + 128 * t + 128],
                    veff_b[:, 128 * hp:128 * hp + 128], start=True, stop=True))(t, hp),
                    waits=w, incs=[(ped, 1)])
                [vo] = p.emit("vector", (lambda t: lambda e: e.tensor_copy(
                    two_run_out(u16[:, VOWN + 130 * t:VOWN + 130 * t + 130]),
                    two_run_in(h_ps[t % 2][0:128, 0:128])))(t),
                    waits=[(ped, ped.v), (dvd, dvd.v)], incs=[(dvd, 1)])
                vo_done.append(vo)
            vready = (dvd, dvd.v)
            for k in range(8):
                job = jobs_a[jidx]
                job["wk0"] = [vready]
                if jidx == 0:
                    emit_prelude(job)
                if jidx + 1 < 64:
                    nxt = jobs_a[jidx + 1]
                    fl["pend_pre"] = (lambda nxt: lambda: emit_prelude(nxt))(nxt)
                emit_kloop(job)
                pv_done = (pvd, pvd.v)
                hpj, hhj, s4j = job["dst"]
                dst = xT[:, 2048 * hpj + 512 * s4j:2048 * hpj + 512 * s4j + 512][64 * hhj:64 * hhj + 64, :]
                fl["pend_epi"] = (lambda job, pv_done, dst: lambda: flash_epilogue(job, pv_done, dst_slice=dst))(job, pv_done, dst)
                jidx += 1
        if fl.get("pend_epi") is not None:
            fn = fl["pend_epi"]; fl["pend_epi"] = None; fn()
        state["stream_ready"] = (dvd, dvd.v)

    def emit_ffn(call_idx):
        cci, cco = (cc_pool_i, cc_pool_o) if call_idx == 0 else (cc_pool_i2, cc_pool_o2)
        emit_ln(True)
        xn = state["xn_ready"]
        [sv] = p.emit("sync", (lambda cci: lambda e: e.dma_start(out=cci.ap(), in_=pool_acc[:]))(cci),
                      waits=[state["pool_ready"]], incs=[(lp, 16)])
        [ccv] = p.emit("gpsimd", (lambda cci, cco: lambda e: e.collective_compute(
            "AllReduce", ALU.add, replica_groups=GROUPS, ins=[cci.ap().opt()], outs=[cco.ap().opt()]))(cci, cco),
            waits=[(lp, sv)], incs=[(ccd, 1)])
        [pl] = p.emit("sync", (lambda cco: lambda e: e.dma_start(out=pool_in[:], in_=cco.ap()))(cco),
                      waits=[(ccd, ccv)], incs=[(lp, 16)])
        pool_ld = (lp, pl)
        state["ld_guard"] = (lp, pl)
        gelu_at = [0, 0]
        for pr in range(4):
            for tt in range(4):
                bk = (4 * pr + tt) % 2
                for c in range(8):
                    w = []
                    if pr == 0 and tt == 0 and c == 0:
                        w = [xn, ffnw_ld] + GUARD()
                    elif c == 0:
                        w = [(acd, gelu_at[bk]), (dvd, dvd.v)]
                    p.emit("tensor", (lambda pr, tt, c, bk: lambda e: e.matmul(
                        g_ps[bk][0:128, 0:512], dwT_b[:, 512 * c + 128 * pr:512 * c + 128 * pr + 128],
                        xnT[:, 2048 * c + 512 * tt:2048 * c + 512 * tt + 512], start=(c == 0), stop=(c == 7)))(pr, tt, c, bk),
                        waits=w, incs=[(ped, 1)])
                [hv] = p.emit("scalar", (lambda pr, tt, bk: lambda e: e.activation(
                    out=ckv_g[:, 2048 * pr + 512 * tt:2048 * pr + 512 * tt + 512],
                    in_=g_ps[bk][0:128, 0:512], func=AF.Gelu, bias=bdr_s[:, pr:pr + 1]))(pr, tt, bk),
                    waits=[(ped, ped.v), (acd, acd.v), (smm, smm.v)], incs=[(acd, 1)])
                gelu_at[bk] = hv
        h_all = (acd, acd.v)
        for c in range(8):
            p.emit("tensor", (lambda c: lambda e: e.matmul(
                h_ps[0][0:1, 0:8], pool_in[:, c:c + 1], gwT_s[:, 8 * c:8 * c + 8], start=(c == 0), stop=(c == 7)))(c),
                waits=([pool_ld, const_ld] + GUARD()) if c == 0 else (), incs=[(ped, 1)])
        p.emit("vector", lambda e: e.tensor_scalar(gate_row[:], h_ps[0][0:1, 0:8], 1.0 / 8192.0, None, ALU.mult),
               waits=[(ped, ped.v), (dvd, dvd.v)], incs=[(dvd, 1)])
        p.emit("vector", lambda e: e.reduce_max(s11[:], gate_row[:], axis=AX.X), waits=[(dvd, dvd.v)], incs=[(dvd, 1)])
        p.emit("vector", lambda e: e.tensor_scalar(g2_row[:], gate_row[:], s11[0:1, 0:1], None, ALU.subtract),
               waits=[(dvd, dvd.v)], incs=[(dvd, 1)])
        p.emit("scalar", lambda e: e.activation(out=gate_row[:], in_=g2_row[:], func=AF.Exp),
               waits=[(dvd, dvd.v), (acd, acd.v)], incs=[(acd, 1)])
        p.emit("vector", lambda e: e.reduce_sum(s11[:], gate_row[:], axis=AX.X),
               waits=[(acd, acd.v), (dvd, dvd.v)], incs=[(dvd, 1)])
        p.emit("vector", lambda e: e.reciprocal(s11b[:], s11[:]), waits=[(dvd, dvd.v)], incs=[(dvd, 1)])
        p.emit("vector", lambda e: e.tensor_scalar(gate_row[:], gate_row[:], s11b[0:1, 0:1], None, ALU.mult),
               waits=[(dvd, dvd.v)], incs=[(dvd, 1)])
        p.emit("vector", lambda e: e.reduce_max(s11[:], gate_row[:], axis=AX.X), waits=[(dvd, dvd.v)], incs=[(dvd, 1)])
        p.emit("vector", lambda e: e.tensor_scalar(g2_row[:], gate_row[:], s11[0:1, 0:1], None, ALU.is_ge),
               waits=[(dvd, dvd.v)], incs=[(dvd, 1)])
        p.emit("vector", lambda e: e.scalar_tensor_tensor(g2_row[:], g2_row[:], -1e9, gate_row[:], ALU.mult, ALU.add),
               waits=[(dvd, dvd.v)], incs=[(dvd, 1)])
        p.emit("vector", lambda e: e.reduce_max(s11[:], g2_row[:], axis=AX.X), waits=[(dvd, dvd.v)], incs=[(dvd, 1)])
        p.emit("vector", lambda e: e.tensor_scalar(g2_row[:], gate_row[:], s11[0:1, 0:1], None, ALU.is_ge),
               waits=[(dvd, dvd.v)], incs=[(dvd, 1)])
        p.emit("vector", lambda e: e.tensor_tensor(mk_row[:], gate_row[:], g2_row[:], ALU.mult),
               waits=[(dvd, dvd.v)], incs=[(dvd, 1)])
        p.emit("vector", lambda e: e.reduce_sum(s11[:], mk_row[:], axis=AX.X), waits=[(dvd, dvd.v)], incs=[(dvd, 1)])
        p.emit("vector", lambda e: e.tensor_scalar(s11[:], s11[:], 1e-8, None, ALU.add), waits=[(dvd, dvd.v)], incs=[(dvd, 1)])
        p.emit("vector", lambda e: e.reciprocal(s11b[:], s11[:]), waits=[(dvd, dvd.v)], incs=[(dvd, 1)])
        [vmk] = p.emit("vector", lambda e: e.tensor_scalar(mk_row[:], mk_row[:], s11b[0:1, 0:1], None, ALU.mult),
                       waits=[(dvd, dvd.v)], incs=[(dvd, 1)])
        p.emit("tensor", lambda e: e.matmul(h_ps[1][0:128, 0:8], onesr_f[:], mk_row[:], start=True, stop=True),
               waits=[(dvd, vmk)] + GUARD(), incs=[(ped, 1)])
        p.emit("vector", lambda e: e.tensor_copy(mkb_s[:], h_ps[1][0:128, 0:8]),
               waits=[(ped, ped.v), (dvd, dvd.v)], incs=[(dvd, 1)])
        for pr in range(4):
            p.emit("vector", (lambda pr: lambda e: e.tensor_scalar(
                mvec_s[:, pr:pr + 1], halves_s[:, 0:1], mkb_s[:, 2 * pr:2 * pr + 1], None, ALU.mult))(pr),
                waits=[(dvd, dvd.v)], incs=[(dvd, 1)])
            p.emit("vector", (lambda pr: lambda e: e.scalar_tensor_tensor(
                mvec_s[:, pr:pr + 1], halves_s[:, 1:2], mkb_s[:, 2 * pr + 1:2 * pr + 2],
                mvec_s[:, pr:pr + 1], ALU.mult, ALU.add))(pr),
                waits=[(dvd, dvd.v)], incs=[(dvd, 1)])
        mvec_done = (dvd, dvd.v)
        for c in range(8):
            p.emit("vector", (lambda c: lambda e: e.tensor_scalar(
                bumix_col[:, c:c + 1], upb_cm_s[:, 8 * c:8 * c + 1], mkb_s[:, 0:1], None, ALU.mult))(c),
                waits=[(dvd, dvd.v)], incs=[(dvd, 1)])
            for e8 in range(1, 8):
                p.emit("vector", (lambda c, e8: lambda e: e.scalar_tensor_tensor(
                    bumix_col[:, c:c + 1], upb_cm_s[:, 8 * c + e8:8 * c + e8 + 1], mkb_s[:, e8:e8 + 1],
                    bumix_col[:, c:c + 1], ALU.mult, ALU.add))(c, e8),
                    waits=[(dvd, dvd.v)], incs=[(dvd, 1)])
        bumix_done = (dvd, dvd.v)
        for pr in range(4):
            p.emit("vector", (lambda pr: lambda e: e.tensor_scalar(
                wu_sc[:, 1024 * pr:1024 * pr + 1024], upT_b[:, 1024 * pr:1024 * pr + 1024],
                mvec_s[:, pr:pr + 1], None, ALU.mult))(pr),
                waits=([mvec_done, ffnw_ld, (ped, ped.v)] if pr == 0 else [(dvd, dvd.v)]), incs=[(dvd, 1)])
        wu_all = (dvd, dvd.v)
        res_bank_free = [0, 0]
        for tt in range(4):
            for c in range(8):
                bk = (4 * tt + c) % 2
                for pr in range(4):
                    w = []
                    if tt == 0 and c == 0 and pr == 0:
                        w = [wu_all, h_all, bumix_done] + GUARD()
                    elif pr == 0:
                        w = [(dvd, res_bank_free[bk])]
                    p.emit("tensor", (lambda tt, c, pr, bk: lambda e: e.matmul(
                        g_ps[bk][0:128, 0:512], wu_sc[:, 1024 * pr + 128 * c:1024 * pr + 128 * c + 128],
                        ckv_g[:, 2048 * pr + 512 * tt:2048 * pr + 512 * tt + 512], start=(pr == 0), stop=(pr == 3)))(tt, c, pr, bk),
                        waits=w, incs=[(ped, 1)])
                [rv] = p.emit("vector", (lambda tt, c, bk: lambda e: e.scalar_tensor_tensor(
                    xT[:, 2048 * c + 512 * tt:2048 * c + 512 * tt + 512],
                    g_ps[bk][0:128, 0:512], bumix_col[:, c:c + 1],
                    xT[:, 2048 * c + 512 * tt:2048 * c + 512 * tt + 512], ALU.add, ALU.add))(tt, c, bk),
                    waits=[(ped, ped.v), (dvd, dvd.v)], incs=[(dvd, 1)])
                res_bank_free[bk] = rv
        state["stream_ready"] = (dvd, dvd.v)

    if stages >= 2:
        emit_ffn(0)

    if stages >= 3:
        emit_ln(False)
        xn3 = state["xn_ready"]
        [sv1] = p.emit("sync", lambda e: e.dma_start(
            out=cc_xn_i1.ap().rearrange("(c p) t -> p c t", p=128),
            in_=xnT[:].rearrange("p (c t) -> p c t", c=8)[:, :, 0:1024]),
            waits=[(dvd, state["xn_tt"][1])], incs=[(lx, 16)])
        [ccx1] = p.emit("gpsimd", lambda e: e.collective_compute(
            "AllGather", ALU.bypass, replica_groups=GROUPS, ins=[cc_xn_i1.ap().opt()], outs=[cc_xn_o1.ap().opt()]),
            waits=[(lx, sv1)], incs=[(ccd, 1)])
        [sv2] = p.emit("sync", lambda e: e.dma_start(
            out=cc_xn_i2.ap().rearrange("(c p) t -> p c t", p=128),
            in_=xnT[:].rearrange("p (c t) -> p c t", c=8)[:, :, 1024:2048]),
            waits=[xn3], incs=[(lx2, 16)])
        [ccx2] = p.emit("gpsimd", lambda e: e.collective_compute(
            "AllGather", ALU.bypass, replica_groups=GROUPS, ins=[cc_xn_i2.ap().opt()], outs=[cc_xn_o2.ap().opt()]),
            waits=[(lx2, sv2)], incs=[(ccd, 1)])
        xn_ag1 = (ccd, ccx1)
        xn_ag2 = (ccd, ccx2)

        for pr2 in range(2):
            [wq_ld] = p.emit("gpsimd", (lambda pr2: lambda e: e.dma_start(
                out=wqkv_b[:].rearrange("p (h c f) -> p h c f", h=2, c=8), in_=wqkv[2 * pr2:2 * pr2 + 2, :, :].rearrange("h (c p) f -> p h c f", p=128)))(pr2),
                waits=[(pvd, pvd.v), (expd, expd.v), (smm, smm.v), (dvd, dvd.v), (acd, acd.v), (ped, ped.v)], incs=[(lds, 16)])
            Qd = [lambda g: xnT[:, 512 * g:512 * g + 512][0:64, :],
                  lambda g: ckv_g[:, 512 * g:512 * g + 512][64:128, :]]
            Kd = [lambda g: xnT[:, 8192 + 512 * g:8192 + 512 * g + 512][0:64, :],
                  lambda g: xnT[:, 8192 + 512 * g:8192 + 512 * g + 512][64:128, :]]
            Vd = [lambda g: xnT[:, 512 * g:512 * g + 512][64:128, :],
                  lambda g: ckv_g[:, 512 * g:512 * g + 512][0:64, :]]
            Kc = [lambda t: xnT[:, 8192 + 128 * t:8192 + 128 * t + 128][0:64, :],
                  lambda t: xnT[:, 8192 + 128 * t:8192 + 128 * t + 128][64:128, :]]
            stage_free = [0, 0]
            evac_free = [0, 0]
            for g in range(16):
                i4, s4 = g % 4, g // 4
                st = ST0 if g % 2 == 0 else ST1
                xnago = (xn_ag1 if s4 < 2 else xn_ag2)
                cco_h = (cc_xn_o1 if s4 < 2 else cc_xn_o2)
                s4h = s4 % 2
                w = [xnago] + ([(pvd, pvd.v), (expd, expd.v), (smm, smm.v), (dvd, dvd.v)] if g < 2 else [(ped, stage_free[g % 2])])
                [stl] = p.emit("sync", (lambda i4, s4h, st, cco_h: lambda e: e.dma_start(
                    out=u16[:, st:st + 4096].rearrange("p (c t) -> p c t", c=8),
                    in_=cco_h.ap()[1024 * i4:1024 * i4 + 1024, 512 * s4h:512 * s4h + 512].rearrange("(c p) t -> p c t", p=128)))(i4, s4h, st, cco_h),
                    waits=w, incs=[(lst[g % 2], 16)])
                for hh in range(2):
                    bk = g % 2
                    qb = 64 * hh
                    wqk = [(lst[g % 2], stl), (lds, wq_ld), (acd, evac_free[bk]), (ped, ped.v)]
                    if g < 2 and hh == 0:
                        wqk += GUARD()
                    for c in range(8):
                        p.emit("tensor", (lambda hh, c, st, bk, qb: lambda e: e.matmul(
                            g_ps[bk][qb:qb + 64, 0:512], wqkv_b[:, 1536 * hh + 192 * c:1536 * hh + 192 * c + 64],
                            u16[:, st + 512 * c:st + 512 * c + 512], start=(c == 0), stop=(c == 7),
                            tile_position=(0, qb)))(hh, c, st, bk, qb),
                            waits=wqk if c == 0 else (), incs=[(ped, 1)])
                    for c in range(8):
                        p.emit("tensor", (lambda hh, c, st, bk: lambda e: e.matmul(
                            h_ps[bk][0:128, 0:512], wqkv_b[:, 1536 * hh + 192 * c + 64:1536 * hh + 192 * c + 192],
                            u16[:, st + 512 * c:st + 512 * c + 512], start=(c == 0), stop=(c == 7)))(hh, c, st, bk),
                            incs=[(ped, 1)])
                    pe_g = (ped, ped.v)
                    p.emit("scalar", (lambda hh, g, bk, pr2, qb: lambda e: e.activation(
                        out=Qd[hh](g), in_=g_ps[bk][qb:qb + 64, 0:512],
                        func=AF.Identity, bias=bq_s[qb:qb + 64, 2 * pr2 + hh:2 * pr2 + hh + 1]))(hh, g, bk, pr2, qb),
                        waits=[pe_g, (acd, acd.v)], incs=[(acd, 1)])
                    p.emit("scalar", (lambda hh, g, bk, pr2, qb: lambda e: e.activation(
                        out=Kd[hh](g), in_=h_ps[bk][qb:qb + 64, 0:512],
                        func=AF.Identity, bias=bk_s[qb:qb + 64, 2 * pr2 + hh:2 * pr2 + hh + 1]))(hh, g, bk, pr2, qb),
                        incs=[(acd, 1)])
                    vb0 = 64 * (1 - hh)
                    [ve] = p.emit("scalar", (lambda hh, g, bk, pr2, vb0: lambda e: e.activation(
                        out=Vd[hh](g), in_=h_ps[bk][vb0:vb0 + 64, 0:512],
                        func=AF.Identity, bias=bv_s[vb0:vb0 + 64, 2 * pr2 + hh:2 * pr2 + hh + 1]))(hh, g, bk, pr2, vb0),
                        incs=[(acd, 1)])
                    evac_free[bk] = ve
                stage_free[g % 2] = ped.v
            qkv_all = (acd, acd.v)
            [vms] = p.emit("vector", lambda e: e.memset(vbuf[:], 1.0),
                           waits=[(pvd, pvd.v), (dvd, dvd.v)], incs=[(dvd, 1)])
            two_run_out2 = lambda ap130: ap130.rearrange("p (a b) -> p a b", b=65)[:, :, 0:64]
            vt_done = []
            for t in range(64):
                for hh in range(2):
                    w = ([qkv_all, (dvd, vms), (expd, expd.v), (smm, smm.v), (ped, ped.v)] if (t == 0 and hh == 0)
                         else ([(dvd, vt_done[2 * t + hh - 2])] if 2 * t + hh >= 2 else []))
                    vsrc = (xnT[:, 128 * t:128 * t + 128][64:128, :] if hh == 0
                            else ckv_g[:, 128 * t:128 * t + 128][0:64, :])
                    ident_sl = id2_s[64:128, :] if hh == 0 else id2_s[0:64, :]
                    p.emit("tensor", (lambda t, hh, vsrc, ident_sl: lambda e: e.transpose(
                        s_ps[(2 * t + hh) % 2][0:128, 0:32].bitcast(BF16)[:, 0:64], vsrc, ident_sl))(t, hh, vsrc, ident_sl),
                        waits=w, incs=[(ped, 1)])
                    [vt] = p.emit("vector", (lambda t, hh: lambda e: e.tensor_copy(
                        vbuf[:, 130 * t + 65 * hh:130 * t + 65 * hh + 64],
                        s_ps[(2 * t + hh) % 2][0:128, 0:32].bitcast(BF16)[:, 0:64]))(t, hh),
                        waits=[(ped, ped.v), (dvd, dvd.v)], incs=[(dvd, 1)])
                    vt_done.append(vt)
            fl["sfree"] = [expd.v, expd.v]
            vb_ready = (dvd, dvd.v)
            for hh in range(2):
                qb = 64 * hh
                for cc in range(16):
                    p.emit("vector", (lambda hh, cc, qb: lambda e: e.tensor_tensor(
                        sq_b[qb:qb + 64, :], Kd[hh](cc), Kd[hh](cc), ALU.mult))(hh, cc, qb),
                        waits=[qkv_all, (dvd, dvd.v)], incs=[(dvd, 1)])
                    p.emit("tensor", (lambda hh, cc, qb: lambda e: e.matmul(
                        g_ps[cc % 2][0:1, 0:512], ones_col_b[qb:qb + 64, :], sq_b[qb:qb + 64, :], start=True, stop=True))(hh, cc, qb),
                        waits=[(dvd, dvd.v)] + (GUARD() if (hh == 0 and cc == 0) else [(ped, ped.v), (dvd, dvd.v)]),
                        incs=[(ped, 1)])
                    if cc == 0:
                        p.emit("vector", (lambda hh, cc: lambda e: e.reduce_max(
                            kmax_s[0:1, 1 + hh:2 + hh], g_ps[cc % 2][0:1, 0:512], axis=AX.X))(hh, cc),
                            waits=[(ped, ped.v), (dvd, dvd.v)], incs=[(dvd, 1)])
                    else:
                        p.emit("vector", (lambda hh, cc: lambda e: e.reduce_max(
                            z_s[0:1, 0:1], g_ps[cc % 2][0:1, 0:512], axis=AX.X))(hh, cc),
                            waits=[(ped, ped.v), (dvd, dvd.v), (acd, fl["z_free"])], incs=[(dvd, 1)])
                        p.emit("vector", (lambda hh: lambda e: e.tensor_max(
                            kmax_s[0:1, 1 + hh:2 + hh], kmax_s[0:1, 1 + hh:2 + hh], z_s[0:1, 0:1]))(hh),
                            waits=[(dvd, dvd.v)], incs=[(dvd, 1)])
            knb_done = (dvd, dvd.v)
            def mk_job_b(hh, jb):
                return {
                    "qp": 64, "qb": 64 * hh, "nA": 4 * jb, "moff0": MOFF_B, "kn_col": 1 + hh,
                    "wkn": [knb_done],
                    "ql_ap": Qd[hh](jb),
                    "wq": [qkv_all, vb_ready],
                    "wk0": [],
                    "lhsA": (lambda t, hh=hh: Kc[hh](t)),
                    "lhsB": (lambda i, hh=hh, jb=jb: Kc[hh](4 * jb + i)),
                    "vA": (lambda t, hh=hh: vbuf[:, 130 * t + 65 * hh:130 * t + 65 * hh + 65]),
                    "vB": (lambda i, hh=hh, jb=jb: vbuf[:, 130 * (4 * jb + i) + 65 * hh:
                                                        130 * (4 * jb + i) + 65 * hh + 65]),
                    "dst": (hh, jb),
                }
            jobs_b = [mk_job_b(hh, jb) for hh in range(2) for jb in range(16)]
            for ji, job in enumerate(jobs_b):
                if ji == 0:
                    emit_prelude(job)
                if ji + 1 < len(jobs_b):
                    fl["pend_pre"] = (lambda nxt: lambda: emit_prelude(nxt))(jobs_b[ji + 1])
                emit_kloop(job)
                pv_done = (pvd, pvd.v)
                hhj, jbj = job["dst"]
                hl = 2 * pr2 + hhj
                own, slot = jbj % 4, jbj // 4
                dram_fn = (lambda own, slot, hl: lambda e, ybuf: e.dma_start(
                    out=cc_y_i.ap()[256 * own + 64 * hl:256 * own + 64 * hl + 64, 512 * slot:512 * slot + 512],
                    in_=ybuf[:]))(own, slot, hl)
                fl["pend_epi"] = (lambda job, pv_done, dram_fn: lambda: flash_epilogue(job, pv_done, dram_fn=dram_fn))(job, pv_done, dram_fn)
        if fl.get("pend_epi") is not None:
            _fn = fl["pend_epi"]
            fl["pend_epi"] = None
            _fn()
        [ccy] = p.emit("gpsimd", lambda e: e.collective_compute(
            "AllGather", ALU.bypass, replica_groups=GROUPS, ins=[cc_y_i.ap().opt()], outs=[cc_y_o.ap().opt()]),
            waits=[(ly[0], fl["ynrm_free"][0]), (ly[1], fl["ynrm_free"][1])], incs=[(ccd, 1)])
        y_ag = (ccd, ccy)
        for tt in range(4):
            for c in range(8):
                i_rk = c // 2
                hl0 = (2 * c) % 4
                def ysrc(o):
                    return cc_y_o.ap()[1024 * i_rk + 256 * o + 64 * hl0:
                                       1024 * i_rk + 256 * o + 64 * hl0 + 128, 512 * tt:512 * tt + 512]
                w0 = [y_ag, (dvd, dvd.v)] if (tt == 0 and c == 0) else [(dvd, dvd.v)]
                [l0] = p.emit("sync", (lambda a=ysrc(0): lambda e: e.dma_start(out=cand[0][:], in_=a))(),
                              waits=w0, incs=[(lc, 16)])
                [l1] = p.emit("sync", (lambda a=ysrc(1): lambda e: e.dma_start(out=cand[1][:], in_=a))(),
                              incs=[(lc, 16)])
                p.emit("vector", lambda e: e.tensor_scalar(cand[0][:], cand[0][:], ind_s[:, 0:1], None, ALU.mult),
                       waits=[(lc, l1), (dvd, dvd.v), (smm, smm.v), (ped, ped.v)], incs=[(dvd, 1)])
                [a1] = p.emit("vector", lambda e: e.scalar_tensor_tensor(
                    cand[0][:], cand[1][:], ind_s[:, 1:2], cand[0][:], ALU.mult, ALU.add),
                    waits=[(dvd, dvd.v)], incs=[(dvd, 1)])
                [l2] = p.emit("sync", (lambda a=ysrc(2): lambda e: e.dma_start(out=cand[1][:], in_=a))(),
                              waits=[(dvd, a1)], incs=[(lc, 16)])
                p.emit("vector", lambda e: e.scalar_tensor_tensor(
                    cand[0][:], cand[1][:], ind_s[:, 2:3], cand[0][:], ALU.mult, ALU.add),
                    waits=[(lc, l2), (dvd, dvd.v)], incs=[(dvd, 1)])
                [l3] = p.emit("sync", (lambda a=ysrc(3): lambda e: e.dma_start(out=cand[1][:], in_=a))(),
                              waits=[(dvd, dvd.v)], incs=[(lc, 16)])
                p.emit("vector", lambda e: e.scalar_tensor_tensor(
                    cand[0][:], cand[1][:], ind_s[:, 3:4], cand[0][:], ALU.mult, ALU.add),
                    waits=[(lc, l3), (dvd, dvd.v)], incs=[(dvd, 1)])
                p.emit("vector", (lambda c, tt: lambda e: e.tensor_copy(
                    xnT[:, 2048 * c + 512 * tt:2048 * c + 512 * tt + 512], cand[0][:]))(c, tt),
                    waits=[(dvd, dvd.v)], incs=[(dvd, 1)])
        ysel_all = (dvd, dvd.v)
        res_bank_free = [0, 0]
        for do in range(8):
            [wl] = p.emit("gpsimd", (lambda do: lambda e: e.dma_start(
                out=woutc_b[do % 2][:].rearrange("p (c f) -> p c f", c=8), in_=wout[:, 128 * do:128 * do + 128].rearrange("(c p) f -> p c f", p=128)))(do),
                waits=[(ped, ped.v), (smm, smm.v)], incs=[(lds, 16)])
            for tt in range(4):
                bk = (4 * do + tt) % 2
                w = [(lds, wl)]
                if do == 0 and tt == 0:
                    w += [ysel_all] + GUARD()
                else:
                    w += [(dvd, res_bank_free[bk])]
                for c in range(8):
                    p.emit("tensor", (lambda do, tt, c, bk: lambda e: e.matmul(
                        g_ps[bk][0:128, 0:512], woutc_b[do % 2][:, 128 * c:128 * c + 128],
                        xnT[:, 2048 * c + 512 * tt:2048 * c + 512 * tt + 512], start=(c == 0), stop=(c == 7)))(do, tt, c, bk),
                        waits=w if c == 0 else (), incs=[(ped, 1)])
                [rv] = p.emit("vector", (lambda do, tt, bk: lambda e: e.scalar_tensor_tensor(
                    xT[:, 2048 * do + 512 * tt:2048 * do + 512 * tt + 512],
                    g_ps[bk][0:128, 0:512], bout_cm_s[:, do:do + 1],
                    xT[:, 2048 * do + 512 * tt:2048 * do + 512 * tt + 512], ALU.add, ALU.add))(do, tt, bk),
                    waits=[(ped, ped.v), (dvd, dvd.v)], incs=[(dvd, 1)])
                res_bank_free[bk] = rv
        state["stream_ready"] = (dvd, dvd.v)

    if stages >= 4:
        emit_ffn(1)

    [so] = p.emit("sync", lambda e: e.dma_start(
        out=out_feat.rearrange("(c p) t -> p c t", p=128), in_=xT[:].rearrange("p (c t) -> p c t", c=8)),
        waits=[state["stream_ready"]], incs=[(std, 16)])
    p.emit("gpsimd", lambda e: e.engine_nop(), waits=[(std, so)])

    p.replay()
    return nc


_NC_CACHE = {}


def _get_nc(stages=4):
    if stages not in _NC_CACHE:
        _NC_CACHE[stages] = _build(stages)
    return _NC_CACHE[stages]


def _make_inmaps(inputs):
    f32 = np.float32
    import ml_dtypes
    bf = ml_dtypes.bfloat16
    C = np.ascontiguousarray
    x = np.asarray(inputs["L0_attn_a_x"], f32)
    W_dq = np.asarray(inputs["L0_attn_a_W_dq"], f32)
    W_uq = np.asarray(inputs["L0_attn_a_W_uq"], f32)
    W_dkv = np.asarray(inputs["L0_attn_a_W_dkv"], f32)
    W_uk = np.asarray(inputs["L0_attn_a_W_uk"], f32)
    W_uv = np.asarray(inputs["L0_attn_a_W_uv"], f32)
    W_o = np.asarray(inputs["L0_attn_a_W_o"], f32)
    Wqkv = np.asarray(inputs["L0_attn_b_Wqkv"], f32)
    bqkv = np.asarray(inputs["L0_attn_b_bqkv"], f32)
    Wout = np.asarray(inputs["L0_attn_b_Wout"], f32)
    bout = np.asarray(inputs["L0_attn_b_bout"], f32)
    gate_w = np.asarray(inputs["L0_ffn_gate_w"], f32)
    down_w = np.asarray(inputs["L0_ffn_down_w"], f32)
    down_b = np.asarray(inputs["L0_ffn_down_b"], f32)
    up_w = np.asarray(inputs["L0_ffn_up_w"], f32)
    up_b = np.asarray(inputs["L0_ffn_up_b"], f32)

    bdr = np.zeros((128, 4), f32)
    for pr in range(4):
        bdr[0:64, pr] = down_b[2 * pr]
        bdr[64:128, pr] = down_b[2 * pr + 1]
    shared = {
        "w_dq": C(W_dq), "w_dqT": C(W_dq.T), "w_uq": C(W_uq), "w_uk": C(W_uk), "w_uv": C(W_uv),
        "w_oT": C(W_o.T), "w_dkvT": C(W_dkv.T),
        "wout": C(Wout), "bout_cm": C(bout.reshape(8, 128).T),
        "gate_wT": C(gate_w.T),
        "dwT": C(down_w.transpose(2, 0, 1).reshape(1024, 512)),
        "bd_rows": bdr,
        "upT": C(up_w.transpose(0, 2, 1).reshape(512, 1024)),
        "upb_cm": C(up_b.reshape(8, 8, 128).transpose(2, 1, 0).reshape(128, 64)),
        "halves": np.concatenate([(np.arange(128) < 64).astype(f32)[:, None],
                                  (np.arange(128) >= 64).astype(f32)[:, None]], axis=1),
        "tri": C(np.triu(np.ones((128, 128))).astype(bf)),
        "ident": C(np.eye(128).astype(bf)),
        "onesf_r128": np.ones((1, 128), f32),
        "negonesf_r128": -np.ones((1, 128), f32),
        "onesf_r64": np.ones((1, 64), f32),
        "ident64f": np.eye(64, dtype=f32),
        "ident2": C(np.vstack([np.eye(64), np.eye(64)]).astype(bf)),
    }
    in_maps = []
    for c in range(8):
        B, R = c // 4, c % 4
        im = dict(shared)
        xo = np.empty((2048, 1024), f32)
        for s in range(4):
            j = 4 * s + R
            xo[512 * s:512 * s + 512] = x[B, 512 * j:512 * j + 512]
        im["x_own"] = C(xo.T)
        mo = np.zeros((128, 209), f32)
        for s in range(4):
            nA = 16 * s + 12
            real = 16 * s + 4 * R
            col = MOFF0[s]
            if real < nA:
                mo[:, col + real:col + nA] = NEG
        im["moff"] = mo
        wq4 = Wqkv[4 * R:4 * R + 4].copy()
        for hl in range(1, 4, 2):
            kcols = wq4[hl, :, 64:128].copy()
            wq4[hl, :, 64:128] = wq4[hl, :, 128:192]
            wq4[hl, :, 128:192] = kcols
        im["wqkv"] = C(wq4)
        bq = np.zeros((128, 4), f32)
        bk = np.zeros((128, 4), f32)
        bv = np.zeros((128, 4), f32)
        for hl in range(4):
            qb = 64 * (hl % 2)
            vb = 64 * (1 - hl % 2)
            bq[qb:qb + 64, hl] = bqkv[4 * R + hl, 0:64]
            bk[qb:qb + 64, hl] = bqkv[4 * R + hl, 64:128]
            bv[vb:vb + 64, hl] = bqkv[4 * R + hl, 128:192]
        im["bq_in"] = bq
        im["bk_in"] = bk
        im["bv_in"] = bv
        indm = np.zeros((128, 4), f32)
        indm[:, R] = 1.0
        im["ind"] = indm
        in_maps.append(im)
    return in_maps


def kernel(**inputs):
    stages = int(inputs.pop("_stages", 4))
    nc = _get_nc(stages)
    in_maps = _make_inmaps(inputs)
    res = run_bass_kernel_spmd(nc, in_maps, core_ids=list(range(8)))
    out = np.empty((2, 8192, 1024), np.float32)
    for c in range(8):
        B, R = c // 4, c % 4
        shard = np.asarray(res.results[c]["out_feat"]).T
        for s in range(4):
            j = 4 * s + R
            out[B, 512 * j:512 * j + 512] = shard[512 * s:512 * s + 512]
    return out


# version 10 (final)
# speedup vs baseline: 1.2101x; 1.0225x over previous
import math
import numpy as np
import concourse.bass as bass
import concourse.mybir as mybir
from concourse.bass_utils import run_bass_kernel_spmd
from contextlib import ExitStack

F32 = mybir.dt.float32
BF16 = mybir.dt.bfloat16
AF = mybir.ActivationFunctionType
ALU = mybir.AluOpType
AX = mybir.AxisListType

NEG = -1.0e9
EPS_LN = 1e-5
GROUPS = [[0, 1, 2, 3], [4, 5, 6, 7]]
MOFF0 = [0, 13, 42, 87]
MOFF_B = 148


class SemC:
    __slots__ = ("h", "v", "name")

    def __init__(self, h, name):
        self.h = h
        self.v = 0
        self.name = name


class Prog:
    ENGS = ("sync", "tensor", "scalar", "vector", "gpsimd")

    def __init__(self, nc, es):
        self.nc = nc
        self.es = es
        self.q = {e: [] for e in self.ENGS}

    def sem(self, name):
        return SemC(self.es.enter_context(self.nc.semaphore(name)), name)

    def emit(self, eng, fn, waits=(), incs=()):
        out_vals = []
        rec = []
        for s, n in incs:
            s.v += n
            out_vals.append(s.v)
            rec.append((s.h, n))
        ws = []
        seen = {}
        for s, v in waits:
            if v and v > 0 and seen.get(id(s), -1) < v:
                seen[id(s)] = v
        for s, v in waits:
            if seen.get(id(s), -1) == v and v and v > 0:
                ws.append((s.h, v))
                seen[id(s)] = -2
        self.q[eng].append((tuple(ws), fn, tuple(rec)))
        return out_vals

    def replay(self):
        nc = self.nc
        with nc.Block() as block:
            def mk(engq):
                def _(eng):
                    for waits, fn, incs in self.q[engq]:
                        for (h, v) in waits:
                            eng.wait_ge(h, v)
                        if fn is None:
                            continue
                        ins = fn(eng)
                        for (h, n) in incs:
                            ins = ins.then_inc(h, n)
                return _
            block.sync(mk("sync"))
            block.tensor(mk("tensor"))
            block.scalar(mk("scalar"))
            block.vector(mk("vector"))
            block.gpsimd(mk("gpsimd"))


def _build(stages=4):
    nc = bass.Bass("TRN2", target_bir_lowering=False)
    D = lambda n, sh, dt=F32, kind="ExternalInput": nc.dram_tensor(n, sh, dt, kind=kind).ap()
    x_own = D("x_own", [1024, 2048])
    w_dq = D("w_dq", [128, 1024])
    w_dqT = D("w_dqT", [1024, 128])
    w_uq = D("w_uq", [1024, 128])
    w_uk = D("w_uk", [1024, 128])
    w_uv = D("w_uv", [1024, 128])
    w_oT = D("w_oT", [1024, 1024])
    w_dkvT = D("w_dkvT", [1024, 128])
    wqkv = D("wqkv", [4, 1024, 192])
    bq_in = D("bq_in", [128, 4])
    bk_in = D("bk_in", [128, 4])
    bv_in = D("bv_in", [128, 4])
    ident64f = D("ident64f", [64, 64])
    ident2 = D("ident2", [128, 64], BF16)
    wout = D("wout", [1024, 1024])
    bout_cm = D("bout_cm", [128, 8])
    gate_wT = D("gate_wT", [1024, 8])
    dwT = D("dwT", [1024, 512])
    bd_rows = D("bd_rows", [128, 4])
    upT = D("upT", [512, 1024])
    upb_cm = D("upb_cm", [128, 64])
    halves = D("halves", [128, 2])
    tri = D("tri", [128, 128], BF16)
    ident = D("ident", [128, 128], BF16)
    moff = D("moff", [128, 209])
    onesf_r128 = D("onesf_r128", [1, 128])
    negonesf_r128 = D("negonesf_r128", [1, 128])
    onesf_r64 = D("onesf_r64", [1, 64])
    ind = D("ind", [128, 4])
    out_feat = D("out_feat", [1024, 2048], kind="ExternalOutput")

    cc_ckv_i = nc.dram_tensor("cc_ckv_i", [128, 2048], BF16)
    cc_ckv_o = nc.dram_tensor("cc_ckv_o", [512, 2048], BF16)
    cc_pool_i = nc.dram_tensor("cc_pool_i", [128, 8], F32)
    cc_pool_o = nc.dram_tensor("cc_pool_o", [128, 8], F32)
    cc_pool_i2 = nc.dram_tensor("cc_pool_i2", [128, 8], F32)
    cc_pool_o2 = nc.dram_tensor("cc_pool_o2", [128, 8], F32)
    cc_xn_i1 = nc.dram_tensor("cc_xn_i1", [1024, 1024], BF16)
    cc_xn_i2 = nc.dram_tensor("cc_xn_i2", [1024, 1024], BF16)
    cc_xn_o1 = nc.dram_tensor("cc_xn_o1", [4096, 1024], BF16)
    cc_xn_o2 = nc.dram_tensor("cc_xn_o2", [4096, 1024], BF16)
    cc_y_i = nc.dram_tensor("cc_y_i", [1024, 2048], BF16)
    cc_y_o = nc.dram_tensor("cc_y_o", [4096, 2048], BF16)

    es = ExitStack()
    S = lambda n, sh, dt=F32: es.enter_context(nc.sbuf_tensor(n, sh, dt))
    P = lambda n, sh, dt=F32: es.enter_context(nc.psum_tensor(n, sh, dt))

    xT = S("xT", [128, 16384])
    xnT = S("xnT", [128, 16384], BF16)
    vbuf = S("vbuf", [128, 130 * 64], BF16)
    u16 = S("u16", [128, 8224], BF16)
    CKVO, Q1, AB, VOWN = 0, 2048, 4096, 6144
    ST0, ST1 = 0, 4096
    wqkv_b = S("wqkv_b", [128, 3072], BF16)
    veff_b = wqkv_b[:, 0:1024]
    woutc_b = [S(f"woutc_b{i}", [128, 1024], BF16) for i in range(2)]
    wdqT_b = woutc_b[0][:]
    wdkvT_b = woutc_b[1][:]
    gwT_s = S("gwT_s", [128, 64])
    upb_cm_s = S("upb_cm_s", [128, 64])
    bout_cm_s = S("bout_cm_s", [128, 8])
    halves_s = S("halves_s", [128, 2])
    mkb_s = S("mkb_s", [128, 8])
    bumix_col = S("bumix_col", [128, 8])
    bdr_s = S("bdr_s", [128, 4])
    bq_s = S("bq_s", [128, 4])
    bk_s = S("bk_s", [128, 4])
    bv_s = S("bv_s", [128, 4])
    id64f_s = S("id64f_s", [64, 64])
    id2_s = S("id2_s", [128, 64], BF16)
    tri_b = S("tri_b", [128, 128], BF16)
    id_b = S("id_b", [128, 128], BF16)
    moff_s = S("moff_s", [128, 209])
    onesr_f = S("onesr_f", [1, 128])
    negonesr_f = S("negonesr_f", [1, 128])
    neg16r_f = S("neg16r_f", [1, 128])
    onesr64_f = S("onesr64_f", [1, 64])
    ind_s = S("ind_s", [128, 4])
    ones_col_b = S("ones_col_b", [128, 1], BF16)
    eps_s = S("eps_s", [1, 1])
    negln8_s = S("negln8_s", [1, 1])
    kmax_s = S("kmax_s", [1, 4])
    z_s = S("z_s", [1, 2])
    mhat_s = S("mhat_s", [1, 1])
    mneg = [S(f"mneg{i}", [128, 1]) for i in range(2)]
    bias_all = [S(f"bias_all{i}", [128, 64]) for i in range(2)]
    qlb = [S(f"qlb{i}", [128, 512], BF16) for i in range(2)]
    psb = [S(f"psb{i}", [128, 1024], BF16) for i in range(2)]
    sq_b = S("sq_b", [128, 512], BF16)
    xb_t = S("xb_t", [128, 512], BF16)
    m_row = S("m_row", [1, 512])
    q_row = S("q_row", [1, 512])
    r_row = S("r_row", [1, 512])
    u_sb = S("u_sb", [128, 512])
    ytmp_sb = u_sb[0:64, :]
    rec_sb = m_row[0:1, :]
    sums_sb = q_row[0:1, :]
    onesf_row = S("onesf_row", [1, 512])
    bc_sb = S("bc_sb", [64, 512])
    ynrm_b = [S(f"ynrm_b{i}", [64, 512], BF16) for i in range(2)]
    pool_acc = S("pool_acc", [128, 8])
    pool_in = S("pool_in", [128, 8])
    pool_t = S("pool_t", [128, 1])
    gate_row = S("gate_row", [1, 8])
    g2_row = S("g2_row", [1, 8])
    mk_row = S("mk_row", [1, 8])
    s11 = S("s11", [1, 1])
    s11b = S("s11b", [1, 1])
    mvec_s = S("mvec_s", [128, 4])
    cand = [S(f"cand{i}", [128, 512], BF16) for i in range(2)]
    scr = S("scr", [128, 10240])
    scr_b = scr[:].bitcast(BF16)
    ckv_g = scr_b[:, 0:8192]
    dwT_b = scr_b[:, 8192:12288]
    upT_b = scr_b[:, 12288:16384]
    wu_sc = scr_b[:, 16384:20480]

    spair = [P(f"spair{i}", [128, 1024]) for i in range(2)]
    y_ps = [P(f"y_ps{i}", [128, 512]) for i in range(2)]
    g_ps = [P(f"g_ps{i}", [128, 512]) for i in range(2)]
    h_ps = [spair[0][:, 0:512], spair[0][:, 512:1024]]
    s_ps = [spair[0][:, 0:512], spair[1][:, 0:512]]

    p = Prog(nc, es)
    ld = p.sem("ld")
    lk = p.sem("lk")
    lp = p.sem("lp")
    lx = p.sem("lx")
    lx2 = p.sem("lx2")
    lst = [p.sem("lst0"), p.sem("lst1")]
    ly = [p.sem("ly0"), p.sem("ly1")]
    lc = p.sem("lc")
    lds = p.sem("lds")
    smm = p.sem("smm")
    expd = p.sem("expd")
    pvd = p.sem("pvd")
    dvd = p.sem("dvd")
    ped = p.sem("ped")
    acd = p.sem("acd")
    ccd = p.sem("ccd")
    plr = p.sem("plr")
    std = p.sem("std")

    def GUARD():
        return [(ped, ped.v), (acd, acd.v), (dvd, dvd.v), (pvd, pvd.v), (expd, expd.v), (smm, smm.v)]

    p.emit("sync", lambda e: e.dma_start(out=xT[:].rearrange("p (c t) -> p c t", c=8), in_=x_own.rearrange("(c p) t -> p c t", p=128)), incs=[(ld, 16)])
    x_ld = None
    for dst, src in ((tri_b, tri), (id_b, ident), (moff_s, moff), (onesr_f, onesf_r128),
                     (negonesr_f, negonesf_r128), (onesr64_f, onesf_r64), (ind_s, ind),
                     (upb_cm_s, upb_cm), (bout_cm_s, bout_cm), (halves_s, halves), (bdr_s, bd_rows), (bq_s, bq_in), (bk_s, bk_in), (bv_s, bv_in), (id64f_s, ident64f), (id2_s, ident2)):
        p.emit("sync", (lambda dst, src: lambda e: e.dma_start(out=dst[:], in_=src[:]))(dst, src), incs=[(ld, 16)])
    p.emit("sync", lambda e: e.dma_start(out=gwT_s[:].rearrange("p (c e) -> p c e", c=8), in_=gate_wT.rearrange("(c p) e -> p c e", p=128)), incs=[(ld, 16)])
    const_ld = None
    SC_WUQ, SC_WUK, SC_WUV, SC_WDQ, SC_J, SC_KEFF = 0, 1024, 2048, 3072, 4096, 4224
    SC_WOT = 5248
    p.emit("sync", lambda e: e.dma_start(out=scr[:, SC_WUQ:SC_WUQ + 1024].rearrange("p (c j) -> p c j", c=8), in_=w_uq.rearrange("(c p) j -> p c j", p=128)), incs=[(ld, 16)])
    p.emit("sync", lambda e: e.dma_start(out=scr[:, SC_WUK:SC_WUK + 1024].rearrange("p (c j) -> p c j", c=8), in_=w_uk.rearrange("(c p) j -> p c j", p=128)), incs=[(ld, 16)])
    p.emit("sync", lambda e: e.dma_start(out=scr[:, SC_WUV:SC_WUV + 1024].rearrange("p (c j) -> p c j", c=8), in_=w_uv.rearrange("(c p) j -> p c j", p=128)), incs=[(ld, 16)])
    p.emit("sync", lambda e: e.dma_start(out=scr[:, SC_WDQ:SC_WDQ + 1024], in_=w_dq[:]), incs=[(ld, 16)])
    [wot0] = p.emit("sync", lambda e: e.dma_start(out=scr[:, SC_WOT:SC_WOT + 4096].rearrange("p (c d) -> p c d", c=8),
        in_=w_oT[:, 0:512].rearrange("(c p) d -> p c d", p=128)), incs=[(ld, 16)])
    boot_ld = (ld, ld.v)
    x_ld = const_ld = wprep_ld = boot_ld
    p.emit("gpsimd", lambda e: e.dma_start(out=wdqT_b.rearrange("p (c j) -> p c j", c=8), in_=w_dqT.rearrange("(c p) j -> p c j", p=128)), incs=[(lds, 16)])
    p.emit("gpsimd", lambda e: e.dma_start(out=wdkvT_b.rearrange("p (c j) -> p c j", c=8), in_=w_dkvT.rearrange("(c p) j -> p c j", p=128)), incs=[(lds, 16)])
    wbf_ld = (lds, lds.v)

    p.emit("vector", lambda e: e.memset(eps_s[:], EPS_LN), waits=[const_ld], incs=[(dvd, 1)])
    p.emit("vector", lambda e: e.memset(negln8_s[:], -math.log(8.0)), incs=[(dvd, 1)])
    p.emit("vector", lambda e: e.memset(neg16r_f[:], -1.0 / 16.0), incs=[(dvd, 1)])
    [v_ones] = p.emit("vector", lambda e: e.memset(ones_col_b[:], 1.0), incs=[(dvd, 1)])
    p.emit("vector", lambda e: e.memset(onesf_row[:], 1.0), incs=[(dvd, 1)])

    for c in range(8):
        p.emit("tensor", (lambda c: lambda e: e.matmul(
            g_ps[0][0:128, 0:128], scr[:, SC_WUQ + 128 * c:SC_WUQ + 128 * c + 128],
            scr[:, SC_WUK + 128 * c:SC_WUK + 128 * c + 128], start=(c == 0), stop=(c == 7)))(c),
            waits=[wprep_ld] if c == 0 else (), incs=[(ped, 1)])
    [vJ] = p.emit("vector", lambda e: e.tensor_copy(scr[:, SC_J:SC_J + 128], g_ps[0][0:128, 0:128]),
                  waits=[(ped, ped.v)], incs=[(dvd, 1)])
    for c in range(8):
        p.emit("tensor", (lambda c: lambda e: e.matmul(
            g_ps[1][0:128, 0:128], scr[:, SC_WDQ + 128 * c:SC_WDQ + 128 * c + 128],
            scr[:, SC_J:SC_J + 128], start=True, stop=True))(c),
            waits=[(dvd, dvd.v), (ped, ped.v)], incs=[(ped, 1)])
        p.emit("vector", (lambda c: lambda e: e.tensor_copy(
            scr[:, SC_KEFF + 128 * c:SC_KEFF + 128 * c + 128], g_ps[1][0:128, 0:128]))(c),
            waits=[(ped, ped.v)], incs=[(dvd, 1)])
    a_done = []
    for h in range(16):
        ro = 64 * (h % 2)
        co = 128 * (h // 2)
        p.emit("tensor", (lambda h, ro, co: lambda e: e.matmul(
            g_ps[h % 2][0:128, 128:256], scr[ro:ro + 64, SC_WUQ + co:SC_WUQ + co + 128],
            scr[ro:ro + 64, SC_KEFF + co:SC_KEFF + co + 128], start=True, stop=True))(h, ro, co),
            waits=[(dvd, dvd.v), (ped, ped.v)], incs=[(ped, 1)])
        [ad] = p.emit("vector", (lambda h: lambda e: e.tensor_copy(
            u16[:, AB + 128 * h:AB + 128 * h + 128], g_ps[h % 2][0:128, 128:256]))(h),
            waits=[(ped, ped.v)], incs=[(dvd, 1)])
        a_done.append(ad)
    for half in range(2):
        wv = [(ld, wot0)] if half == 0 else [(ld, ld.v)]
        for c in range(8):
            p.emit("tensor", (lambda half, c: lambda e: e.matmul(
                h_ps[half][0:128, 0:512], scr[:, SC_WUV + 128 * c:SC_WUV + 128 * c + 128],
                scr[:, SC_WOT + 512 * c:SC_WOT + 512 * c + 512],
                start=(c == 0), stop=(c == 7)))(half, c),
                waits=(wv + [(ped, ped.v)]) if c == 0 else (), incs=[(ped, 1)])
        pe_half = (ped, ped.v)
        if half == 0:
            p.emit("sync", lambda e: e.dma_start(out=scr[:, SC_WOT:SC_WOT + 4096].rearrange("p (c d) -> p c d", c=8),
                in_=w_oT[:, 512:1024].rearrange("(c p) d -> p c d", p=128)),
                waits=[pe_half], incs=[(ld, 16)])
        p.emit("vector", (lambda half: lambda e: e.tensor_copy(
            veff_b[:, 512 * half:512 * half + 512], h_ps[half][0:128, 0:512]))(half),
            waits=[pe_half, (dvd, dvd.v)], incs=[(dvd, 1)])
    prep_done = (dvd, dvd.v)
    p.emit("gpsimd", lambda e: e.dma_start(out=dwT_b.rearrange("p (c f) -> p c f", c=8), in_=dwT.rearrange("(c p) f -> p c f", p=128)),
           waits=[prep_done, (ped, ped.v), wbf_ld], incs=[(lds, 16)])
    p.emit("gpsimd", lambda e: e.dma_start(out=upT_b.rearrange("p (c d) -> p c d", c=4), in_=upT.rearrange("(c p) d -> p c d", p=128)),
           incs=[(lds, 16)])
    ffnw_ld = (lds, lds.v)

    state = {"stream_ready": (dvd, 0), "xn_ready": None, "pool_ready": None, "ld_guard": (lp, 0)}
    fl = {"sfree": [0, 0], "pfree": [0, 0], "yfree": [0, 0], "bias_free": [0, 0],
          "z_free": 0, "mhat_free": 0, "epi_free": 0, "ynrm_free": [0, 0], "epi_par": 0,
          "pend_epi": None, "jobctr": 0, "rec_free": 0}

    def emit_ln(with_pool):
        res_tt = state.get("res_tt")
        for tt in range(4):
            for c in range(8):
                if c == 0:
                    w = [(dvd, res_tt[tt]) if res_tt is not None else state["stream_ready"], (dvd, dvd.v)]
                else:
                    w = [(dvd, dvd.v)]
                w = list(w) + [(ped, ped.v)]
                sl = xT[:, 2048 * c + 512 * tt: 2048 * c + 512 * tt + 512]
                p.emit("vector", (lambda sl: lambda e: e.tensor_copy(xb_t[:], sl))(sl), waits=w, incs=[(dvd, 1)])
                cast_v = dvd.v
                p.emit("vector", lambda e: e.tensor_tensor(sq_b[:], xb_t[:], xb_t[:], ALU.mult),
                       waits=[(dvd, dvd.v)], incs=[(dvd, 1)])
                p.emit("tensor", (lambda c: lambda e: e.matmul(
                    g_ps[0][0:1, 0:512], ones_col_b[:], xb_t[:], start=(c == 0), stop=(c == 7)))(c),
                    waits=([(dvd, cast_v), (dvd, v_ones)] + (GUARD() if c == 0 else [])), incs=[(ped, 1)])
                p.emit("tensor", (lambda c: lambda e: e.matmul(
                    g_ps[1][0:1, 0:512], ones_col_b[:], sq_b[:], start=(c == 0), stop=(c == 7)))(c),
                    waits=[(dvd, dvd.v)], incs=[(ped, 1)])
            stat_done = (ped, ped.v)
            p.emit("vector", lambda e: e.tensor_scalar(m_row[:], g_ps[0][0:1, 0:512], 1.0 / 1024, None, ALU.mult),
                   waits=[stat_done, (dvd, dvd.v)], incs=[(dvd, 1)])
            p.emit("vector", lambda e: e.tensor_scalar(q_row[:], g_ps[1][0:1, 0:512], 1.0 / 1024, None, ALU.mult),
                   waits=[stat_done, (dvd, dvd.v)], incs=[(dvd, 1)])
            p.emit("vector", lambda e: e.tensor_tensor(r_row[:], m_row[:], m_row[:], ALU.mult),
                   waits=[(dvd, dvd.v), (acd, acd.v)], incs=[(dvd, 1)])
            p.emit("vector", lambda e: e.tensor_tensor(q_row[:], q_row[:], r_row[:], ALU.subtract),
                   waits=[(dvd, dvd.v)], incs=[(dvd, 1)])
            p.emit("scalar", lambda e: e.activation(out=q_row[:], in_=q_row[:], func=AF.Ln, bias=eps_s[:]),
                   waits=[(dvd, dvd.v), (acd, acd.v)], incs=[(acd, 1)])
            p.emit("scalar", lambda e: e.activation(out=r_row[:], in_=q_row[:], func=AF.Exp, scale=-0.5),
                   waits=[(acd, acd.v)], incs=[(acd, 1)])
            rrow_v = acd.v
            [v_nmr] = p.emit("vector", lambda e: e.scalar_tensor_tensor(
                m_row[:], m_row[:], -1.0, r_row[:], ALU.mult, ALU.mult),
                waits=[(acd, rrow_v), (dvd, dvd.v)], incs=[(dvd, 1)])
            p.emit("tensor", lambda e: e.matmul(h_ps[0][0:128, 0:512], onesr_f[:], r_row[:], start=True, stop=True),
                   waits=[(dvd, v_nmr)] + GUARD(), incs=[(ped, 1)])
            p.emit("tensor", lambda e: e.matmul(h_ps[1][0:128, 0:512], onesr_f[:], m_row[:], start=True, stop=True),
                   incs=[(ped, 1)])
            bcs_done = (ped, ped.v)
            if with_pool:
                p.emit("vector", lambda e: e.reduce_sum(s11b[:], m_row[:], axis=AX.X),
                       waits=[(dvd, v_nmr)], incs=[(dvd, 1)])
                if tt == 0:
                    p.emit("vector", lambda e: e.tensor_copy(s11[:], s11b[:]), waits=[(dvd, dvd.v)], incs=[(dvd, 1)])
                else:
                    p.emit("vector", lambda e: e.tensor_tensor(s11[:], s11[:], s11b[:], ALU.add),
                           waits=[(dvd, dvd.v)], incs=[(dvd, 1)])
            for c in range(8):
                sl = xT[:, 2048 * c + 512 * tt: 2048 * c + 512 * tt + 512]
                osl = xnT[:, 2048 * c + 512 * tt: 2048 * c + 512 * tt + 512]
                p.emit("vector", (lambda sl: lambda e: e.tensor_tensor(u_sb[:], sl, h_ps[0][0:128, 0:512], ALU.mult))(sl),
                       waits=[bcs_done, (dvd, dvd.v)], incs=[(dvd, 1)])
                u_v = dvd.v
                p.emit("vector", (lambda osl: lambda e: e.tensor_tensor(osl, u_sb[:], h_ps[1][0:128, 0:512], ALU.add))(osl),
                       waits=[(dvd, dvd.v), (smm, smm.v), (ped, ped.v)], incs=[(dvd, 1)])
                if with_pool:
                    p.emit("vector", lambda e: e.reduce_sum(pool_t[:], u_sb[:], axis=AX.X),
                           waits=[(dvd, u_v)], incs=[(dvd, 1)])
                    if tt == 0:
                        p.emit("vector", (lambda c: lambda e: e.tensor_copy(pool_acc[:, c:c + 1], pool_t[:]))(c),
                               waits=[(dvd, dvd.v), state["ld_guard"]], incs=[(dvd, 1)])
                    else:
                        p.emit("vector", (lambda c: lambda e: e.tensor_tensor(
                            pool_acc[:, c:c + 1], pool_acc[:, c:c + 1], pool_t[:], ALU.add))(c),
                            waits=[(dvd, dvd.v)], incs=[(dvd, 1)])
            state.setdefault("xn_tt", [0, 0, 0, 0])[tt] = dvd.v
        if with_pool:
            p.emit("tensor", lambda e: e.matmul(g_ps[0][0:128, 0:1], onesr_f[:], s11[:], start=True, stop=True),
                   waits=[(dvd, dvd.v)] + GUARD(), incs=[(ped, 1)])
            p.emit("vector", lambda e: e.tensor_copy(pool_t[:], g_ps[0][0:128, 0:1]),
                   waits=[(ped, ped.v), (dvd, dvd.v)], incs=[(dvd, 1)])
            [vp] = p.emit("vector", lambda e: e.tensor_scalar(pool_acc[:], pool_acc[:], pool_t[:], None, ALU.add),
                          waits=[(dvd, dvd.v)], incs=[(dvd, 1)])
            state["pool_ready"] = (dvd, vp)
        state["xn_ready"] = (dvd, dvd.v)
        state["res_tt"] = None

    def emit_prelude(job):
        par = fl["jobctr"] & 1
        fl["jobctr"] += 1
        job["par"] = par
        qp = job["qp"]
        qb = job.get("qb", 0)
        nA = job["nA"]
        if job.get("qlmm") is not None:
            job["qlmm"]()
        p.emit("vector", (lambda job, qp, qb: lambda e: e.tensor_tensor(
            sq_b[qb:qb + qp, :], job["ql_ap"], job["ql_ap"], ALU.mult))(job, qp, qb),
            waits=job["wq"] + [(dvd, dvd.v), (ped, fl["mhat_free"])], incs=[(dvd, 1)])
        p.emit("tensor", (lambda par, qp, qb: lambda e: e.matmul(
            g_ps[par][0:1, 0:512], ones_col_b[qb:qb + qp, :], sq_b[qb:qb + qp, :], start=True, stop=True))(par, qp, qb),
            waits=[(dvd, dvd.v), (ped, ped.v), (dvd, v_ones)], incs=[(ped, 1)])
        p.emit("vector", (lambda par: lambda e: e.reduce_max(z_s[0:1, 0:1], g_ps[par][0:1, 0:512], axis=AX.X))(par),
               waits=[(ped, ped.v), (dvd, dvd.v)], incs=[(dvd, 1)])
        [mhv] = p.emit("vector", (lambda job: lambda e: e.tensor_tensor(
            mhat_s[:], z_s[0:1, 0:1], kmax_s[0:1, job["kn_col"]:job["kn_col"] + 1], ALU.add))(job),
            waits=[(dvd, dvd.v), (ped, fl["mhat_free"])] + job.get("wkn", []), incs=[(dvd, 1)])
        p.emit("tensor", (lambda par: lambda e: e.matmul(
            g_ps[par][0:128, 256:257], neg16r_f[:], mhat_s[:], start=True, stop=True))(par),
            waits=[(dvd, mhv), (ped, ped.v)], incs=[(ped, 1)])
        fl["mhat_free"] = ped.v
        p.emit("vector", (lambda par: lambda e: e.tensor_copy(mneg[par][:], g_ps[par][0:128, 256:257]))(par),
               waits=[(ped, ped.v), (dvd, dvd.v)], incs=[(dvd, 1)])
        [bv] = p.emit("vector", (lambda job, par, nA: lambda e: e.tensor_scalar(
            bias_all[par][:, 0:nA + 1], moff_s[:, job["moff0"]:job["moff0"] + nA + 1],
            mneg[par][:], None, ALU.add))(job, par, nA),
            waits=[(dvd, dvd.v), (expd, fl["bias_free"][par])], incs=[(dvd, 1)])
        job["pre_done"] = (dvd, bv)
        return job

    def emit_kloop(job):
        par = job["par"]
        nA = job["nA"]
        pre_done = job["pre_done"]
        npair = nA // 2
        nE = npair + 4
        exp_free = []
        pv_last = []
        pending_pv = None
        first_pv = [True]
        for e in range(nE):
            diag = e >= npair
            i = e - npair
            col0 = 128 * i if diag else 0
            bk = e % 2
            if e >= 2:
                w = [(expd, exp_free[e - 2])]
            else:
                w = [pre_done, (expd, fl["sfree"][bk]), (dvd, fl["yfree"][par])] + job.get("wk0", [])
            if diag:
                lhs = job["lhsB"](i)
                p.emit("tensor", (lambda lhs, bk, col0, job: lambda e_: e_.matmul(
                    spair[bk][0:128, col0:512], lhs, job["ql_ap"][:, col0:512], start=True, stop=True))(lhs, bk, col0, job),
                    waits=w, incs=[(smm, 1)])
            else:
                lhs0 = job["lhsA"](2 * e)
                lhs1 = job["lhsA"](2 * e + 1)
                p.emit("tensor", (lambda lhs0, bk, job: lambda e_: e_.matmul(
                    spair[bk][0:128, 0:512], lhs0, job["ql_ap"], start=True, stop=True))(lhs0, bk, job),
                    waits=w, incs=[(smm, 1)])
                p.emit("tensor", (lambda lhs1, bk, job: lambda e_: e_.matmul(
                    spair[bk][0:128, 512:1024], lhs1, job["ql_ap"], start=True, stop=True))(lhs1, bk, job),
                    incs=[(smm, 1)])
            if pending_pv is not None:
                pending_pv()
                pending_pv = None
            if e == 2 and fl.get("pend_epi") is not None:
                pe_fn = fl["pend_epi"]
                fl["pend_epi"] = None
                pe_fn()
            if e == 4 and fl.get("pend_pre") is not None:
                pp = fl["pend_pre"]
                fl["pend_pre"] = None
                pp()
            bcol = nA if diag else 2 * e
            we = [(smm, smm.v)]
            we.append((pvd, pv_last[e - 2]) if e >= 2 else (pvd, fl["pfree"][bk]))
            if diag:
                [ev] = p.emit("scalar", (lambda bk, col0, bcol, par: lambda e_: e_.activation(
                    out=psb[bk][:, col0:512], in_=spair[bk][0:128, col0:512], func=AF.Exp,
                    scale=0.125, bias=bias_all[par][:, bcol:bcol + 1]))(bk, col0, bcol, par),
                    waits=we, incs=[(expd, 1)])
            else:
                [ev] = p.emit("scalar", (lambda bk, bcol, par: lambda e_: e_.activation(
                    out=psb[bk][:, 0:1024], in_=spair[bk][0:128, 0:1024], func=AF.Exp,
                    scale=0.125, bias=bias_all[par][:, bcol:bcol + 1]))(bk, bcol, par),
                    waits=we, incs=[(expd, 1)])
            exp_free.append(ev)
            if diag:
                [mv] = p.emit("vector", (lambda bk, i: lambda e_: e_.tensor_tensor(
                    psb[bk][:, 128 * i:128 * i + 128], psb[bk][:, 128 * i:128 * i + 128], tri_b[:], ALU.mult))(bk, i),
                    waits=[(expd, ev), (dvd, dvd.v)], incs=[(dvd, 1)])
                pv_w = [(dvd, mv)]
            else:
                pv_w = [(expd, ev)]
            last_e = (e == nE - 1)
            if diag:
                vap = job["vB"](i)
                def mk_pv(vap=vap, bk=bk, col0=col0, par=par, last_e=last_e, pv_w=tuple(pv_w)):
                    st = first_pv[0]
                    first_pv[0] = False
                    [pv] = p.emit("tensor", lambda e_: e_.matmul(
                        y_ps[par][0:65, col0:512], vap, psb[bk][:, col0:512],
                        start=st, stop=last_e),
                        waits=list(pv_w), incs=[(pvd, 1)])
                    pv_last.append(pv)
            else:
                vap0 = job["vA"](2 * e)
                vap1 = job["vA"](2 * e + 1)
                def mk_pv(vap0=vap0, vap1=vap1, bk=bk, par=par, pv_w=tuple(pv_w)):
                    st = first_pv[0]
                    first_pv[0] = False
                    p.emit("tensor", lambda e_: e_.matmul(
                        y_ps[par][0:65, 0:512], vap0, psb[bk][:, 0:512],
                        start=st, stop=False),
                        waits=list(pv_w), incs=[(pvd, 1)])
                    [pv] = p.emit("tensor", lambda e_: e_.matmul(
                        y_ps[par][0:65, 0:512], vap1, psb[bk][:, 512:1024],
                        start=False, stop=False),
                        incs=[(pvd, 1)])
                    pv_last.append(pv)
            pending_pv = mk_pv
        pending_pv()
        if fl.get("pend_epi") is not None:
            pe_fn = fl["pend_epi"]; fl["pend_epi"] = None; pe_fn()
        if fl.get("pend_pre") is not None:
            pp = fl["pend_pre"]; fl["pend_pre"] = None; pp()
        fl["sfree"][nE % 2] = exp_free[-2]
        fl["sfree"][(nE - 1) % 2] = exp_free[-1]
        fl["pfree"][nE % 2] = pv_last[-2]
        fl["pfree"][(nE - 1) % 2] = pv_last[-1]
        fl["bias_free"][par] = expd.v
        return (pvd, pv_last[-1])
        fl["sfree"][nT % 2] = exp_free[-2] if nT >= 2 else exp_free[-1]
        fl["sfree"][(nT - 1) % 2] = exp_free[-1]
        fl["pfree"][nT % 2] = pv_at[-2] if nT >= 2 else pv_at[-1]
        fl["pfree"][(nT - 1) % 2] = pv_at[-1]
        fl["bias_free"][par] = expd.v
        return (pvd, pv_at[-1])

    def flash_epilogue(job, pv_done, dst_slice=None, dram_fn=None):
        par = job["par"]
        p.emit("vector", (lambda par: lambda e: e.reciprocal(rec_sb[:], y_ps[par][64:65, 0:512]))(par),
               waits=[pv_done, (dvd, dvd.v), (ped, fl["epi_free"])], incs=[(dvd, 1)])
        p.emit("tensor", (lambda par: lambda e: e.matmul(
            g_ps[par][0:64, 0:512], onesr64_f[:], rec_sb[:], start=True, stop=True))(par),
            waits=[(dvd, dvd.v), (ped, ped.v)], incs=[(ped, 1)])
        fl["epi_free"] = ped.v
        p.emit("vector", (lambda par: lambda e: e.tensor_copy(bc_sb[:], g_ps[par][0:64, 0:512]))(par),
               waits=[(ped, ped.v), (dvd, dvd.v)], incs=[(dvd, 1)])
        if dram_fn is None:
            hi = job.get("hi", False)
            p.emit("vector", (lambda par: lambda e: e.tensor_tensor(
                ytmp_sb[:], y_ps[par][0:64, 0:512], bc_sb[:], ALU.mult))(par),
                waits=[(dvd, dvd.v)], incs=[(dvd, 1)])
            if hi:
                p.emit("tensor", (lambda par: lambda e: e.matmul(
                    g_ps[par][64:128, 0:512], id64f_s[:], ytmp_sb[:],
                    start=True, stop=True, tile_position=(0, 64)))(par),
                    waits=[(dvd, dvd.v), (ped, ped.v)], incs=[(ped, 1)])
                [va] = p.emit("vector", (lambda dst_slice, par: lambda e: e.tensor_tensor(
                    dst_slice, dst_slice, g_ps[par][64:128, 0:512], ALU.add))(dst_slice, par),
                    waits=[(ped, ped.v), (dvd, dvd.v)], incs=[(dvd, 1)])
                fl["epi_free"] = ped.v
            else:
                [va] = p.emit("vector", (lambda dst_slice: lambda e: e.tensor_tensor(
                    dst_slice, dst_slice, ytmp_sb[:], ALU.add))(dst_slice),
                    waits=[(dvd, dvd.v)], incs=[(dvd, 1)])
            fl["yfree"][par] = dvd.v
            return (dvd, va)
        else:
            yp = fl["epi_par"]
            fl["epi_par"] ^= 1
            [vm] = p.emit("vector", (lambda par, yp: lambda e: e.tensor_tensor(
                ynrm_b[yp][:], y_ps[par][0:64, 0:512], bc_sb[:], ALU.mult))(par, yp),
                waits=[(dvd, dvd.v), (ly[yp], fl["ynrm_free"][yp])], incs=[(dvd, 1)])
            fl["yfree"][par] = dvd.v
            [sv] = p.emit("sync", (lambda yp: lambda e: dram_fn(e, ynrm_b[yp]))(yp),
                          waits=[(dvd, vm)], incs=[(ly[yp], 16)])
            fl["ynrm_free"][yp] = sv
            return (ly[yp], sv)

    assert stages >= 1
    emit_ln(False)
    xn1 = state["xn_ready"]
    if True:
        ckv_own_done = []
        q1_done = []
        for tt in range(4):
            for c in range(8):
                p.emit("tensor", (lambda tt, c: lambda e: e.matmul(
                    g_ps[tt % 2][0:128, 0:512], wdkvT_b[:, 128 * c:128 * c + 128],
                    xnT[:, 2048 * c + 512 * tt:2048 * c + 512 * tt + 512], start=(c == 0), stop=(c == 7)))(tt, c),
                    waits=(([(dvd, state["xn_tt"][tt]), wbf_ld] + (GUARD() if tt == 0 else [(ped, ped.v), (dvd, dvd.v)]))
                           if c == 0 else ()), incs=[(ped, 1)])
            [vv] = p.emit("vector", (lambda tt: lambda e: e.tensor_copy(
                u16[:, CKVO + 512 * tt:CKVO + 512 * tt + 512], g_ps[tt % 2][0:128, 0:512]))(tt),
                waits=[(ped, ped.v), (dvd, dvd.v)], incs=[(dvd, 1)])
            ckv_own_done.append(vv)
            for c in range(8):
                p.emit("tensor", (lambda tt, c: lambda e: e.matmul(
                    h_ps[tt % 2][0:128, 0:512], wdqT_b[:, 128 * c:128 * c + 128],
                    xnT[:, 2048 * c + 512 * tt:2048 * c + 512 * tt + 512], start=(c == 0), stop=(c == 7)))(tt, c),
                    incs=[(ped, 1)])
            [vq] = p.emit("vector", (lambda tt: lambda e: e.tensor_copy(
                u16[:, Q1 + 512 * tt:Q1 + 512 * tt + 512], h_ps[tt % 2][0:128, 0:512]))(tt),
                waits=[(ped, ped.v), (dvd, dvd.v)], incs=[(dvd, 1)])
            q1_done.append(vq)

        [sv] = p.emit("sync", lambda e: e.dma_start(out=cc_ckv_i.ap(), in_=u16[:, CKVO:CKVO + 2048]),
                      waits=[(dvd, ckv_own_done[-1])], incs=[(lk, 16)])
        [cc1] = p.emit("gpsimd", lambda e: e.collective_compute(
            "AllGather", ALU.bypass, replica_groups=GROUPS, ins=[cc_ckv_i.ap().opt()], outs=[cc_ckv_o.ap().opt()]),
            waits=[(lk, sv), wbf_ld], incs=[(ccd, 1)])
        for i in range(4):
            for s4 in range(4):
                j = 4 * s4 + i
                p.emit("sync", (lambda i, s4, j: lambda e: e.dma_start(
                    out=ckv_g[:, 512 * j:512 * j + 512],
                    in_=cc_ckv_o.ap()[128 * i:128 * i + 128, 512 * s4:512 * s4 + 512]))(i, s4, j),
                    waits=[(ccd, cc1), prep_done], incs=[(lk, 16)])
        ckv_ld = (lk, lk.v)

        for cc in range(16):
            p.emit("vector", (lambda cc: lambda e: e.tensor_tensor(
                sq_b[:], ckv_g[:, 512 * cc:512 * cc + 512], ckv_g[:, 512 * cc:512 * cc + 512], ALU.mult))(cc),
                waits=[ckv_ld, (dvd, dvd.v)], incs=[(dvd, 1)])
            p.emit("tensor", (lambda cc: lambda e: e.matmul(
                g_ps[cc % 2][0:1, 0:512], ones_col_b[:], sq_b[:], start=True, stop=True))(cc),
                waits=[(dvd, dvd.v)] + GUARD(), incs=[(ped, 1)])
            if cc == 0:
                p.emit("vector", (lambda cc: lambda e: e.reduce_max(
                    kmax_s[0:1, 0:1], g_ps[cc % 2][0:1, 0:512], axis=AX.X))(cc),
                    waits=[(ped, ped.v), (dvd, dvd.v)], incs=[(dvd, 1)])
            else:
                p.emit("vector", (lambda cc: lambda e: e.reduce_max(
                    z_s[0:1, 0:1], g_ps[cc % 2][0:1, 0:512], axis=AX.X))(cc),
                    waits=[(ped, ped.v), (dvd, dvd.v)], incs=[(dvd, 1)])
                p.emit("vector", lambda e: e.tensor_max(kmax_s[0:1, 0:1], kmax_s[0:1, 0:1], z_s[0:1, 0:1]),
                       waits=[(dvd, dvd.v)], incs=[(dvd, 1)])
        kn_done = (dvd, dvd.v)

        two_run_out = lambda ap130: ap130.rearrange("p (a b) -> p a b", b=65)[:, :, 0:64]
        two_run_in = lambda ap128: ap128.rearrange("p (a b) -> p a b", b=64)

        def mk_job_a(hp, hh, s4):
            h = 2 * hp + hh
            par_dummy = 0
            def qlmm():
                p.emit("tensor", (lambda h, s4: lambda e: e.matmul(
                    g_ps[job["par"]][0:128, 0:512], u16[:, AB + 128 * h:AB + 128 * h + 128],
                    u16[:, Q1 + 512 * s4:Q1 + 512 * s4 + 512], start=True, stop=True))(h, s4),
                    waits=[(dvd, q1_done[s4]), (ped, ped.v), (dvd, a_done[h]),
                           (acd, acd.v), (dvd, dvd.v)], incs=[(ped, 1)])
                parx = job["par"]
                [vq] = p.emit("vector", (lambda parx: lambda e: e.tensor_copy(
                    qlb[parx][:], g_ps[parx][0:128, 0:512]))(parx),
                    waits=[(ped, ped.v), (dvd, dvd.v), (smm, smm.v)], incs=[(dvd, 1)])
                job["wq"] = [(dvd, vq)]
                job["ql_ap"] = qlb[parx][:]
            job = {
                "qp": 128, "hi": (hh == 1), "nA": 16 * s4 + 12, "moff0": MOFF0[s4], "kn_col": 0,
                "wkn": [kn_done],
                "wq": [],
                "lhsA": (lambda t: ckv_g[:, 128 * t:128 * t + 128]),
                "lhsB": (lambda i, s4=s4: u16[:, CKVO + 512 * s4 + 128 * i:CKVO + 512 * s4 + 128 * i + 128]),
                "vA": (lambda t, hh=hh: vbuf[:, 130 * t + 65 * hh:130 * t + 65 * hh + 65]),
                "vB": (lambda i, s4=s4, hh=hh: u16[:, VOWN + 130 * (4 * s4 + i) + 65 * hh:
                                                   VOWN + 130 * (4 * s4 + i) + 65 * hh + 65]),
                "dst": (hp, hh, s4),
            }
            job["qlmm"] = qlmm
            return job

        jobs_a = [mk_job_a(hp, hh, s4) for hp in range(8) for hh in range(2) for s4 in range(4)]
        jidx = 0
        for hp in range(8):
            p.emit("vector", lambda e: e.memset(vbuf[:], 1.0),
                   waits=[(dvd, dvd.v), (pvd, pvd.v)], incs=[(dvd, 1)])
            [vo_ms] = p.emit("vector", lambda e: e.memset(u16[:, VOWN:VOWN + 2080], 1.0),
                             waits=[(dvd, dvd.v)], incs=[(dvd, 1)])
            vb_done = []
            for t in range(64):
                w = ([prep_done, kn_done, (dvd, vo_ms)] + GUARD()) if t == 0 else                     ([(dvd, vb_done[t - 2])] if t >= 2 else [])
                p.emit("tensor", (lambda t, hp: lambda e: e.matmul(
                    g_ps[t % 2][0:128, 0:128], ckv_g[:, 128 * t:128 * t + 128],
                    veff_b[:, 128 * hp:128 * hp + 128], start=True, stop=True))(t, hp),
                    waits=w, incs=[(ped, 1)])
                [vb] = p.emit("vector", (lambda t: lambda e: e.tensor_copy(
                    two_run_out(vbuf[:, 130 * t:130 * t + 130]),
                    two_run_in(g_ps[t % 2][0:128, 0:128])))(t),
                    waits=[(ped, ped.v), (dvd, dvd.v)], incs=[(dvd, 1)])
                vb_done.append(vb)
            vo_done = []
            for t in range(16):
                w = [(dvd, vo_done[t - 2])] if t >= 2 else [(dvd, vb_done[0])]
                p.emit("tensor", (lambda t, hp: lambda e: e.matmul(
                    h_ps[t % 2][0:128, 0:128], u16[:, CKVO + 128 * t:CKVO + 128 * t + 128],
                    veff_b[:, 128 * hp:128 * hp + 128], start=True, stop=True))(t, hp),
                    waits=w, incs=[(ped, 1)])
                [vo] = p.emit("vector", (lambda t: lambda e: e.tensor_copy(
                    two_run_out(u16[:, VOWN + 130 * t:VOWN + 130 * t + 130]),
                    two_run_in(h_ps[t % 2][0:128, 0:128])))(t),
                    waits=[(ped, ped.v), (dvd, dvd.v)], incs=[(dvd, 1)])
                vo_done.append(vo)
            vready = (dvd, dvd.v)
            for k in range(8):
                job = jobs_a[jidx]
                job["wk0"] = [vready]
                if jidx == 0:
                    emit_prelude(job)
                if jidx + 1 < 64:
                    nxt = jobs_a[jidx + 1]
                    fl["pend_pre"] = (lambda nxt: lambda: emit_prelude(nxt))(nxt)
                emit_kloop(job)
                pv_done = (pvd, pvd.v)
                hpj, hhj, s4j = job["dst"]
                dst = xT[:, 2048 * hpj + 512 * s4j:2048 * hpj + 512 * s4j + 512][64 * hhj:64 * hhj + 64, :]
                fl["pend_epi"] = (lambda job, pv_done, dst: lambda: flash_epilogue(job, pv_done, dst_slice=dst))(job, pv_done, dst)
                jidx += 1
        if fl.get("pend_epi") is not None:
            fn = fl["pend_epi"]; fl["pend_epi"] = None; fn()
        state["stream_ready"] = (dvd, dvd.v)

    def emit_ffn(call_idx):
        cci, cco = (cc_pool_i, cc_pool_o) if call_idx == 0 else (cc_pool_i2, cc_pool_o2)
        emit_ln(True)
        xn = state["xn_ready"]
        [sv] = p.emit("sync", (lambda cci: lambda e: e.dma_start(out=cci.ap(), in_=pool_acc[:]))(cci),
                      waits=[state["pool_ready"]], incs=[(lp, 16)])
        [ccv] = p.emit("gpsimd", (lambda cci, cco: lambda e: e.collective_compute(
            "AllReduce", ALU.add, replica_groups=GROUPS, ins=[cci.ap().opt()], outs=[cco.ap().opt()]))(cci, cco),
            waits=[(lp, sv)], incs=[(ccd, 1)])
        [pl] = p.emit("sync", (lambda cco: lambda e: e.dma_start(out=pool_in[:], in_=cco.ap()))(cco),
                      waits=[(ccd, ccv)], incs=[(lp, 16)])
        pool_ld = (lp, pl)
        state["ld_guard"] = (lp, pl)
        gelu_at = [0, 0]
        for tt in range(4):
            for pr in range(4):
                bk = (4 * tt + pr) % 2
                for c in range(8):
                    w = []
                    if pr == 0 and tt == 0 and c == 0:
                        w = [(dvd, state["xn_tt"][0]), ffnw_ld] + GUARD()
                    elif c == 0:
                        w = [(acd, gelu_at[bk]), (dvd, dvd.v), (dvd, state["xn_tt"][tt])]
                    p.emit("tensor", (lambda pr, tt, c, bk: lambda e: e.matmul(
                        g_ps[bk][0:128, 0:512], dwT_b[:, 512 * c + 128 * pr:512 * c + 128 * pr + 128],
                        xnT[:, 2048 * c + 512 * tt:2048 * c + 512 * tt + 512], start=(c == 0), stop=(c == 7)))(pr, tt, c, bk),
                        waits=w, incs=[(ped, 1)])
                [hv] = p.emit("scalar", (lambda pr, tt, bk: lambda e: e.activation(
                    out=ckv_g[:, 2048 * pr + 512 * tt:2048 * pr + 512 * tt + 512],
                    in_=g_ps[bk][0:128, 0:512], func=AF.Gelu, bias=bdr_s[:, pr:pr + 1]))(pr, tt, bk),
                    waits=[(ped, ped.v), (acd, acd.v), (smm, smm.v)], incs=[(acd, 1)])
                gelu_at[bk] = hv
        h_all = (acd, acd.v)
        for c in range(8):
            p.emit("tensor", (lambda c: lambda e: e.matmul(
                h_ps[0][0:1, 0:8], pool_in[:, c:c + 1], gwT_s[:, 8 * c:8 * c + 8], start=(c == 0), stop=(c == 7)))(c),
                waits=([pool_ld, const_ld] + GUARD()) if c == 0 else (), incs=[(ped, 1)])
        p.emit("vector", lambda e: e.tensor_scalar(gate_row[:], h_ps[0][0:1, 0:8], 1.0 / 8192.0, None, ALU.mult),
               waits=[(ped, ped.v), (dvd, dvd.v)], incs=[(dvd, 1)])
        p.emit("vector", lambda e: e.reduce_max(s11[:], gate_row[:], axis=AX.X), waits=[(dvd, dvd.v)], incs=[(dvd, 1)])
        p.emit("vector", lambda e: e.tensor_scalar(g2_row[:], gate_row[:], s11[0:1, 0:1], None, ALU.subtract),
               waits=[(dvd, dvd.v)], incs=[(dvd, 1)])
        p.emit("scalar", lambda e: e.activation(out=gate_row[:], in_=g2_row[:], func=AF.Exp),
               waits=[(dvd, dvd.v), (acd, acd.v)], incs=[(acd, 1)])
        p.emit("vector", lambda e: e.reduce_sum(s11[:], gate_row[:], axis=AX.X),
               waits=[(acd, acd.v), (dvd, dvd.v)], incs=[(dvd, 1)])
        p.emit("vector", lambda e: e.reciprocal(s11b[:], s11[:]), waits=[(dvd, dvd.v)], incs=[(dvd, 1)])
        p.emit("vector", lambda e: e.tensor_scalar(gate_row[:], gate_row[:], s11b[0:1, 0:1], None, ALU.mult),
               waits=[(dvd, dvd.v)], incs=[(dvd, 1)])
        p.emit("vector", lambda e: e.reduce_max(s11[:], gate_row[:], axis=AX.X), waits=[(dvd, dvd.v)], incs=[(dvd, 1)])
        p.emit("vector", lambda e: e.tensor_scalar(g2_row[:], gate_row[:], s11[0:1, 0:1], None, ALU.is_ge),
               waits=[(dvd, dvd.v)], incs=[(dvd, 1)])
        p.emit("vector", lambda e: e.scalar_tensor_tensor(g2_row[:], g2_row[:], -1e9, gate_row[:], ALU.mult, ALU.add),
               waits=[(dvd, dvd.v)], incs=[(dvd, 1)])
        p.emit("vector", lambda e: e.reduce_max(s11[:], g2_row[:], axis=AX.X), waits=[(dvd, dvd.v)], incs=[(dvd, 1)])
        p.emit("vector", lambda e: e.tensor_scalar(g2_row[:], gate_row[:], s11[0:1, 0:1], None, ALU.is_ge),
               waits=[(dvd, dvd.v)], incs=[(dvd, 1)])
        p.emit("vector", lambda e: e.tensor_tensor(mk_row[:], gate_row[:], g2_row[:], ALU.mult),
               waits=[(dvd, dvd.v)], incs=[(dvd, 1)])
        p.emit("vector", lambda e: e.reduce_sum(s11[:], mk_row[:], axis=AX.X), waits=[(dvd, dvd.v)], incs=[(dvd, 1)])
        p.emit("vector", lambda e: e.tensor_scalar(s11[:], s11[:], 1e-8, None, ALU.add), waits=[(dvd, dvd.v)], incs=[(dvd, 1)])
        p.emit("vector", lambda e: e.reciprocal(s11b[:], s11[:]), waits=[(dvd, dvd.v)], incs=[(dvd, 1)])
        [vmk] = p.emit("vector", lambda e: e.tensor_scalar(mk_row[:], mk_row[:], s11b[0:1, 0:1], None, ALU.mult),
                       waits=[(dvd, dvd.v)], incs=[(dvd, 1)])
        p.emit("tensor", lambda e: e.matmul(h_ps[1][0:128, 0:8], onesr_f[:], mk_row[:], start=True, stop=True),
               waits=[(dvd, vmk)] + GUARD(), incs=[(ped, 1)])
        p.emit("vector", lambda e: e.tensor_copy(mkb_s[:], h_ps[1][0:128, 0:8]),
               waits=[(ped, ped.v), (dvd, dvd.v)], incs=[(dvd, 1)])
        for pr in range(4):
            p.emit("vector", (lambda pr: lambda e: e.tensor_scalar(
                mvec_s[:, pr:pr + 1], halves_s[:, 0:1], mkb_s[:, 2 * pr:2 * pr + 1], None, ALU.mult))(pr),
                waits=[(dvd, dvd.v)], incs=[(dvd, 1)])
            p.emit("vector", (lambda pr: lambda e: e.scalar_tensor_tensor(
                mvec_s[:, pr:pr + 1], halves_s[:, 1:2], mkb_s[:, 2 * pr + 1:2 * pr + 2],
                mvec_s[:, pr:pr + 1], ALU.mult, ALU.add))(pr),
                waits=[(dvd, dvd.v)], incs=[(dvd, 1)])
        mvec_done = (dvd, dvd.v)
        for c in range(8):
            p.emit("vector", (lambda c: lambda e: e.tensor_scalar(
                bumix_col[:, c:c + 1], upb_cm_s[:, 8 * c:8 * c + 1], mkb_s[:, 0:1], None, ALU.mult))(c),
                waits=[(dvd, dvd.v)], incs=[(dvd, 1)])
            for e8 in range(1, 8):
                p.emit("vector", (lambda c, e8: lambda e: e.scalar_tensor_tensor(
                    bumix_col[:, c:c + 1], upb_cm_s[:, 8 * c + e8:8 * c + e8 + 1], mkb_s[:, e8:e8 + 1],
                    bumix_col[:, c:c + 1], ALU.mult, ALU.add))(c, e8),
                    waits=[(dvd, dvd.v)], incs=[(dvd, 1)])
        bumix_done = (dvd, dvd.v)
        for pr in range(4):
            p.emit("vector", (lambda pr: lambda e: e.tensor_scalar(
                wu_sc[:, 1024 * pr:1024 * pr + 1024], upT_b[:, 1024 * pr:1024 * pr + 1024],
                mvec_s[:, pr:pr + 1], None, ALU.mult))(pr),
                waits=([mvec_done, ffnw_ld, (ped, ped.v)] if pr == 0 else [(dvd, dvd.v)]), incs=[(dvd, 1)])
        wu_all = (dvd, dvd.v)
        res_bank_free = [0, 0]
        rtt = [0, 0, 0, 0]
        for tt in range(4):
            for c in range(8):
                bk = (4 * tt + c) % 2
                for pr in range(4):
                    w = []
                    if tt == 0 and c == 0 and pr == 0:
                        w = [wu_all, h_all, bumix_done] + GUARD()
                    elif pr == 0:
                        w = [(dvd, res_bank_free[bk])]
                    p.emit("tensor", (lambda tt, c, pr, bk: lambda e: e.matmul(
                        g_ps[bk][0:128, 0:512], wu_sc[:, 1024 * pr + 128 * c:1024 * pr + 128 * c + 128],
                        ckv_g[:, 2048 * pr + 512 * tt:2048 * pr + 512 * tt + 512], start=(pr == 0), stop=(pr == 3)))(tt, c, pr, bk),
                        waits=w, incs=[(ped, 1)])
                [rv] = p.emit("vector", (lambda tt, c, bk: lambda e: e.scalar_tensor_tensor(
                    xT[:, 2048 * c + 512 * tt:2048 * c + 512 * tt + 512],
                    g_ps[bk][0:128, 0:512], bumix_col[:, c:c + 1],
                    xT[:, 2048 * c + 512 * tt:2048 * c + 512 * tt + 512], ALU.add, ALU.add))(tt, c, bk),
                    waits=[(ped, ped.v), (dvd, dvd.v)], incs=[(dvd, 1)])
                res_bank_free[bk] = rv
            rtt[tt] = dvd.v
        state["res_tt"] = rtt
        state["stream_ready"] = (dvd, dvd.v)

    if stages >= 2:
        emit_ffn(0)

    if stages >= 3:
        emit_ln(False)
        xn3 = state["xn_ready"]
        [sv1] = p.emit("sync", lambda e: e.dma_start(
            out=cc_xn_i1.ap().rearrange("(c p) t -> p c t", p=128),
            in_=xnT[:].rearrange("p (c t) -> p c t", c=8)[:, :, 0:1024]),
            waits=[(dvd, state["xn_tt"][1])], incs=[(lx, 16)])
        [ccx1] = p.emit("gpsimd", lambda e: e.collective_compute(
            "AllGather", ALU.bypass, replica_groups=GROUPS, ins=[cc_xn_i1.ap().opt()], outs=[cc_xn_o1.ap().opt()]),
            waits=[(lx, sv1)], incs=[(ccd, 1)])
        [sv2] = p.emit("sync", lambda e: e.dma_start(
            out=cc_xn_i2.ap().rearrange("(c p) t -> p c t", p=128),
            in_=xnT[:].rearrange("p (c t) -> p c t", c=8)[:, :, 1024:2048]),
            waits=[xn3], incs=[(lx2, 16)])
        [ccx2] = p.emit("gpsimd", lambda e: e.collective_compute(
            "AllGather", ALU.bypass, replica_groups=GROUPS, ins=[cc_xn_i2.ap().opt()], outs=[cc_xn_o2.ap().opt()]),
            waits=[(lx2, sv2)], incs=[(ccd, 1)])
        xn_ag1 = (ccd, ccx1)
        xn_ag2 = (ccd, ccx2)

        for pr2 in range(2):
            [wq_ld] = p.emit("gpsimd", (lambda pr2: lambda e: e.dma_start(
                out=wqkv_b[:].rearrange("p (h c f) -> p h c f", h=2, c=8), in_=wqkv[2 * pr2:2 * pr2 + 2, :, :].rearrange("h (c p) f -> p h c f", p=128)))(pr2),
                waits=[(pvd, pvd.v), (expd, expd.v), (smm, smm.v), (dvd, dvd.v), (acd, acd.v), (ped, ped.v)], incs=[(lds, 16)])
            Qd = [lambda g: xnT[:, 512 * g:512 * g + 512][0:64, :],
                  lambda g: ckv_g[:, 512 * g:512 * g + 512][64:128, :]]
            Kd = [lambda g: xnT[:, 8192 + 512 * g:8192 + 512 * g + 512][0:64, :],
                  lambda g: xnT[:, 8192 + 512 * g:8192 + 512 * g + 512][64:128, :]]
            Vd = [lambda g: xnT[:, 512 * g:512 * g + 512][64:128, :],
                  lambda g: ckv_g[:, 512 * g:512 * g + 512][0:64, :]]
            Kc = [lambda t: xnT[:, 8192 + 128 * t:8192 + 128 * t + 128][0:64, :],
                  lambda t: xnT[:, 8192 + 128 * t:8192 + 128 * t + 128][64:128, :]]
            stage_free = [0, 0]
            evac_free = [0, 0]
            for g in range(16):
                i4, s4 = g % 4, g // 4
                st = ST0 if g % 2 == 0 else ST1
                xnago = (xn_ag1 if s4 < 2 else xn_ag2)
                cco_h = (cc_xn_o1 if s4 < 2 else cc_xn_o2)
                s4h = s4 % 2
                w = [xnago] + ([(pvd, pvd.v), (expd, expd.v), (smm, smm.v), (dvd, dvd.v)] if g < 2 else [(ped, stage_free[g % 2])])
                [stl] = p.emit("sync", (lambda i4, s4h, st, cco_h: lambda e: e.dma_start(
                    out=u16[:, st:st + 4096].rearrange("p (c t) -> p c t", c=8),
                    in_=cco_h.ap()[1024 * i4:1024 * i4 + 1024, 512 * s4h:512 * s4h + 512].rearrange("(c p) t -> p c t", p=128)))(i4, s4h, st, cco_h),
                    waits=w, incs=[(lst[g % 2], 16)])
                for hh in range(2):
                    bk = g % 2
                    qb = 64 * hh
                    wqk = [(lst[g % 2], stl), (lds, wq_ld), (acd, evac_free[bk]), (ped, ped.v)]
                    if g < 2 and hh == 0:
                        wqk += GUARD()
                    for c in range(8):
                        p.emit("tensor", (lambda hh, c, st, bk, qb: lambda e: e.matmul(
                            g_ps[bk][qb:qb + 64, 0:512], wqkv_b[:, 1536 * hh + 192 * c:1536 * hh + 192 * c + 64],
                            u16[:, st + 512 * c:st + 512 * c + 512], start=(c == 0), stop=(c == 7),
                            tile_position=(0, qb)))(hh, c, st, bk, qb),
                            waits=wqk if c == 0 else (), incs=[(ped, 1)])
                    for c in range(8):
                        p.emit("tensor", (lambda hh, c, st, bk: lambda e: e.matmul(
                            h_ps[bk][0:128, 0:512], wqkv_b[:, 1536 * hh + 192 * c + 64:1536 * hh + 192 * c + 192],
                            u16[:, st + 512 * c:st + 512 * c + 512], start=(c == 0), stop=(c == 7)))(hh, c, st, bk),
                            incs=[(ped, 1)])
                    pe_g = (ped, ped.v)
                    p.emit("scalar", (lambda hh, g, bk, pr2, qb: lambda e: e.activation(
                        out=Qd[hh](g), in_=g_ps[bk][qb:qb + 64, 0:512],
                        func=AF.Identity, bias=bq_s[qb:qb + 64, 2 * pr2 + hh:2 * pr2 + hh + 1]))(hh, g, bk, pr2, qb),
                        waits=[pe_g, (acd, acd.v)], incs=[(acd, 1)])
                    p.emit("scalar", (lambda hh, g, bk, pr2, qb: lambda e: e.activation(
                        out=Kd[hh](g), in_=h_ps[bk][qb:qb + 64, 0:512],
                        func=AF.Identity, bias=bk_s[qb:qb + 64, 2 * pr2 + hh:2 * pr2 + hh + 1]))(hh, g, bk, pr2, qb),
                        incs=[(acd, 1)])
                    vb0 = 64 * (1 - hh)
                    [ve] = p.emit("scalar", (lambda hh, g, bk, pr2, vb0: lambda e: e.activation(
                        out=Vd[hh](g), in_=h_ps[bk][vb0:vb0 + 64, 0:512],
                        func=AF.Identity, bias=bv_s[vb0:vb0 + 64, 2 * pr2 + hh:2 * pr2 + hh + 1]))(hh, g, bk, pr2, vb0),
                        incs=[(acd, 1)])
                    evac_free[bk] = ve
                stage_free[g % 2] = ped.v
            qkv_all = (acd, acd.v)
            [vms] = p.emit("vector", lambda e: e.memset(vbuf[:], 1.0),
                           waits=[(pvd, pvd.v), (dvd, dvd.v)], incs=[(dvd, 1)])
            two_run_out2 = lambda ap130: ap130.rearrange("p (a b) -> p a b", b=65)[:, :, 0:64]
            vt_done = []
            for t in range(64):
                for hh in range(2):
                    w = ([qkv_all, (dvd, vms), (expd, expd.v), (smm, smm.v), (ped, ped.v)] if (t == 0 and hh == 0)
                         else ([(dvd, vt_done[2 * t + hh - 2])] if 2 * t + hh >= 2 else []))
                    vsrc = (xnT[:, 128 * t:128 * t + 128][64:128, :] if hh == 0
                            else ckv_g[:, 128 * t:128 * t + 128][0:64, :])
                    ident_sl = id2_s[64:128, :] if hh == 0 else id2_s[0:64, :]
                    p.emit("tensor", (lambda t, hh, vsrc, ident_sl: lambda e: e.transpose(
                        s_ps[(2 * t + hh) % 2][0:128, 0:32].bitcast(BF16)[:, 0:64], vsrc, ident_sl))(t, hh, vsrc, ident_sl),
                        waits=w, incs=[(ped, 1)])
                    [vt] = p.emit("vector", (lambda t, hh: lambda e: e.tensor_copy(
                        vbuf[:, 130 * t + 65 * hh:130 * t + 65 * hh + 64],
                        s_ps[(2 * t + hh) % 2][0:128, 0:32].bitcast(BF16)[:, 0:64]))(t, hh),
                        waits=[(ped, ped.v), (dvd, dvd.v)], incs=[(dvd, 1)])
                    vt_done.append(vt)
            fl["sfree"] = [expd.v, expd.v]
            vb_ready = (dvd, dvd.v)
            for hh in range(2):
                qb = 64 * hh
                for cc in range(16):
                    p.emit("vector", (lambda hh, cc, qb: lambda e: e.tensor_tensor(
                        sq_b[qb:qb + 64, :], Kd[hh](cc), Kd[hh](cc), ALU.mult))(hh, cc, qb),
                        waits=[qkv_all, (dvd, dvd.v)], incs=[(dvd, 1)])
                    p.emit("tensor", (lambda hh, cc, qb: lambda e: e.matmul(
                        g_ps[cc % 2][0:1, 0:512], ones_col_b[qb:qb + 64, :], sq_b[qb:qb + 64, :], start=True, stop=True))(hh, cc, qb),
                        waits=[(dvd, dvd.v)] + (GUARD() if (hh == 0 and cc == 0) else [(ped, ped.v), (dvd, dvd.v)]),
                        incs=[(ped, 1)])
                    if cc == 0:
                        p.emit("vector", (lambda hh, cc: lambda e: e.reduce_max(
                            kmax_s[0:1, 1 + hh:2 + hh], g_ps[cc % 2][0:1, 0:512], axis=AX.X))(hh, cc),
                            waits=[(ped, ped.v), (dvd, dvd.v)], incs=[(dvd, 1)])
                    else:
                        p.emit("vector", (lambda hh, cc: lambda e: e.reduce_max(
                            z_s[0:1, 0:1], g_ps[cc % 2][0:1, 0:512], axis=AX.X))(hh, cc),
                            waits=[(ped, ped.v), (dvd, dvd.v), (acd, fl["z_free"])], incs=[(dvd, 1)])
                        p.emit("vector", (lambda hh: lambda e: e.tensor_max(
                            kmax_s[0:1, 1 + hh:2 + hh], kmax_s[0:1, 1 + hh:2 + hh], z_s[0:1, 0:1]))(hh),
                            waits=[(dvd, dvd.v)], incs=[(dvd, 1)])
            knb_done = (dvd, dvd.v)
            def mk_job_b(hh, jb):
                return {
                    "qp": 64, "qb": 64 * hh, "nA": 4 * jb, "moff0": MOFF_B, "kn_col": 1 + hh,
                    "wkn": [knb_done],
                    "ql_ap": Qd[hh](jb),
                    "wq": [qkv_all, vb_ready],
                    "wk0": [],
                    "lhsA": (lambda t, hh=hh: Kc[hh](t)),
                    "lhsB": (lambda i, hh=hh, jb=jb: Kc[hh](4 * jb + i)),
                    "vA": (lambda t, hh=hh: vbuf[:, 130 * t + 65 * hh:130 * t + 65 * hh + 65]),
                    "vB": (lambda i, hh=hh, jb=jb: vbuf[:, 130 * (4 * jb + i) + 65 * hh:
                                                        130 * (4 * jb + i) + 65 * hh + 65]),
                    "dst": (hh, jb),
                }
            jobs_b = [mk_job_b(hh, jb) for hh in range(2) for jb in range(16)]
            for ji, job in enumerate(jobs_b):
                if ji == 0:
                    emit_prelude(job)
                if ji + 1 < len(jobs_b):
                    fl["pend_pre"] = (lambda nxt: lambda: emit_prelude(nxt))(jobs_b[ji + 1])
                emit_kloop(job)
                pv_done = (pvd, pvd.v)
                hhj, jbj = job["dst"]
                hl = 2 * pr2 + hhj
                own, slot = jbj % 4, jbj // 4
                dram_fn = (lambda own, slot, hl: lambda e, ybuf: e.dma_start(
                    out=cc_y_i.ap()[256 * own + 64 * hl:256 * own + 64 * hl + 64, 512 * slot:512 * slot + 512],
                    in_=ybuf[:]))(own, slot, hl)
                fl["pend_epi"] = (lambda job, pv_done, dram_fn: lambda: flash_epilogue(job, pv_done, dram_fn=dram_fn))(job, pv_done, dram_fn)
        if fl.get("pend_epi") is not None:
            _fn = fl["pend_epi"]
            fl["pend_epi"] = None
            _fn()
        [ccy] = p.emit("gpsimd", lambda e: e.collective_compute(
            "AllGather", ALU.bypass, replica_groups=GROUPS, ins=[cc_y_i.ap().opt()], outs=[cc_y_o.ap().opt()]),
            waits=[(ly[0], fl["ynrm_free"][0]), (ly[1], fl["ynrm_free"][1])], incs=[(ccd, 1)])
        y_ag = (ccd, ccy)
        for tt in range(4):
            for c in range(8):
                i_rk = c // 2
                hl0 = (2 * c) % 4
                def ysrc(o):
                    return cc_y_o.ap()[1024 * i_rk + 256 * o + 64 * hl0:
                                       1024 * i_rk + 256 * o + 64 * hl0 + 128, 512 * tt:512 * tt + 512]
                w0 = [y_ag, (dvd, dvd.v)] if (tt == 0 and c == 0) else [(dvd, dvd.v)]
                [l0] = p.emit("sync", (lambda a=ysrc(0): lambda e: e.dma_start(out=cand[0][:], in_=a))(),
                              waits=w0, incs=[(lc, 16)])
                [l1] = p.emit("sync", (lambda a=ysrc(1): lambda e: e.dma_start(out=cand[1][:], in_=a))(),
                              incs=[(lc, 16)])
                p.emit("vector", lambda e: e.tensor_scalar(cand[0][:], cand[0][:], ind_s[:, 0:1], None, ALU.mult),
                       waits=[(lc, l1), (dvd, dvd.v), (smm, smm.v), (ped, ped.v)], incs=[(dvd, 1)])
                [a1] = p.emit("vector", lambda e: e.scalar_tensor_tensor(
                    cand[0][:], cand[1][:], ind_s[:, 1:2], cand[0][:], ALU.mult, ALU.add),
                    waits=[(dvd, dvd.v)], incs=[(dvd, 1)])
                [l2] = p.emit("sync", (lambda a=ysrc(2): lambda e: e.dma_start(out=cand[1][:], in_=a))(),
                              waits=[(dvd, a1)], incs=[(lc, 16)])
                p.emit("vector", lambda e: e.scalar_tensor_tensor(
                    cand[0][:], cand[1][:], ind_s[:, 2:3], cand[0][:], ALU.mult, ALU.add),
                    waits=[(lc, l2), (dvd, dvd.v)], incs=[(dvd, 1)])
                [l3] = p.emit("sync", (lambda a=ysrc(3): lambda e: e.dma_start(out=cand[1][:], in_=a))(),
                              waits=[(dvd, dvd.v)], incs=[(lc, 16)])
                p.emit("vector", lambda e: e.scalar_tensor_tensor(
                    cand[0][:], cand[1][:], ind_s[:, 3:4], cand[0][:], ALU.mult, ALU.add),
                    waits=[(lc, l3), (dvd, dvd.v)], incs=[(dvd, 1)])
                p.emit("vector", (lambda c, tt: lambda e: e.tensor_copy(
                    xnT[:, 2048 * c + 512 * tt:2048 * c + 512 * tt + 512], cand[0][:]))(c, tt),
                    waits=[(dvd, dvd.v)], incs=[(dvd, 1)])
        ysel_all = (dvd, dvd.v)
        res_bank_free = [0, 0]
        for do in range(8):
            [wl] = p.emit("gpsimd", (lambda do: lambda e: e.dma_start(
                out=woutc_b[do % 2][:].rearrange("p (c f) -> p c f", c=8), in_=wout[:, 128 * do:128 * do + 128].rearrange("(c p) f -> p c f", p=128)))(do),
                waits=[(ped, ped.v), (smm, smm.v)], incs=[(lds, 16)])
            for tt in range(4):
                bk = (4 * do + tt) % 2
                w = [(lds, wl)]
                if do == 0 and tt == 0:
                    w += [ysel_all] + GUARD()
                else:
                    w += [(dvd, res_bank_free[bk])]
                for c in range(8):
                    p.emit("tensor", (lambda do, tt, c, bk: lambda e: e.matmul(
                        g_ps[bk][0:128, 0:512], woutc_b[do % 2][:, 128 * c:128 * c + 128],
                        xnT[:, 2048 * c + 512 * tt:2048 * c + 512 * tt + 512], start=(c == 0), stop=(c == 7)))(do, tt, c, bk),
                        waits=w if c == 0 else (), incs=[(ped, 1)])
                [rv] = p.emit("vector", (lambda do, tt, bk: lambda e: e.scalar_tensor_tensor(
                    xT[:, 2048 * do + 512 * tt:2048 * do + 512 * tt + 512],
                    g_ps[bk][0:128, 0:512], bout_cm_s[:, do:do + 1],
                    xT[:, 2048 * do + 512 * tt:2048 * do + 512 * tt + 512], ALU.add, ALU.add))(do, tt, bk),
                    waits=[(ped, ped.v), (dvd, dvd.v)], incs=[(dvd, 1)])
                res_bank_free[bk] = rv
        state["stream_ready"] = (dvd, dvd.v)

    if stages >= 4:
        emit_ffn(1)

    fin_tt = state["res_tt"] if state.get("res_tt") is not None else None
    for tt in range(4):
        [so] = p.emit("sync", (lambda tt: lambda e: e.dma_start(
            out=out_feat.rearrange("(c p) t -> p c t", p=128)[:, :, 512 * tt:512 * tt + 512],
            in_=xT[:].rearrange("p (c t) -> p c t", c=8)[:, :, 512 * tt:512 * tt + 512]))(tt),
            waits=[(dvd, fin_tt[tt]) if fin_tt is not None else state["stream_ready"]], incs=[(std, 16)])
    p.emit("gpsimd", lambda e: e.engine_nop(), waits=[(std, so)])

    p.replay()
    return nc


_NC_CACHE = {}


def _get_nc(stages=4):
    if stages not in _NC_CACHE:
        _NC_CACHE[stages] = _build(stages)
    return _NC_CACHE[stages]


def _make_inmaps(inputs):
    f32 = np.float32
    import ml_dtypes
    bf = ml_dtypes.bfloat16
    C = np.ascontiguousarray
    x = np.asarray(inputs["L0_attn_a_x"], f32)
    W_dq = np.asarray(inputs["L0_attn_a_W_dq"], f32)
    W_uq = np.asarray(inputs["L0_attn_a_W_uq"], f32)
    W_dkv = np.asarray(inputs["L0_attn_a_W_dkv"], f32)
    W_uk = np.asarray(inputs["L0_attn_a_W_uk"], f32)
    W_uv = np.asarray(inputs["L0_attn_a_W_uv"], f32)
    W_o = np.asarray(inputs["L0_attn_a_W_o"], f32)
    Wqkv = np.asarray(inputs["L0_attn_b_Wqkv"], f32)
    bqkv = np.asarray(inputs["L0_attn_b_bqkv"], f32)
    Wout = np.asarray(inputs["L0_attn_b_Wout"], f32)
    bout = np.asarray(inputs["L0_attn_b_bout"], f32)
    gate_w = np.asarray(inputs["L0_ffn_gate_w"], f32)
    down_w = np.asarray(inputs["L0_ffn_down_w"], f32)
    down_b = np.asarray(inputs["L0_ffn_down_b"], f32)
    up_w = np.asarray(inputs["L0_ffn_up_w"], f32)
    up_b = np.asarray(inputs["L0_ffn_up_b"], f32)

    bdr = np.zeros((128, 4), f32)
    for pr in range(4):
        bdr[0:64, pr] = down_b[2 * pr]
        bdr[64:128, pr] = down_b[2 * pr + 1]
    shared = {
        "w_dq": C(W_dq), "w_dqT": C(W_dq.T), "w_uq": C(W_uq), "w_uk": C(W_uk), "w_uv": C(W_uv),
        "w_oT": C(W_o.T), "w_dkvT": C(W_dkv.T),
        "wout": C(Wout), "bout_cm": C(bout.reshape(8, 128).T),
        "gate_wT": C(gate_w.T),
        "dwT": C(down_w.transpose(2, 0, 1).reshape(1024, 512)),
        "bd_rows": bdr,
        "upT": C(up_w.transpose(0, 2, 1).reshape(512, 1024)),
        "upb_cm": C(up_b.reshape(8, 8, 128).transpose(2, 1, 0).reshape(128, 64)),
        "halves": np.concatenate([(np.arange(128) < 64).astype(f32)[:, None],
                                  (np.arange(128) >= 64).astype(f32)[:, None]], axis=1),
        "tri": C(np.triu(np.ones((128, 128))).astype(bf)),
        "ident": C(np.eye(128).astype(bf)),
        "onesf_r128": np.ones((1, 128), f32),
        "negonesf_r128": -np.ones((1, 128), f32),
        "onesf_r64": np.ones((1, 64), f32),
        "ident64f": np.eye(64, dtype=f32),
        "ident2": C(np.vstack([np.eye(64), np.eye(64)]).astype(bf)),
    }
    in_maps = []
    for c in range(8):
        B, R = c // 4, c % 4
        im = dict(shared)
        xo = np.empty((2048, 1024), f32)
        for s in range(4):
            j = 4 * s + R
            xo[512 * s:512 * s + 512] = x[B, 512 * j:512 * j + 512]
        im["x_own"] = C(xo.T)
        mo = np.zeros((128, 209), f32)
        for s in range(4):
            nA = 16 * s + 12
            real = 16 * s + 4 * R
            col = MOFF0[s]
            if real < nA:
                mo[:, col + real:col + nA] = NEG
        im["moff"] = mo
        wq4 = Wqkv[4 * R:4 * R + 4].copy()
        for hl in range(1, 4, 2):
            kcols = wq4[hl, :, 64:128].copy()
            wq4[hl, :, 64:128] = wq4[hl, :, 128:192]
            wq4[hl, :, 128:192] = kcols
        im["wqkv"] = C(wq4)
        bq = np.zeros((128, 4), f32)
        bk = np.zeros((128, 4), f32)
        bv = np.zeros((128, 4), f32)
        for hl in range(4):
            qb = 64 * (hl % 2)
            vb = 64 * (1 - hl % 2)
            bq[qb:qb + 64, hl] = bqkv[4 * R + hl, 0:64]
            bk[qb:qb + 64, hl] = bqkv[4 * R + hl, 64:128]
            bv[vb:vb + 64, hl] = bqkv[4 * R + hl, 128:192]
        im["bq_in"] = bq
        im["bk_in"] = bk
        im["bv_in"] = bv
        indm = np.zeros((128, 4), f32)
        indm[:, R] = 1.0
        im["ind"] = indm
        in_maps.append(im)
    return in_maps


def kernel(**inputs):
    stages = int(inputs.pop("_stages", 4))
    nc = _get_nc(stages)
    in_maps = _make_inmaps(inputs)
    res = run_bass_kernel_spmd(nc, in_maps, core_ids=list(range(8)))
    out = np.empty((2, 8192, 1024), np.float32)
    for c in range(8):
        B, R = c // 4, c % 4
        shard = np.asarray(res.results[c]["out_feat"]).T
        for s in range(4):
            j = 4 * s + R
            out[B, 512 * j:512 * j + 512] = shard[512 * s:512 * s + 512]
    return out
